# version 26
# speedup vs baseline: 1.0422x; 1.0023x over previous
import os
import numpy as np
import ml_dtypes
import concourse.bass as bass
import concourse.mybir as mybir
from concourse.bass_utils import run_bass_kernel_spmd
from contextlib import ExitStack

F32 = mybir.dt.float32
BF16 = mybir.dt.bfloat16
FP16 = mybir.dt.float16
FP8 = mybir.dt.float8e4
AF = mybir.ActivationFunctionType
ALU = mybir.AluOpType
AX = mybir.AxisListType

S = 4096
D = 1024
H = 16
DH = 64
TB = 512
NB = S // TB
NST = 8
MAGIC = 12582912.0

ENGS = ("sync", "vector", "scalar", "tensor", "gpsimd")
SKIP = set(os.environ.get("K1_SKIP", "").split(","))

SCAN_TILES = [(0, 1, 1)] + [(1 + 320 * j, 320, 32) for j in range(12)] + [(3841, 255, 25)]


class Sem:
    def __init__(self, h):
        self.h = h
        self.n = 0


class Rec:

    def __init__(self, nc, dpool, auto):
        self.nc = nc
        self.q = {k: [] for k in ENGS}
        self.dpool = dpool
        self.didx = 0
        self.auto = auto

    def dma(self, dst, src, **kw):
        sem = self.dpool[self.didx % len(self.dpool)]
        self.didx += 1
        if sem.n > 0:
            self.wait("sync", sem)
        self.op("sync", "dma_start", inc=(sem, 16), out=dst, in_=src, **kw)
        return (sem, sem.n)

    def wait_tok(self, eng, tok):
        if tok is not None:
            self.q[eng].append(lambda e, h=tok[0].h, v=tok[1]: e.wait_ge(h, v))

    def wait_all_dma(self, eng):
        for sem in self.dpool:
            if sem.n:
                self.wait(eng, sem)

    def op(self, eng, method, *args, inc=None, **kw):
        if eng in self.auto:
            asem = self.auto[eng]
            if asem.n > 0:
                self.wait(eng, asem)
            if inc is None:
                inc = (asem, 1)
        tgt = None
        if inc is not None:
            sem, k = inc
            sem.n += k
            tgt = (sem.h, k)

        def thunk(e, method=method, args=args, kw=kw, tgt=tgt):
            o = getattr(e, method)(*args, **kw)
            if tgt is not None:
                o.then_inc(*tgt)
        self.q[eng].append(thunk)

    def wait(self, eng, sem, val=None):
        v = sem.n if val is None else val
        self.q[eng].append(lambda e, h=sem.h, v=v: e.wait_ge(h, v))

    def play(self):
        with self.nc.Block() as blk:
            for name in ENGS:
                lst = self.q[name]
                if lst:
                    getattr(blk, name)(lambda e, lst=lst: [f(e) for f in lst])
        self.nc.all_engine_barrier()
        self.q = {k: [] for k in ENGS}


def _np_rope_tables():
    inv = 1.0 / (10000.0 ** (np.arange(0, DH, 2, dtype=np.float32) / DH))
    t = np.arange(S, dtype=np.float32)
    fr = np.einsum("t,d->td", t, inv).astype(np.float32)
    fr = np.concatenate([fr, fr], axis=-1)
    cos = np.cos(fr).astype(np.float32).T
    sin = np.sin(fr).astype(np.float32).T
    cosT = np.concatenate([cos, cos], axis=0)
    sinT = np.concatenate([sin, sin], axis=0)
    return (np.asarray(cosT, dtype=ml_dtypes.bfloat16),
            np.asarray(sinT, dtype=ml_dtypes.bfloat16))


def build(stop_after=None):
    nc = bass.Bass("TRN2", target_bir_lowering=False)

    x_in = nc.dram_tensor("x", [S, D], F32, kind="ExternalInput").ap()
    qkv_w = nc.dram_tensor("qkv_w", [3 * D, D], F32, kind="ExternalInput").ap()
    qkv_b = nc.dram_tensor("qkv_b", [1, 3 * D], F32, kind="ExternalInput").ap()
    proj_w = nc.dram_tensor("proj_w", [D, D], F32, kind="ExternalInput").ap()
    proj_b = nc.dram_tensor("proj_b", [128, 8], F32, kind="ExternalInput").ap()
    f0_w = nc.dram_tensor("f0_w", [D, D], F32, kind="ExternalInput").ap()
    f0_b = nc.dram_tensor("f0_b", [128, 8], F32, kind="ExternalInput").ap()
    w_dt = nc.dram_tensor("w_dt", [D, D], F32, kind="ExternalInput").ap()
    b_dt = nc.dram_tensor("b_dt", [128, 8], F32, kind="ExternalInput").ap()
    w_B = nc.dram_tensor("w_B", [NST, D], F32, kind="ExternalInput").ap()
    w_C = nc.dram_tensor("w_C", [NST, D], F32, kind="ExternalInput").ap()
    a_log = nc.dram_tensor("a_log", [128, 64], F32, kind="ExternalInput").ap()
    d_par = nc.dram_tensor("d_par", [128, 8], F32, kind="ExternalInput").ap()
    f1_w = nc.dram_tensor("f1_w", [D, D], F32, kind="ExternalInput").ap()
    out = nc.dram_tensor("out", [S, D], F32, kind="ExternalOutput").ap()

    ident_d = nc.inline_tensor(np.eye(128, dtype=np.float32), name="ident_f").ap()
    cosT_np, sinT_np = _np_rope_tables()
    cos_d = nc.inline_tensor(cosT_np, name="cosT").ap()
    sin_d = nc.inline_tensor(sinT_np, name="sinT").ap()
    Rm = np.zeros((64, 64), dtype=np.float32)
    Rm[0:32, 32:64] = -np.eye(32)
    Rm[32:64, 0:32] = np.eye(32)
    RT2 = np.zeros((128, 128), dtype=np.float32)
    RT2[0:64, 0:64] = Rm.T
    RT2[64:128, 64:128] = Rm.T
    rot_d = nc.inline_tensor(np.asarray(RT2, dtype=ml_dtypes.bfloat16), name="rotT").ap()
    tri_np = (np.arange(128)[:, None] <= np.arange(128)[None, :])
    tri_d = nc.inline_tensor(np.asarray(tri_np, dtype=ml_dtypes.bfloat16), name="tri").ap()

    def scratch(name, shape, dt):
        return nc.dram_tensor(name, shape, dt, kind="Internal").ap()

    wqkvT_h = scratch("wqkvT_h", [D, 3 * D], BF16)
    wprojT_h = scratch("wprojT_h", [D, D], BF16)
    wdtT_h = scratch("wdtT_h", [D, D], BF16)
    wf0T_h = scratch("wf0T_h", [D, D], FP8)
    wq1T_h = scratch("wq1T_h", [D, D], BF16)
    k_hbm = scratch("k_hbm", [4, 128, S], BF16)
    v_hbm = scratch("v_hbm", [128, 8 * 32 * 65], BF16)
    g_hbm = scratch("g_hbm", [D, S], FP16)
    ctx_hbm = scratch("ctx_hbm", [512, S], BF16)
    xnT0_hbm = scratch("xnT0_hbm", [D, S], BF16)
    x1_hbm = scratch("x1_hbm", [S, D], F32)
    xn1T_hbm = scratch("xn1T_hbm", [D, S], BF16)
    dt_hbm = scratch("dt_hbm", [D, S], F32)
    dtx_hbm = scratch("dtx_hbm", [D, S], FP16)
    g1_hbm = scratch("g1_hbm", [D, S], FP16)
    y_hbm = scratch("y_hbm", [D, S], FP16)

    top = ExitStack()
    dpool = [Sem(top.enter_context(nc.semaphore(f"DP{i}"))) for i in range(12)]
    SV = Sem(top.enter_context(nc.semaphore("SV")))
    SS = Sem(top.enter_context(nc.semaphore("SS")))
    TN = Sem(top.enter_context(nc.semaphore("TN")))
    SG = Sem(top.enter_context(nc.semaphore("SG")))
    AUTO = {"vector": SV, "scalar": SS, "gpsimd": SG}

    sbt = lambda n, s, d=F32: top.enter_context(nc.sbuf_tensor(n, s, d))
    identf = sbt("identf", [128, 128])
    rotT = sbt("rotTs", [128, 128], BF16)
    tri = sbt("tris", [128, 128], BF16)
    ones_b = sbt("ones_b", [1, TB], BF16)
    ones_f = sbt("ones_f", [1, 128])
    ones_c = sbt("ones_c", [128, 1])
    c1 = sbt("c1", [128, 1])
    cS1 = sbt("cS1", [128, 1])
    qkvb_row = sbt("qkvb_row", [1, 3 * D], BF16)
    projb_c = sbt("projb_c", [128, 8])
    f0b_c = sbt("f0b_c", [128, 8])
    bdt_c = sbt("bdt_c", [128, 8])
    dpar_c = sbt("dpar_c", [128, 8])
    a16 = sbt("a16", [128, 64])
    wBT = sbt("wBT", [128, 64])
    wCT = sbt("wCT", [128, 64])
    wsinv_c = sbt("wsinv_c", [128, 1])
    wsd127 = sbt("wsd127", [1, 1])

    with ExitStack() as ph:
        R = Rec(nc, dpool, AUTO)
        ps_t = ph.enter_context(nc.psum_tensor("ps_t", [128, 512], F32))
        wld2 = [ph.enter_context(nc.sbuf_tensor(f"wld{i}", [128, D], F32))
                for i in range(2)]
        wtmp = ph.enter_context(nc.sbuf_tensor("wtmp", [128, D], BF16))
        wtmp8 = ph.enter_context(nc.sbuf_tensor("wtmp8", [128, D], FP8))
        rowtmp = ph.enter_context(nc.sbuf_tensor("rowtmp", [1, 3 * D], F32))
        colstage = ph.enter_context(nc.sbuf_tensor("colstage", [128, 64], F32))
        wqtmp = ph.enter_context(nc.sbuf_tensor("wqtmp", [128, D], F32))
        redtmp = ph.enter_context(nc.sbuf_tensor("redtmp", [128, 16], F32))
        sc1 = ph.enter_context(nc.sbuf_tensor("sc1", [1, 16], F32))

        for dst, src in ((identf[:], ident_d), (rotT[:], rot_d), (tri[:], tri_d),
                         (rowtmp[:], qkv_b), (projb_c[:], proj_b),
                         (f0b_c[:], f0_b), (bdt_c[:], b_dt), (dpar_c[:], d_par),
                         (colstage[:], a_log)):
            R.dma(dst, src)
        R.wait_all_dma("vector")
        R.wait_all_dma("scalar")
        for t_, v_ in ((ones_b, 1.0), (ones_f, 1.0), (ones_c, 1.0), (c1, 1.0), (cS1, -0.0505 / 0.099)):
            R.op("vector", "memset", t_[:], v_)
        R.op("vector", "tensor_copy", qkvb_row[:], rowtmp[:], inc=(SV, 1))
        R.op("scalar", "activation", colstage[:], colstage[:], AF.Tanh, scale=0.1,
             inc=(SS, 1))
        R.op("scalar", "activation", colstage[:], colstage[:], AF.Exp, scale=10.0,
             inc=(SS, 1))
        R.wait("vector", SS)
        R.op("vector", "tensor_scalar", a16[:], colstage[:], -1.0 / 16.0, None,
             ALU.mult, inc=(SV, 1))

        wtoks = {}
        wtrn = {0: 0, 1: 0}
        for wi, (src, dst) in enumerate(((w_B, wBT), (w_C, wCT))):
            R.wait("sync", TN)
            wld = wld2[wi % 2]
            tok = R.dma(wld[0:8, :], src)
            for dc in range(8):
                R.wait_tok("tensor", tok)
                R.wait("tensor", SV)
                R.op("tensor", "transpose", ps_t[:, 0:8], wld[0:8, bass.ts(dc, 128)],
                     identf[0:8, 0:8], inc=(TN, 1))
                R.wait("vector", TN)
                wtrn[wi % 2] = TN.n
                R.op("vector", "tensor_copy", dst[:, bass.ts(dc, 8)], ps_t[:, 0:8],
                     inc=(SV, 1))

        def stage_weight(src_hbm, n_oc, dst_hbm, dst_cols, cast_tile, oc_off=0,
                         st_tok=[None]):
            def issue(oc):
                src_ap = bass.AP(src_hbm.tensor,
                                 src_hbm.offset + (oc_off + oc * 128) * D,
                                 [[D, 128], [1, D]])
                if wtrn[oc % 2]:
                    R.wait("sync", TN, wtrn[oc % 2])
                wtoks[oc % 2] = R.dma(wld2[oc % 2][:], src_ap)
            issue(0)
            for oc in range(n_oc):
                par = oc % 2
                if oc + 1 < n_oc:
                    issue(oc + 1)
                wld = wld2[par]
                R.wait_tok("tensor", wtoks[par])
                for half in range(2):
                    R.wait("tensor", SV)
                    for q4 in range(4):
                        R.op("tensor", "transpose", ps_t[:, bass.ts(q4, 128)],
                             wld[:, bass.ts(half * 4 + q4, 128)], identf[:],
                             inc=(TN, 1))
                    R.wait("vector", TN)
                    if half == 0:
                        R.wait_tok("vector", st_tok[0])
                    R.op("vector", "tensor_copy",
                         cast_tile[:, half * 512: half * 512 + 512],
                         ps_t[:, 0:512], inc=(SV, 1))
                wtrn[par] = TN.n
                R.wait("sync", SV)
                dst_ap = bass.AP(dst_hbm.tensor, dst_hbm.offset + oc * 128,
                                 [[dst_cols, 128], [dst_cols * 128, 8], [1, 128]])
                src_s = bass.AP(cast_tile[:].tensor, cast_tile[:].offset,
                                [[cast_tile[:].ap[0][0], 128], [128, 8], [1, 128]])
                st_tok[0] = R.dma(dst_ap, src_s)

        stage_weight(qkv_w, 24, wqkvT_h, 3 * D, wtmp)
        stage_weight(proj_w, 8, wprojT_h, D, wtmp)
        stage_weight(w_dt, 8, wdtT_h, D, wtmp)
        stage_weight(f0_w, 8, wf0T_h, D, wtmp8)

        wsv = {0: 0, 1: 0}

        def issue_f1(oc):
            src_ap = bass.AP(f1_w.tensor, f1_w.offset + oc * 128 * D,
                             [[D, 128], [1, D]])
            if wtrn[oc % 2]:
                R.wait("sync", TN, wtrn[oc % 2])
            if wsv[oc % 2]:
                R.wait("sync", SV, wsv[oc % 2])
            wtoks[oc % 2] = R.dma(wld2[oc % 2][:], src_ap)
        issue_f1(0)
        for oc in range(8):
            par = oc % 2
            if oc + 1 < 8:
                issue_f1(oc + 1)
            R.wait_tok("vector", wtoks[par])
            R.op("vector", "reduce_sum", redtmp[:, oc:oc + 1], wld2[par][:],
                 AX.X, apply_absolute_value=True, inc=(SV, 1))
            wsv[par] = SV.n
        R.wait("tensor", SV)
        R.op("tensor", "matmul", ps_t[0:1, 0:8], ones_c[:], redtmp[:, 0:8],
             start=True, stop=True, inc=(TN, 1))
        R.wait("vector", TN)
        R.op("vector", "reduce_sum", sc1[:, 0:1], ps_t[0:1, 0:8], AX.X, inc=(SV, 1))
        R.op("vector", "tensor_scalar", sc1[:, 1:2], sc1[:, 0:1],
             1.0 / (D * D), 1e-8, ALU.mult, ALU.add, inc=(SV, 1))
        R.op("vector", "reciprocal", sc1[:, 2:3], sc1[:, 1:2], inc=(SV, 1))
        R.op("vector", "tensor_scalar", wsd127[:], sc1[:, 1:2], 1.0 / 127.0, None,
             ALU.mult, inc=(SV, 1))
        R.wait("tensor", SV)
        R.op("tensor", "matmul", ps_t[:, 0:1], ones_f[:], sc1[:, 2:3],
             start=True, stop=True, inc=(TN, 1))
        R.wait("vector", TN)
        R.op("vector", "tensor_copy", wsinv_c[:], ps_t[:, 0:1], inc=(SV, 1))
        wq_st = None
        issue_f1(0)
        for oc in range(8):
            par = oc % 2
            if oc + 1 < 8:
                issue_f1(oc + 1)
            wld = wld2[par]
            R.wait_tok("vector", wtoks[par])
            R.op("vector", "tensor_scalar", wqtmp[:], wld[:], wsinv_c[:, 0:1], None,
                 ALU.mult, inc=(SV, 1))
            R.op("vector", "tensor_scalar", wld[:], wqtmp[:], 0.5, None, ALU.is_ge,
                 inc=(SV, 1))
            R.op("vector", "tensor_scalar", wqtmp[:], wqtmp[:], -0.5, None,
                 ALU.is_le, inc=(SV, 1))
            R.op("vector", "tensor_sub", wld[:], wld[:], wqtmp[:], inc=(SV, 1))
            for half in range(2):
                R.wait("tensor", SV)
                for q4 in range(4):
                    R.op("tensor", "transpose", ps_t[:, bass.ts(q4, 128)],
                         wld[:, bass.ts(half * 4 + q4, 128)], identf[:],
                         inc=(TN, 1))
                R.wait("vector", TN)
                if half == 0:
                    R.wait_tok("vector", wq_st)
                R.op("vector", "tensor_copy",
                     wtmp[:, half * 512: half * 512 + 512],
                     ps_t[:, 0:512], inc=(SV, 1))
            wtrn[par] = TN.n
            wsv[par] = SV.n
            R.wait("sync", SV)
            dst_ap = bass.AP(wq1T_h.tensor, wq1T_h.offset + oc * 128,
                             [[D, 128], [D * 128, 8], [1, 128]])
            src_s = bass.AP(wtmp[:].tensor, wtmp[:].offset,
                            [[wtmp[:].ap[0][0], 128], [128, 8], [1, 128]])
            wq_st = R.dma(dst_ap, src_s)
        R.wait_all_dma("sync")
        R.play()
    if stop_after == "p0":
        return nc

    l0 = ExitStack()
    s0 = lambda n, s, d=F32: l0.enter_context(nc.sbuf_tensor(n, s, d))
    cos_blk2 = [s0(f"cos_blk{i}", [128, TB], BF16) for i in range(2)]
    sin_blk2 = [s0(f"sin_blk{i}", [128, TB], BF16) for i in range(2)]
    kv_res = s0("kv_res", [128, 4 * S], BF16)
    v_res = s0("v_res", [128, 8 * 32 * 65], BF16)
    xblk2 = [s0(f"xblk{i}", [128, 4 * D]) for i in range(2)]
    xnT2 = [s0(f"xnT2_{i}", [128, 8 * TB], BF16) for i in range(2)]

    def emit_ln4(R, src_tile, dst_tile, sq, s2, st):
        m4, var4, rstd4 = st["m"], st["var"], st["rstd"]
        y24, u4, w4 = st["y2"], st["u"], st["w"]
        for tc in range(4):
            R.op("vector", "reduce_sum", m4[:, tc:tc + 1],
                 src_tile[:, bass.ts(tc, D)], AX.X)
            R.wait("scalar", SV)
            R.op("scalar", "activation", sq[:], src_tile[:, bass.ts(tc, D)],
                 AF.Square, accum_out=s2[:, tc:tc + 1], inc=(SS, 1))
        R.wait("vector", SS)
        R.op("vector", "tensor_scalar", m4[:], m4[:], 1.0 / D, None, ALU.mult)
        R.op("vector", "tensor_scalar", var4[:], s2[:], 1.0 / D, None, ALU.mult)
        R.op("vector", "tensor_mul", y24[:], m4[:], m4[:])
        R.op("vector", "tensor_sub", var4[:], var4[:], y24[:])
        R.op("vector", "tensor_scalar", var4[:], var4[:], 1.0, 1e-5, ALU.mult,
             ALU.add)
        R.op("vector", "reciprocal", rstd4[:], var4[:])
        for _ in range(4):
            R.op("vector", "tensor_mul", y24[:], rstd4[:], rstd4[:])
            R.op("vector", "tensor_mul", u4[:], var4[:], y24[:])
            R.op("vector", "tensor_scalar", w4[:], u4[:], -0.5, 1.5, ALU.mult,
                 ALU.add)
            R.op("vector", "tensor_mul", rstd4[:], rstd4[:], w4[:])
        for tc in range(4):
            R.op("vector", "tensor_scalar", dst_tile[:, bass.ts(tc, D)],
                 src_tile[:, bass.ts(tc, D)], m4[:, tc:tc + 1],
                 rstd4[:, tc:tc + 1], ALU.subtract, ALU.mult, inc=(SV, 1))

    xtoks = {}
    xnt_toks = {}

    def emit_xload(R, blkidx, par, want_x=True, want_xnt=False):
        t0_ = blkidx * TB
        toks = []
        if want_x:
            src = bass.AP(x_in.tensor, x_in.offset + blkidx * TB * D,
                          [[D, 128], [128 * D, 4], [1, D]])
            xb = xblk2[par]
            xdst = bass.AP(xb[:].tensor, xb[:].offset,
                           [xb[:].ap[0], [D, 4], [1, D]])
            toks.append(R.dma(xdst, src))
        cs = bass.AP(cos_d.tensor, cos_d.offset + t0_, [[S, 128], [1, TB]])
        sn = bass.AP(sin_d.tensor, sin_d.offset + t0_, [[S, 128], [1, TB]])
        toks.append(R.dma(cos_blk2[par][:], cs))
        toks.append(R.dma(sin_blk2[par][:], sn))
        xtoks[par] = tuple(toks)
        if want_xnt:
            src = bass.AP(xnT0_hbm.tensor, xnT0_hbm.offset + t0_,
                          [[S, 128], [S * 128, 8], [1, TB]])
            dst = bass.AP(xnT2[par][:].tensor, xnT2[par][:].offset,
                          [xnT2[par][:].ap[0], [TB, 8], [1, TB]])
            xnt_toks[par] = R.dma(dst, src)

    with ExitStack() as ph:
        R = Rec(nc, dpool, AUTO)
        wkvT = ph.enter_context(nc.sbuf_tensor("wkvT", [128, 8 * 2048], BF16))
        wf0T = ph.enter_context(nc.sbuf_tensor("wf0T", [128, 8 * D], FP8))
        xnT_f8 = ph.enter_context(nc.sbuf_tensor("xnT_f8", [128, 8 * TB], FP8))
        k_raw = ph.enter_context(nc.sbuf_tensor("k_raw", [128, TB], BF16))
        k_t1 = ph.enter_context(nc.sbuf_tensor("k_t1", [128, TB], BF16))
        k_t2 = ph.enter_context(nc.sbuf_tensor("k_t2", [128, TB], BF16))
        kstage = ph.enter_context(nc.sbuf_tensor("kstage", [128, TB], BF16))
        vstage = ph.enter_context(nc.sbuf_tensor("vstage", [128, TB], BF16))
        gstage = ph.enter_context(nc.sbuf_tensor("gstage", [128, TB], FP16))
        xnblk = ph.enter_context(nc.sbuf_tensor("xnblk", [128, 4 * D], F32))
        sqdump = ph.enter_context(nc.sbuf_tensor("sqdump", [128, D], F32))
        s2t = ph.enter_context(nc.sbuf_tensor("s2t", [128, 4], F32))
        stat = {k: ph.enter_context(nc.sbuf_tensor("st_" + k, [128, 4], F32))
                for k in ("m", "var", "rstd", "y2", "u", "w")}
        ps_a = ph.enter_context(nc.psum_tensor("p1_a", [128, 512], F32))
        ps_k = [ph.enter_context(nc.psum_tensor(f"p1_k{i}", [128, 512], F32))
                for i in range(2)]
        ps_r = ph.enter_context(nc.psum_tensor("p1_r", [128, 512], F32))
        ps_v = [ph.enter_context(nc.psum_tensor(f"p1_v{i}", [128, 512], F32))
                for i in range(2)]
        ps_g = [ph.enter_context(nc.psum_tensor(f"p1_g{i}", [128, 512], F32))
                for i in range(2)]

        if "ones" not in SKIP:
            ones_ap = bass.AP(v_res[:].tensor, v_res[:].offset + 64,
                              [v_res[:].ap[0], [65, 8 * 32]])
            R.op("vector", "memset", ones_ap, 1.0, inc=(SV, 1))
        wkv_toks = []
        for ic in range(8):
            wsrc = bass.AP(wqkvT_h.tensor, wqkvT_h.offset + ic * 3 * D * 128 + 1024,
                           [[3 * D, 128], [1, 2048]])
            wkv_toks.append(R.dma(wkvT[:, ic * 2048:(ic + 1) * 2048], wsrc))
        wf0_toks = []
        for ic in range(8):
            w8src = bass.AP(wf0T_h.tensor, wf0T_h.offset + ic * D * 128,
                            [[D, 128], [1, D]])
            wf0_toks.append(R.dma(wf0T[:, ic * D:(ic + 1) * D], w8src))
        st_k = None
        st_v = None
        st_g = None
        st_xnt = [None, None]
        emit_xload(R, 0, 0)
        for b in range(NB):
            t0 = b * TB
            par = b % 2
            cos_blk, sin_blk = cos_blk2[par], sin_blk2[par]
            xnT_bf = xnT2[par]
            xblk = xblk2[par]
            for tk in xtoks[par]:
                R.wait_tok("vector", tk)
            R.wait_tok("scalar", xtoks[par][0])
            emit_ln4(R, xblk[:], xnblk[:], sqdump, s2t, stat)
            R.wait_tok("vector", st_xnt[par])
            p1tr = {0: 0, 1: 0}
            R.wait("tensor", SV)
            R.wait("tensor", SS)
            for ic in range(8):
                pst = (ps_a, ps_r)[ic % 2]
                if p1tr[ic % 2]:
                    R.wait("tensor", SV, p1tr[ic % 2])
                for tc in range(4):
                    R.op("tensor", "transpose", pst[:, bass.ts(tc, 128)],
                         xnblk[:, tc * D + ic * 128: tc * D + ic * 128 + 128],
                         identf[:], inc=(TN, 1))
                R.wait("vector", TN)
                R.op("vector", "tensor_copy", xnT_bf[:, bass.ts(ic, TB)],
                     pst[:, 0:512], inc=(SV, 1))
                R.op("vector", "tensor_copy", xnT_f8[:, bass.ts(ic, TB)],
                     pst[:, 0:512], inc=(SV, 1))
                p1tr[ic % 2] = SV.n
            R.wait("sync", SV)
            xts = bass.AP(xnT_bf[:].tensor, xnT_bf[:].offset,
                          [xnT_bf[:].ap[0], [TB, 8], [1, TB]])
            xtd = bass.AP(xnT0_hbm.tensor, xnT0_hbm.offset + t0,
                          [[S, 128], [S * 128, 8], [1, TB]])
            st_xnt[par] = R.dma(xtd, xts)
            if b + 1 < NB:
                R.wait("sync", SV)
                emit_xload(R, b + 1, 1 - par)
            def emit_k(kc):
                nonlocal st_k
                pk = ps_k[kc % 2]
                R.wait("tensor", SS)
                R.wait("tensor", SV)
                for ic in range(8):
                    if b == 0:
                        R.wait_tok("tensor", wkv_toks[ic])
                    R.op("tensor", "matmul", pk[:],
                         wkvT[:, ic * 2048 + kc * 128: ic * 2048 + kc * 128 + 128],
                         xnT_bf[:, bass.ts(ic, TB)],
                         start=(ic == 0), stop=False, inc=(TN, 1))
                R.op("tensor", "matmul", pk[:],
                     qkvb_row[0:1, 1024 + kc * 128: 1024 + kc * 128 + 128],
                     ones_b[:], start=False, stop=True, inc=(TN, 1))
                R.wait("scalar", TN)
                R.wait("scalar", SV)
                R.op("scalar", "copy", k_raw[:], pk[:], inc=(SS, 1))
                R.wait("tensor", SS)
                R.wait("tensor", SV)
                R.op("tensor", "matmul", ps_r[:], rotT[:], k_raw[:],
                     start=True, stop=True, inc=(TN, 1))
                R.wait("vector", SS)
                R.op("vector", "tensor_mul", k_t1[:], k_raw[:],
                     cos_blk[:], inc=(SV, 1))
                R.wait("vector", TN)
                R.op("vector", "tensor_mul", k_t2[:], ps_r[:],
                     sin_blk[:], inc=(SV, 1))
                if kc < 4:
                    R.op("vector", "tensor_add",
                         kv_res[:, kc * S + t0: kc * S + t0 + TB],
                         k_t1[:], k_t2[:], inc=(SV, 1))
                else:
                    R.wait_tok("vector", st_k)
                    R.op("vector", "tensor_add", kstage[:], k_t1[:], k_t2[:],
                         inc=(SV, 1))
                    R.wait("sync", SV)
                    dst = bass.AP(k_hbm.tensor,
                                  k_hbm.offset + (kc - 4) * 128 * S + t0,
                                  [[S, 128], [1, TB]])
                    st_k = R.dma(dst, kstage[:])

            def emit_v(vi):
                nonlocal st_v
                tc, dvc = vi // 2, vi % 2
                pv = ps_v[dvc]
                R.wait("tensor", SV)
                for ic in range(8):
                    if b == 0:
                        R.wait_tok("tensor", wkv_toks[ic])
                    R.op("tensor", "matmul", pv[:],
                         xnT_bf[:, ic * TB + tc * 128: ic * TB + tc * 128 + 128],
                         wkvT[:, ic * 2048 + 1024 + dvc * 512:
                              ic * 2048 + 1024 + dvc * 512 + 512],
                         start=(ic == 0), stop=False, inc=(TN, 1))
                R.op("tensor", "matmul", pv[:], ones_b[0:1, 0:128],
                     qkvb_row[0:1, 2048 + dvc * 512: 2048 + dvc * 512 + 512],
                     start=False, stop=True, inc=(TN, 1))
                R.wait("vector", TN)
                kcg = b * 4 + tc
                if dvc == 0:
                    dsts = bass.AP(v_res[:].tensor, v_res[:].offset + kcg * 65,
                                   [v_res[:].ap[0], [32 * 65, 8], [1, 64]])
                    srcs = bass.AP(pv[:].tensor, pv[:].offset,
                                   [pv[:].ap[0], [64, 8], [1, 64]])
                    R.op("vector", "tensor_copy", dsts, srcs, inc=(SV, 1))
                else:
                    R.wait_tok("vector", st_v)
                    R.op("vector", "tensor_copy", vstage[:], pv[:], inc=(SV, 1))
                    R.wait("sync", SV)
                    dst = bass.AP(v_hbm.tensor, v_hbm.offset + kcg * 65,
                                  [[8 * 32 * 65, 128], [32 * 65, 8], [1, 64]])
                    vsrc = bass.AP(vstage[:].tensor, vstage[:].offset,
                                   [vstage[:].ap[0], [64, 8], [1, 64]])
                    st_v = R.dma(dst, vsrc)

            def emit_f(fc):
                nonlocal st_g
                pg = ps_g[fc % 2]
                R.wait("tensor", SS)
                for ic in range(8):
                    if b == 0:
                        R.wait_tok("tensor", wf0_toks[ic])
                    R.op("tensor", "matmul", pg[:],
                         wf0T[:, ic * D + fc * 128: ic * D + fc * 128 + 128],
                         xnT_f8[:, bass.ts(ic, TB)],
                         start=(ic == 0), stop=(ic == 7), inc=(TN, 1))
                R.wait("scalar", TN)
                R.wait_tok("scalar", st_g)
                R.op("scalar", "activation", gstage[:], pg[:], AF.Silu,
                     bias=f0b_c[:, fc:fc + 1], inc=(SS, 1))
                R.wait("sync", SS)
                dst = bass.AP(g_hbm.tensor, g_hbm.offset + fc * 128 * S + t0,
                              [[S, 128], [1, TB]])
                st_g = R.dma(dst, gstage[:])

            for i in range(8):
                if 'k' not in SKIP:
                    emit_k(i)
                if 'v' not in SKIP:
                    emit_v(i)
                if 'ffn0' not in SKIP:
                    emit_f(i)
        R.wait_all_dma("sync")
        R.play()

    def attn_phase(group):
        G = f"g{group}_"
        with ExitStack() as ph:
            R = Rec(nc, dpool, AUTO)
            wqT = ph.enter_context(nc.sbuf_tensor(G + "wqT", [128, 8 * 512], BF16))
            q_res2 = [ph.enter_context(nc.sbuf_tensor(G + f"q_res{i}", [128, 4 * TB], BF16))
                      for i in range(2)]
            q_raw = ph.enter_context(nc.sbuf_tensor(G + "q_raw", [128, TB], BF16))
            q_t1 = ph.enter_context(nc.sbuf_tensor(G + "q_t1", [128, TB], BF16))
            pTAB = ph.enter_context(nc.sbuf_tensor(G + "pTAB", [128, 4096], BF16))
            zrow = ph.enter_context(nc.sbuf_tensor(G + "zrow", [1, 512], F32))
            zrow2 = ph.enter_context(nc.sbuf_tensor(G + "zrow2", [1, 512], F32))
            zbf = ph.enter_context(nc.sbuf_tensor(G + "zbf", [128, 512], F32))
            ctxst = ph.enter_context(nc.sbuf_tensor(G + "ctxst", [64, 512], BF16))
            ctxst2 = ph.enter_context(nc.sbuf_tensor(G + "ctxst2", [64, 512], BF16))
            ps_AB = ph.enter_context(nc.psum_tensor(G + "p2_AB", [128, 2048], F32))
            ps_C = ph.enter_context(nc.psum_tensor(G + "p2_C", [128, 512], F32))
            ps_cA = ph.enter_context(nc.psum_tensor(G + "p2_cA", [65, 512], F32))
            ps_cB = ph.enter_context(nc.psum_tensor(G + "p2_cB", [65, 512], F32))
            ps_q = ph.enter_context(nc.psum_tensor(G + "p2_q", [128, 512], F32))
            if group == 1:
                wpT = ph.enter_context(nc.sbuf_tensor(G + "wpT", [128, 8 * D], BF16))
                ctxa_ld = ph.enter_context(nc.sbuf_tensor(G + "ctxa_ld", [128, 2048], BF16))
                ctxb_res = ph.enter_context(nc.sbuf_tensor(G + "ctxb_res", [128, 2048], BF16))
                g_fc = ph.enter_context(nc.sbuf_tensor(G + "g_fc", [128, 512], FP16))
                sg_fc = ph.enter_context(nc.sbuf_tensor(G + "sg_fc", [128, 512], F32))

            wsrc = bass.AP(wqkvT_h.tensor, wqkvT_h.offset + group * 512,
                           [[3 * D, 128], [3 * D * 128, 8], [1, 512]])
            wdstq = bass.AP(wqT[:].tensor, wqT[:].offset,
                            [wqT[:].ap[0], [512, 8], [1, 512]])
            wq_tok = R.dma(wdstq, wsrc)
            k_toks = [None] * 4
            v_toks = [None] * 4
            wp_tok = None
            if group == 1:
                for hp in range(4):
                    src = bass.AP(k_hbm.tensor, k_hbm.offset + hp * 128 * S,
                                  [[S, 128], [1, S]])
                    k_toks[hp] = R.dma(kv_res[:, hp * S:(hp + 1) * S], src)
                    vsw = bass.AP(v_hbm.tensor,
                                  v_hbm.offset + hp * 2 * 32 * 65,
                                  [[8 * 32 * 65, 128], [1, 2 * 32 * 65]])
                    v_toks[hp] = R.dma(
                        v_res[:, hp * 2 * 32 * 65:(hp + 1) * 2 * 32 * 65], vsw)
                wps = bass.AP(wprojT_h.tensor, wprojT_h.offset,
                              [[D, 128], [D * 128, 8], [1, D]])
                wpdst = bass.AP(wpT[:].tensor, wpT[:].offset,
                                [wpT[:].ap[0], [D, 8], [1, D]])
                wp_tok = R.dma(wpdst, wps)
                ones_ap2 = bass.AP(v_res[:].tensor, v_res[:].offset + 64,
                                   [v_res[:].ap[0], [65, 8 * 32]])
                for hp in range(4):
                    R.wait_tok("vector", v_toks[hp])
                R.op("vector", "memset", ones_ap2, 1.0, inc=(SV, 1))
                ones_sv = SV.n
            R.wait_tok("tensor", wq_tok)
            st_ctx_h = [None, None]
            st_x1 = None
            st_out = None
            slot_ss = [0, 0]
            ctx_sv = [0]
            q_ss = [0]
            q_sv = [0, 0]
            q_psv = [0]
            gate_sv = [0]
            pend_nt = [None]
            znt_tn = [0]

            def norm_tail():
                if pend_nt[0] is None:
                    return
                hA_, t0_, rsv = pend_nt[0]
                pend_nt[0] = None
                R.wait("tensor", SV, rsv)
                R.op("tensor", "matmul", ps_C[0:64, :], ones_f[0:1, 0:64],
                     zrow[:], start=True, stop=True, inc=(TN, 1))
                R.op("tensor", "matmul", ps_C[64:128, :], ones_f[0:1, 0:64],
                     zrow2[:], start=True, stop=True, inc=(TN, 1))
                znt_tn[0] = TN.n
                R.wait("vector", TN, znt_tn[0])
                R.op("vector", "tensor_copy", zbf[:], ps_C[:], inc=(SV, 1))
                if group == 0:
                    R.wait_tok("vector", st_ctx_h[0])
                    R.op("vector", "tensor_mul", ctxst[:], ps_cA[0:64, :],
                         zbf[0:64, :], inc=(SV, 1))
                    R.wait_tok("vector", st_ctx_h[1])
                    R.op("vector", "tensor_mul", ctxst2[:], ps_cB[0:64, :],
                         zbf[64:128, :], inc=(SV, 1))
                    R.wait("sync", SV)
                    dstA = bass.AP(ctx_hbm.tensor,
                                   ctx_hbm.offset + (hA_ * 64) * S + t0_,
                                   [[S, 64], [1, TB]])
                    st_ctx_h[0] = R.dma(dstA, ctxst[:])
                    dstB = bass.AP(ctx_hbm.tensor,
                                   ctx_hbm.offset + ((hA_ + 1) * 64) * S + t0_,
                                   [[S, 64], [1, TB]])
                    st_ctx_h[1] = R.dma(dstB, ctxst2[:])
                else:
                    R.op("vector", "tensor_mul",
                         ctxb_res[0:64, bass.ts(hA_ // 2, TB)], ps_cA[0:64, :],
                         zbf[0:64, :], inc=(SV, 1))
                    R.op("vector", "tensor_mul",
                         ctxb_res[64:128, bass.ts(hA_ // 2, TB)], ps_cB[0:64, :],
                         zbf[64:128, :], inc=(SV, 1))
                ctx_sv[0] = SV.n

            def emit_qblock(b_, par_):
                q_res = q_res2[par_]
                cosb, sinb = cos_blk2[par_], sin_blk2[par_]
                xnt = xnT2[par_]
                R.wait_tok("tensor", xnt_toks[par_])
                for tk in xtoks[par_]:
                    R.wait_tok("vector", tk)
                for qc in range(4):
                    if q_ss[0]:
                        R.wait("tensor", SS, q_ss[0])
                    if q_psv[0]:
                        R.wait("tensor", SV, q_psv[0])
                    for ic in range(8):
                        R.op("tensor", "matmul", ps_q[:],
                             wqT[:, ic * 512 + qc * 128: ic * 512 + qc * 128 + 128],
                             xnt[:, bass.ts(ic, TB)],
                             start=(ic == 0), stop=False, inc=(TN, 1))
                    R.op("tensor", "matmul", ps_q[:],
                         qkvb_row[0:1, group * 512 + qc * 128:
                                  group * 512 + qc * 128 + 128],
                         ones_b[:], start=False, stop=True, inc=(TN, 1))
                    R.wait("scalar", TN)
                    R.wait("scalar", SV)
                    R.op("scalar", "copy", q_raw[:], ps_q[:], inc=(SS, 1))
                    q_ss[0] = SS.n
                    R.wait("tensor", SS, q_ss[0])
                    R.op("tensor", "matmul", ps_q[:], rotT[:], q_raw[:],
                         start=True, stop=True, inc=(TN, 1))
                    rot_tn = TN.n
                    R.wait("vector", SS, q_ss[0])
                    R.op("vector", "tensor_mul", q_t1[:], q_raw[:],
                         cosb[:], inc=(SV, 1))
                    R.wait("vector", TN, rot_tn)
                    R.op("vector", "tensor_mul", q_res[:, bass.ts(qc, TB)],
                         ps_q[:], sinb[:], inc=(SV, 1))
                    q_psv[0] = SV.n
                    R.op("vector", "tensor_add", q_res[:, bass.ts(qc, TB)],
                         q_res[:, bass.ts(qc, TB)], q_t1[:], inc=(SV, 1))
                q_sv[par_] = SV.n
            emit_xload(R, 0, 0, want_x=(group == 1), want_xnt=True)
            for b in range(NB):
                t0 = b * TB
                par = b % 2
                cos_blk, sin_blk = cos_blk2[par], sin_blk2[par]
                xblk = xblk2[par]
                xnT_bf = xnT2[par]
                for tk in xtoks[par]:
                    R.wait_tok("vector", tk)
                R.wait_tok("tensor", xnt_toks[par])
                if b + 1 < NB:
                    R.wait("sync", SV)
                    R.wait("sync", TN)
                    if group == 1:
                        R.wait_tok("sync", st_x1)
                        R.wait_tok("sync", st_out)
                    emit_xload(R, b + 1, 1 - par, want_x=(group == 1),
                               want_xnt=True)
                if b == 0:
                    emit_qblock(0, 0)
                q_res = q_res2[par]
                R.wait("tensor", SV, q_sv[par])
                if gate_sv[0]:
                    R.wait("tensor", SV, gate_sv[0])
                for hp in range(4):
                    hA = 2 * hp
                    n_kc = b + 1
                    qsl = slice(hp * TB, hp * TB + TB)
                    if group == 1 and b == 0:
                        R.wait_tok("tensor", k_toks[hp])
                        R.wait_tok("tensor", v_toks[hp])
                        R.wait("tensor", SV, ones_sv)
                    exp_ss = [0] * 4
                    mask_sv = [0] * 4

                    def emit_av(kc, kk, diag):
                        R.wait("tensor", SS, exp_ss[kk])
                        if diag:
                            R.wait("tensor", SV, mask_sv[kk])
                        qs = kk * 128 if diag else 0
                        kcg = kc * 4 + kk
                        first = (kc == 0 and kk == 0)
                        last = (diag and kk == 3)
                        if first:
                            R.wait("tensor", SV, ctx_sv[0])
                        vA = bass.AP(v_res[:].tensor,
                                     v_res[:].offset + ((hA) * 32 + kcg) * 65,
                                     [v_res[:].ap[0], [1, 65]])
                        vB = bass.AP(v_res[:].tensor,
                                     v_res[:].offset + ((hA + 1) * 32 + kcg) * 65,
                                     [v_res[:].ap[0], [1, 65]])
                        R.op("tensor", "matmul", ps_cA[:, qs:512], vA,
                             pTAB[:, kk * 1024 + qs: kk * 1024 + 512],
                             start=first, stop=last, inc=(TN, 1),
                             skip_group_check=True)
                        R.op("tensor", "matmul", ps_cB[:, qs:512], vB,
                             pTAB[:, kk * 1024 + 512 + qs: kk * 1024 + 1024],
                             start=first, stop=last, inc=(TN, 1),
                             skip_group_check=True)

                    pend = []
                    for kc in range(n_kc):
                        diag = (kc == b)
                        for kk in range(4):
                            kcol = kc * 512 + kk * 128
                            sl = kk % 2
                            if slot_ss[sl]:
                                R.wait("tensor", SS, slot_ss[sl])
                            R.op("tensor", "matmul",
                                 ps_AB[:, sl * 1024: sl * 1024 + 512],
                                 kv_res[0:64, hp * S + kcol: hp * S + kcol + 128],
                                 q_res[0:64, qsl], start=True, stop=True,
                                 tile_position=(0, 0), inc=(TN, 1))
                            R.op("tensor", "matmul",
                                 ps_AB[:, sl * 1024 + 512: sl * 1024 + 1024],
                                 kv_res[64:128, hp * S + kcol: hp * S + kcol + 128],
                                 q_res[64:128, qsl], start=True, stop=True,
                                 tile_position=(64, 0), inc=(TN, 1))
                            R.wait("scalar", TN)
                            R.op("scalar", "activation",
                                 pTAB[:, kk * 1024: kk * 1024 + 1024],
                                 ps_AB[:, sl * 1024: sl * 1024 + 1024],
                                 AF.Exp, scale=0.125, inc=(SS, 1))
                            exp_ss[kk] = SS.n
                            slot_ss[sl] = SS.n
                            if diag:
                                R.wait("vector", SS, exp_ss[kk])
                                for half in range(2):
                                    col = kk * 1024 + half * 512 + kk * 128
                                    R.op("vector", "tensor_mul",
                                         pTAB[:, col:col + 128],
                                         pTAB[:, col:col + 128], tri[:],
                                         inc=(SV, 1))
                                mask_sv[kk] = SV.n
                            pend.append((kc, kk, diag))
                            if kc == 0 and kk == 1:
                                norm_tail()
                            if len(pend) > 2:
                                emit_av(*pend.pop(0))
                    for args_ in pend:
                        emit_av(*args_)
                    pend = []
                    if hp == 0 and b + 1 < NB:
                        emit_qblock(b + 1, 1 - par)
                    R.wait("vector", TN)
                    if znt_tn[0]:
                        R.wait("vector", TN, znt_tn[0])
                    R.op("vector", "reciprocal", zrow[:], ps_cA[64:65, :],
                         inc=(SV, 1))
                    R.op("vector", "reciprocal", zrow2[:], ps_cB[64:65, :],
                         inc=(SV, 1))
                    pend_nt[0] = (hA, t0, SV.n)
                norm_tail()
                if group == 1:
                    ctx_toks = []
                    for ic in range(4):
                        src = bass.AP(ctx_hbm.tensor,
                                      ctx_hbm.offset + ic * 128 * S + t0,
                                      [[S, 128], [1, TB]])
                        ctx_toks.append(R.dma(ctxa_ld[:, bass.ts(ic, TB)], src))
                    for dp in range(8):
                        if dp == 0:
                            R.wait_tok("tensor", wp_tok)
                            for tk in ctx_toks:
                                R.wait_tok("tensor", tk)
                        R.wait("tensor", SV)
                        R.wait("tensor", SS)
                        for hd in range(8):
                            lhs = wpT[:, hd * D + dp * 128: hd * D + dp * 128 + 128]
                            rhs = (ctxa_ld[:, bass.ts(hd, TB)] if hd < 4
                                   else ctxb_res[:, bass.ts(hd - 4, TB)])
                            R.op("tensor", "matmul", ps_C[:], lhs, rhs,
                                 start=(hd == 0), stop=(hd == 7), inc=(TN, 1))
                        src = bass.AP(g_hbm.tensor, g_hbm.offset + dp * 128 * S + t0,
                                      [[S, 128], [1, TB]])
                        R.wait("sync", SV)
                        tok_g = R.dma(g_fc[:], src)
                        R.wait_tok("vector", tok_g)
                        R.wait("vector", TN)
                        R.op("vector", "scalar_tensor_tensor", sg_fc[:], ps_C[:],
                             projb_c[:, dp:dp + 1], g_fc[:], ALU.add, ALU.mult,
                             inc=(SV, 1))
                        R.wait("tensor", SV)
                        R.wait("tensor", SS)
                        for tc in range(4):
                            R.op("tensor", "transpose",
                                 ps_AB[:, bass.ts(tc, 128)],
                                 sg_fc[:, bass.ts(tc, 128)], identf[:],
                                 inc=(TN, 1))
                        R.wait("vector", TN)
                        o_d = bass.AP(xblk[:].tensor, xblk[:].offset + dp * 128,
                                      [xblk[:].ap[0], [D, 4], [1, 128]])
                        p_i = bass.AP(ps_AB[:].tensor, ps_AB[:].offset,
                                      [ps_AB[:].ap[0], [128, 4], [1, 128]])
                        R.op("vector", "tensor_add", o_d, p_i, o_d, inc=(SV, 1))
                    gate_sv[0] = SV.n
                    R.wait("sync", SV)
                    dst = bass.AP(x1_hbm.tensor, x1_hbm.offset + t0 * D,
                                  [[D, 128], [128 * D, 4], [1, D]])
                    x1src = bass.AP(xblk[:].tensor, xblk[:].offset,
                                    [xblk[:].ap[0], [D, 4], [1, D]])
                    st_x1 = R.dma(dst, x1src)
                    if stop_after == "l0":
                        dst2 = bass.AP(out.tensor, out.offset + t0 * D,
                                       [[D, 128], [128 * D, 4], [1, D]])
                        st_out = R.dma(dst2, x1src)
            R.wait_all_dma("sync")
            R.play()

    if stop_after == "p1":
        l0.close()
        return nc
    attn_phase(0)
    if stop_after == "p2a":
        l0.close()
        return nc
    attn_phase(1)
    l0.close()
    nc.all_engine_barrier()
    if stop_after == "l0":
        return nc

    bp_hbm = scratch("bp_hbm", [NST, S], F32)
    cp_hbm = scratch("cp_hbm", [NST, S], F32)

    l1 = ExitStack()

    with ExitStack() as ph:
        R = Rec(nc, dpool, AUTO)
        xb2 = [ph.enter_context(nc.sbuf_tensor(f"xblk1{i}", [128, 4 * D], F32))
               for i in range(2)]
        xnblk1 = ph.enter_context(nc.sbuf_tensor("xnblk1", [128, 4 * D], F32))
        sqdump1 = ph.enter_context(nc.sbuf_tensor("sqdump1", [128, D], F32))
        s2t1 = ph.enter_context(nc.sbuf_tensor("s2t1", [128, 4], F32))
        stat1 = {k: ph.enter_context(nc.sbuf_tensor("s1_" + k, [128, 4], F32))
                 for k in ("m", "var", "rstd", "y2", "u", "w")}
        wdtT = ph.enter_context(nc.sbuf_tensor("wdtT", [128, 8 * D], BF16))
        wq1T = ph.enter_context(nc.sbuf_tensor("wq1T", [128, 8 * D], BF16))
        xn1T_bf = ph.enter_context(nc.sbuf_tensor("xn1T_bf", [128, 8 * TB], BF16))
        xn1T_f = ph.enter_context(nc.sbuf_tensor("xn1T_f", [128, 8 * TB], F32))
        aqT = ph.enter_context(nc.sbuf_tensor("aqT", [128, 8 * TB], BF16))
        dtblk = ph.enter_context(nc.sbuf_tensor("dtblk", [128, 8 * TB], F32))
        dtxblk = ph.enter_context(nc.sbuf_tensor("dtxblk", [128, 8 * TB], FP16))
        t1t = ph.enter_context(nc.sbuf_tensor("t1t", [128, 8 * TB], F32))
        aqtmp = ph.enter_context(nc.sbuf_tensor("aqtmp", [128, 4 * D], F32))
        rmax4 = ph.enter_context(nc.sbuf_tensor("rmax4", [128, 4], F32))
        arcp = ph.enter_context(nc.sbuf_tensor("arcp", [128, 1], F32))
        sclrow = ph.enter_context(nc.sbuf_tensor("sclrow", [1, TB], F32))
        sclb = ph.enter_context(nc.sbuf_tensor("sclb", [128, TB], F32))
        g1stage = ph.enter_context(nc.sbuf_tensor("g1stage", [128, TB], FP16))
        bpstage = ph.enter_context(nc.sbuf_tensor("bpstage", [8, TB], F32))
        cpstage = ph.enter_context(nc.sbuf_tensor("cpstage", [8, TB], F32))
        ps_tr = ph.enter_context(nc.psum_tensor("l1_tr", [128, 512], F32))
        ps_dt = [ph.enter_context(nc.psum_tensor(f"l1_dt{i}", [128, 512], F32))
                 for i in range(2)]
        ps_bp = ph.enter_context(nc.psum_tensor("l1_bp", [8, 512], F32))
        ps_bn = [ph.enter_context(nc.psum_tensor(f"l1_bn{i}", [128, 512], F32))
                 for i in range(2)]
        ps_msc = ph.enter_context(nc.psum_tensor("l1_msc", [128, 512], F32))

        wdts = bass.AP(wdtT_h.tensor, wdtT_h.offset,
                       [[D, 128], [D * 128, 8], [1, D]])
        wdtd = bass.AP(wdtT[:].tensor, wdtT[:].offset,
                       [wdtT[:].ap[0], [D, 8], [1, D]])
        tok_a = R.dma(wdtd, wdts)
        wqs = bass.AP(wq1T_h.tensor, wq1T_h.offset,
                      [[D, 128], [D * 128, 8], [1, D]])
        wqd = bass.AP(wq1T[:].tensor, wq1T[:].offset,
                      [wq1T[:].ap[0], [D, 8], [1, D]])
        tok_b = R.dma(wqd, wqs)
        R.wait_tok("tensor", tok_a)
        R.wait_tok("tensor", tok_b)
        st_dt = None
        st_dtx = None
        aq_tn = [0]
        bp_sv = [0]
        st_g1 = None
        st_bp = None
        st_cp = None
        st_xt = None

        x1toks = {}

        def xload1(b_, par_):
            src = bass.AP(x1_hbm.tensor, x1_hbm.offset + b_ * TB * D,
                          [[D, 128], [128 * D, 4], [1, D]])
            xb = xb2[par_]
            xdst = bass.AP(xb[:].tensor, xb[:].offset,
                           [xb[:].ap[0], [D, 4], [1, D]])
            x1toks[par_] = R.dma(xdst, src)

        def emit_ln14(R, src_tile, dst_tile):
            m4, var4, rstd4 = stat1["m"], stat1["var"], stat1["rstd"]
            y24, u4, w4 = stat1["y2"], stat1["u"], stat1["w"]
            for tc in range(4):
                R.op("vector", "reduce_sum", m4[:, tc:tc + 1],
                     src_tile[:, bass.ts(tc, D)], AX.X)
                R.wait("scalar", SV)
                R.op("scalar", "activation", sqdump1[:],
                     src_tile[:, bass.ts(tc, D)], AF.Square,
                     accum_out=s2t1[:, tc:tc + 1], inc=(SS, 1))
            R.wait("vector", SS)
            R.op("vector", "tensor_scalar", m4[:], m4[:], 1.0 / D, None, ALU.mult)
            R.op("vector", "tensor_scalar", var4[:], s2t1[:], 1.0 / D, None,
                 ALU.mult)
            R.op("vector", "tensor_mul", y24[:], m4[:], m4[:])
            R.op("vector", "tensor_sub", var4[:], var4[:], y24[:])
            R.op("vector", "tensor_scalar", var4[:], var4[:], 1.0, 1e-5,
                 ALU.mult, ALU.add)
            R.op("vector", "reciprocal", rstd4[:], var4[:])
            for _ in range(4):
                R.op("vector", "tensor_mul", y24[:], rstd4[:], rstd4[:])
                R.op("vector", "tensor_mul", u4[:], var4[:], y24[:])
                R.op("vector", "tensor_scalar", w4[:], u4[:], -0.5, 1.5,
                     ALU.mult, ALU.add)
                R.op("vector", "tensor_mul", rstd4[:], rstd4[:], w4[:])
            for tc in range(4):
                R.op("vector", "tensor_scalar", dst_tile[:, bass.ts(tc, D)],
                     src_tile[:, bass.ts(tc, D)], m4[:, tc:tc + 1],
                     rstd4[:, tc:tc + 1], ALU.subtract, ALU.mult, inc=(SV, 1))

        xload1(0, 0)
        for b in range(NB):
            t0 = b * TB
            par = b % 2
            xblk1 = xb2[par]
            R.wait_tok("vector", x1toks[par])
            R.wait_tok("scalar", x1toks[par])
            if b + 1 < NB:
                R.wait("sync", SV)
                R.wait("sync", SS)
                xload1(b + 1, 1 - par)
            emit_ln14(R, xblk1[:], xnblk1[:])
            R.wait_tok("vector", st_xt)
            xtsv = {0: 0, 1: 0}
            R.wait("tensor", SV)
            for ic in range(8):
                pst = (ps_tr, ps_msc)[ic % 2]
                if xtsv[ic % 2]:
                    R.wait("tensor", SV, xtsv[ic % 2])
                for tc in range(4):
                    R.op("tensor", "transpose", pst[:, bass.ts(tc, 128)],
                         xnblk1[:, tc * D + ic * 128: tc * D + ic * 128 + 128],
                         identf[:], inc=(TN, 1))
                R.wait("vector", TN)
                R.op("vector", "tensor_copy", xn1T_bf[:, bass.ts(ic, TB)],
                     pst[:, 0:512])
                R.op("vector", "tensor_copy", xn1T_f[:, bass.ts(ic, TB)],
                     pst[:, 0:512])
                xtsv[ic % 2] = SV.n
            R.wait("sync", SV)
            R.wait_tok("sync", st_xt)
            xts = bass.AP(xn1T_bf[:].tensor, xn1T_bf[:].offset,
                          [xn1T_bf[:].ap[0], [TB, 8], [1, TB]])
            xtd = bass.AP(xn1T_hbm.tensor, xn1T_hbm.offset + t0,
                          [[S, 128], [S * 128, 8], [1, TB]])
            st_xt = R.dma(xtd, xts)
            for tc in range(4):
                xc = xnblk1[:, bass.ts(tc, D)]
                R.op("vector", "reduce_max", rmax4[:, tc:tc + 1], xc, AX.X,
                     apply_absolute_value=True, inc=(SV, 1))
            R.op("vector", "tensor_scalar", stat1["u"][:], rmax4[:],
                 1.0, 1e-8, ALU.mult, ALU.add, inc=(SV, 1))
            R.op("vector", "reciprocal", stat1["u"][:], stat1["u"][:],
                 inc=(SV, 1))
            if aq_tn[0]:
                R.wait("vector", TN, aq_tn[0])
            for tc in range(4):
                xc = xnblk1[:, bass.ts(tc, D)]
                R.op("vector", "tensor_scalar", aqtmp[:, bass.ts(tc, D)], xc,
                     stat1["u"][:, tc:tc + 1], 127.0, ALU.mult, ALU.mult,
                     inc=(SV, 1))
            R.op("vector", "tensor_scalar", aqtmp[:], aqtmp[:], MAGIC, MAGIC,
                 ALU.add, ALU.subtract, inc=(SV, 1))
            for dc in range(8):
                pd = ps_dt[dc % 2]
                R.wait("tensor", SS)
                R.wait("tensor", SV)
                for ic in range(8):
                    R.op("tensor", "matmul", pd[:],
                         wdtT[:, ic * D + dc * 128: ic * D + dc * 128 + 128],
                         xn1T_bf[:, bass.ts(ic, TB)],
                         start=(ic == 0), stop=(ic == 7), inc=(TN, 1))
                R.wait("scalar", TN)
                if dc == 0:
                    R.wait("scalar", SV)
                R.op("scalar", "activation", t1t[:, bass.ts(dc, TB)], pd[:],
                     AF.Exp, bias=bdt_c[:, dc:dc + 1], inc=(SS, 1))
            R.op("scalar", "activation", t1t[:], t1t[:], AF.Ln,
                 bias=c1[:, 0:1], inc=(SS, 1))
            R.op("scalar", "activation", t1t[:], t1t[:], AF.Tanh,
                 scale=1.0 / 0.099, bias=cS1[:, 0:1], inc=(SS, 1))
            R.wait("vector", SS)
            R.wait_tok("vector", st_dt)
            R.op("vector", "tensor_scalar", dtblk[:], t1t[:], 0.099, 0.0505,
                 ALU.mult, ALU.add, inc=(SV, 1))
            R.wait_tok("vector", st_dtx)
            R.op("vector", "tensor_mul", dtxblk[:], dtblk[:], xn1T_f[:],
                 inc=(SV, 1))
            R.wait("sync", SV)
            dstd = bass.AP(dt_hbm.tensor, dt_hbm.offset + t0,
                           [[S, 128], [128 * S, 8], [1, TB]])
            dsrc = bass.AP(dtblk[:].tensor, dtblk[:].offset,
                           [dtblk[:].ap[0], [TB, 8], [1, TB]])
            st_dt = R.dma(dstd, dsrc)
            dstx = bass.AP(dtx_hbm.tensor, dtx_hbm.offset + t0,
                           [[S, 128], [128 * S, 8], [1, TB]])
            xsrc = bass.AP(dtxblk[:].tensor, dtxblk[:].offset,
                           [dtxblk[:].ap[0], [TB, 8], [1, TB]])
            st_dtx = R.dma(dstx, xsrc)
            trsv = {0: 0, 1: 0}
            R.wait("tensor", SV)
            for ic in range(8):
                pst = (ps_tr, ps_msc)[ic % 2]
                if trsv[ic % 2]:
                    R.wait("tensor", SV, trsv[ic % 2])
                for tc in range(4):
                    R.op("tensor", "transpose", pst[:, bass.ts(tc, 128)],
                         aqtmp[:, tc * D + ic * 128: tc * D + ic * 128 + 128],
                         identf[:], inc=(TN, 1))
                R.wait("vector", TN)
                dsta = bass.AP(aqT[:].tensor, aqT[:].offset + ic * TB,
                               [aqT[:].ap[0], [128, 4], [1, 128]])
                srca = bass.AP(pst[:].tensor, pst[:].offset,
                               [pst[:].ap[0], [128, 4], [1, 128]])
                R.op("vector", "tensor_copy", dsta, srca, inc=(SV, 1))
                trsv[ic % 2] = SV.n
                aq_tn[0] = TN.n
            R.wait("tensor", SV)
            for tc in range(4):
                R.op("tensor", "transpose", ps_msc[0:1, bass.ts(tc, 128)],
                     rmax4[:, tc:tc + 1], identf[:], inc=(TN, 1))
            R.wait("scalar", TN)
            R.op("scalar", "activation", sclrow[:], ps_msc[0:1, 0:512], AF.Identity,
                 scale=wsd127[0:1, 0:1], inc=(SS, 1))
            R.wait("tensor", SS)
            R.op("tensor", "matmul", ps_msc[:, 0:512], ones_f[:], sclrow[:],
                 start=True, stop=True, inc=(TN, 1))
            R.wait("vector", TN)
            R.op("vector", "tensor_copy", sclb[:], ps_msc[:, 0:512], inc=(SV, 1))
            for wt, stage, hbm, sttok in ((wBT, bpstage, bp_hbm, "bp"),
                                          (wCT, cpstage, cp_hbm, "cp")):
                if bp_sv[0]:
                    R.wait("tensor", SV, bp_sv[0])
                for ic in range(8):
                    R.op("tensor", "matmul", ps_bp[:],
                         wt[:, ic * 8: ic * 8 + 8],
                         xn1T_f[:, bass.ts(ic, TB)],
                         start=(ic == 0), stop=(ic == 7), inc=(TN, 1))
                R.wait("vector", TN)
                R.wait_tok("vector", st_bp if sttok == "bp" else st_cp)
                R.op("vector", "tensor_copy", stage[:], ps_bp[:], inc=(SV, 1))
                bp_sv[0] = SV.n
                R.wait("sync", SV)
                dstb = bass.AP(hbm.tensor, hbm.offset + t0, [[S, 8], [1, TB]])
                if sttok == "bp":
                    st_bp = R.dma(dstb, stage[:])
                else:
                    st_cp = R.dma(dstb, stage[:])
            for fc in range(8):
                pb = ps_bn[fc % 2]
                R.wait("tensor", SV)
                for ic in range(8):
                    R.op("tensor", "matmul", pb[:],
                         wq1T[:, ic * D + fc * 128: ic * D + fc * 128 + 128],
                         aqT[:, bass.ts(ic, TB)],
                         start=(ic == 0), stop=(ic == 7), inc=(TN, 1))
                R.wait("vector", TN)
                R.wait("vector", SS)
                R.wait_tok("vector", st_g1)
                R.op("vector", "tensor_mul", g1stage[:], pb[:], sclb[:],
                     inc=(SV, 1))
                R.wait("scalar", SV)
                R.op("scalar", "activation", g1stage[:], g1stage[:], AF.Silu,
                     inc=(SS, 1))
                R.wait("sync", SS)
                dst = bass.AP(g1_hbm.tensor, g1_hbm.offset + fc * 128 * S + t0,
                              [[S, 128], [1, TB]])
                st_g1 = R.dma(dst, g1stage[:])
        R.wait_all_dma("sync")
        R.play()

    if stop_after == "l1a":
        l1.close()
        return nc

    TMAX = 120
    SSTR = 12
    from collections import deque
    with ExitStack() as ph:
        R = Rec(nc, dpool, AUTO)
        sb = lambda nm, sh, d=F32: ph.enter_context(nc.sbuf_tensor(nm, sh, d))
        dA_all = sb("dA_all", [128, 64 * TMAX], FP16)
        u_all = sb("u_all", [128, 64 * TMAX], FP16)
        Z3 = sb("Z3", [128, 3 * 64 * TMAX], FP16)
        P3 = sb("P3", [128, 3 * 64 * TMAX], FP16)
        dt8 = [sb(f"dt8_{i}", [128, 8 * TMAX]) for i in range(2)]
        dtx8 = [sb(f"dtx8_{i}", [128, 8 * TMAX], FP16) for i in range(2)]
        xn1T8 = [sb(f"xn1T8_{i}", [128, 8 * TMAX], BF16) for i in range(2)]
        bp8 = [sb(f"bp8_{i}", [1, 8 * TMAX]) for i in range(2)]
        cp8 = [sb(f"cp8_{i}", [1, 8 * TMAX]) for i in range(3)]
        y83 = sb("y83", [128, 3 * 8 * TMAX], FP16)
        etp = [sb("etp_0", [128, 16 * TMAX])]
        uptmp = sb("uptmp", [128, 16 * TMAX], FP16)
        htmp2 = [sb(f"htmp8_{i}", [128, 16 * TMAX], FP16) for i in range(2)]
        be4 = [sb(f"be4_{i}", [128, 4 * TMAX], FP16) for i in range(2)]
        ce4 = [sb(f"ce4_{i}", [128, 4 * TMAX], FP16) for i in range(2)]
        Hc = sb("Hc", [128, 64 * SSTR])
        Hprev2 = [sb(f"Hprev{i}", [128, 64 * SSTR]) for i in range(2)]
        cbf2 = [sb(f"cbf{i}", [128, SSTR]) for i in range(2)]
        tmp64 = sb("tmp64", [128, 64])
        sq64 = sb("sq64", [128, 64])
        nrmcol = sb("nrmcol", [128, 1])
        c_row2 = [sb(f"c_row{i}", [1, SSTR]) for i in range(2)]
        c_sc = sb("c_sc", [1, 8])
        hcar = sb("hcar", [128, 64])
        hcar_in = sb("hcar_in", [128, 64])
        ps_be = ph.enter_context(nc.psum_tensor("sc_be", [128, 4 * TMAX], F32))
        ps_ce = ph.enter_context(nc.psum_tensor("sc_ce", [128, 4 * TMAX], F32))
        ps_abd = ph.enter_context(nc.psum_tensor("sc_abd", [1, 8], F32))
        ps_cb = ph.enter_context(nc.psum_tensor("sc_cb", [128, SSTR], F32))

        TILES = ([(0, 1, 1)] + [(1 + 120 * j, 120, 12) for j in range(34)]
                 + [(4081, 15, 1)])
        NT = len(TILES)
        R.op("vector", "memset", hcar[:], 0.0, inc=(SV, 1))

        ld_toks = [None, None]
        cp_toks = [None, None, None]
        st_y = [None, None, None]
        ess = [0]
        essn = {}
        argsv = [0]
        be_hold = [0]
        be_rd = [0, 0]
        be_usv = [0, 0]
        bmtn = [0]
        ce_hold = [0]
        ce_rd = [0, 0]
        ce_usv = [0, 0]
        cmtn = [0]
        cpre_sv = [0]
        ht_sg = [0, 0]
        up_T = [0]
        cst = {"cb": 0, "abd": 0, "cbf": 0}

        def emit_loads(j):
            t0, T, _ = TILES[j]
            jl, js = j % 2, j % 3
            R.wait("sync", SV)
            R.wait("sync", TN)
            toks = []
            for hbm_t, dstl in ((dt_hbm, dt8), (dtx_hbm, dtx8),
                                (xn1T_hbm, xn1T8)):
                dst = dstl[jl]
                srcp = bass.AP(hbm_t.tensor, hbm_t.offset + t0,
                               [[S, 128], [S * 128, 8], [1, T]])
                dstp = bass.AP(dst[:].tensor, dst[:].offset,
                               [dst[:].ap[0], [T, 8], [1, T]])
                toks.append(R.dma(dstp, srcp,
                                  allow_slow_non_contiguous=(T < 8)))
            bdst = bass.AP(bp8[jl][:].tensor, bp8[jl][:].offset,
                           [bp8[jl][:].ap[0], [T, 8], [1, T]])
            toks.append(R.dma(bdst,
                              bass.AP(bp_hbm.tensor, bp_hbm.offset + t0,
                                      [[S * 8, 1], [S, 8], [1, T]]),
                              allow_slow_non_contiguous=(T < 8)))
            ld_toks[jl] = toks
            cdst = bass.AP(cp8[js][:].tensor, cp8[js][:].offset,
                           [cp8[js][:].ap[0], [T, 8], [1, T]])
            cp_toks[js] = R.dma(cdst,
                                bass.AP(cp_hbm.tensor, cp_hbm.offset + t0,
                                        [[S * 8, 1], [S, 8], [1, T]]),
                                allow_slow_non_contiguous=(T < 8))

        def s1_stage(j):
            t0, T, nseg = TILES[j]
            tail = T - nseg * 10
            nst = nseg + (1 if tail > 0 else 0)
            seglen = 10 if T >= 10 else T
            T8 = 8 * T
            jl, js = j % 2, j % 3
            zoff = js * 64 * TMAX
            yoff = js * 8 * TMAX
            pre_start = []
            pre_pairs = [[] for _ in range(4)]
            ops = []
            em = lambda c, f: ops.append((c, f))
            emp = lambda c, f: pre_start.append((c, f))

            def fstart():
                for tk in ld_toks[jl]:
                    R.wait_tok("vector", tk)
                R.wait_tok("tensor", ld_toks[jl][3])
            emp(0, fstart)
            for q in range(2):
                def f_mm(q=q):
                    if be_hold[0]:
                        R.wait("tensor", SS, be_hold[0])
                    R.op("tensor", "matmul", ps_be[:, 0:4 * T], ones_f[:],
                         bp8[jl][0:1, q * 4 * T: (q + 1) * 4 * T],
                         start=True, stop=True, inc=(TN, 1))
                    bmtn[0] = TN.n

                def f_evac(q=q):
                    R.wait("scalar", TN, bmtn[0])
                    if be_usv[q]:
                        R.wait("scalar", SV, be_usv[q])
                    R.op("scalar", "copy", be4[q][:, 0:4 * T],
                         ps_be[:, 0:4 * T], inc=(SS, 1))
                    be_hold[0] = SS.n
                    be_rd[q] = SS.n
                emp(0, f_mm)
                emp(0, f_evac)
            for p in range(4):
                n = 2 * p
                nof = n * 8 * T
                ep = p % 2

                def f_arg(n=n, ep=ep):
                    if ess[0]:
                        R.wait("vector", SS, ess[0])
                    et = etp[0]
                    arg_o = bass.AP(et[:].tensor, et[:].offset,
                                    [et[:].ap[0], [T8, 2], [T, 8], [1, T]])
                    dt_i = bass.AP(dt8[jl][:].tensor, dt8[jl][:].offset,
                                   [dt8[jl][:].ap[0], [0, 2], [T, 8], [1, T]])
                    a_i = bass.AP(a16[:].tensor, a16[:].offset + n,
                                  [a16[:].ap[0], [1, 2], [8, 8], [0, T]])
                    R.op("vector", "tensor_mul", arg_o, dt_i, a_i, inc=(SV, 1))
                    argsv[0] = SV.n

                def f_tanh(n=n, nof=nof, ep=ep, p=p):
                    R.wait("scalar", SV, argsv[0])
                    et = etp[0]
                    R.op("scalar", "activation", et[:, 0:2 * T8], et[:, 0:2 * T8],
                         AF.Tanh, inc=(SS, 1))
                    R.op("scalar", "activation", dA_all[:, nof: nof + 2 * T8],
                         et[:, 0:2 * T8], AF.Exp, scale=16.0, inc=(SS, 1))
                    ess[0] = SS.n
                    essn[(j, p)] = SS.n
                pre_pairs[p].append((2.2, f_arg))
                pre_pairs[p].append((0, f_tanh))
            for p in range(4):
                n = 2 * p
                nof = n * 8 * T
                q = p // 2

                def f_u(n=n, nof=nof, q=q):
                    R.wait("vector", SS, be_rd[q])
                    u_o = bass.AP(u_all[:].tensor, u_all[:].offset + nof,
                                  [u_all[:].ap[0], [T8, 2], [T, 8], [1, T]])
                    dtx_i = bass.AP(dtx8[jl][:].tensor, dtx8[jl][:].offset,
                                    [dtx8[jl][:].ap[0], [0, 2], [T, 8], [1, T]])
                    be_i = bass.AP(be4[q][:].tensor,
                                   be4[q][:].offset + (n % 4) * T,
                                   [be4[q][:].ap[0], [T, 2], [0, 8], [1, T]])
                    R.op("vector", "tensor_mul", u_o, dtx_i, be_i, inc=(SV, 1))
                    be_usv[q] = SV.n

                def f_up(n=n, nof=nof, p=p):
                    R.wait("vector", SS, essn[(j, p)])
                    if up_T[0] != T:
                        R.op("vector", "memset", uptmp[:, 0:16 * TMAX], 0.0,
                             inc=(SV, 1))
                        up_T[0] = T
                    upd = bass.AP(uptmp[:].tensor, uptmp[:].offset,
                                  [uptmp[:].ap[0], [T8, 2], [T, 8], [10, nst]])
                    das = bass.AP(dA_all[:].tensor, dA_all[:].offset + nof,
                                  [dA_all[:].ap[0], [T8, 2], [T, 8], [10, nst]])
                    R.op("vector", "tensor_copy", upd, das, inc=(SV, 1))
                    R.op("vector", "memset", das, 0.0, inc=(SV, 1))

                def f_zscan(nof=nof):
                    R.op("vector", "tensor_tensor_scan",
                         Z3[:, zoff + nof: zoff + nof + 2 * T8],
                         dA_all[:, nof: nof + 2 * T8],
                         u_all[:, nof: nof + 2 * T8],
                         0.0, ALU.mult, ALU.add, inc=(SV, 1))

                def f_pscan(nof=nof):
                    R.op("vector", "tensor_tensor_scan",
                         P3[:, zoff + nof: zoff + nof + 2 * T8],
                         dA_all[:, nof: nof + 2 * T8], uptmp[:, 0:2 * T8],
                         0.0, ALU.mult, ALU.add, inc=(SV, 1))

                em(1.4, f_u)

                def f_upz(f1=f_up, f2=f_zscan):
                    f1()
                    f2()
                em(5.8, f_upz)
                em(4.6, f_pscan)

            def f_yinit():
                R.wait_tok("vector", st_y[js])
                y_o = bass.AP(y83[:].tensor, y83[:].offset + yoff,
                              [y83[:].ap[0], [T, 8], [1, T]])
                x_i = bass.AP(xn1T8[jl][:].tensor, xn1T8[jl][:].offset,
                              [xn1T8[jl][:].ap[0], [T, 8], [1, T]])
                d_i = bass.AP(dpar_c[:].tensor, dpar_c[:].offset,
                              [dpar_c[:].ap[0], [1, 8], [0, T]])
                R.op("vector", "tensor_mul", y_o, x_i, d_i, inc=(SV, 1))
            em(0.7, f_yinit)
            if j + 1 < NT:
                em(0, lambda j=j: emit_loads(j + 1))
            return pre_start, pre_pairs, ops

        def s2_stage(j):
            t0, T, nseg = TILES[j]
            tail = T - nseg * 10
            seglen = 10 if T >= 10 else T
            T8 = 8 * T
            js, jp = j % 3, j % 2
            zoff = js * 64 * TMAX
            yoff = js * 8 * TMAX
            ops = []
            em = lambda c, f: ops.append((c, f))
            def f_cpre():
                R.wait_tok("vector", cp_toks[js])
                ce_d = bass.AP(cp8[js][:].tensor, cp8[js][:].offset + seglen - 1,
                               [cp8[js][:].ap[0], [T, 8], [10, nseg]])
                cr_i = bass.AP(c_row2[jp][:].tensor, c_row2[jp][:].offset,
                               [c_row2[jp][:].ap[0], [0, 8], [1, nseg]])
                R.op("vector", "tensor_mul", ce_d, ce_d, cr_i, inc=(SV, 1))
                cpre_sv[0] = SV.n
            em(0.3, f_cpre)
            for q in range(2):
                def f_mm(q=q):
                    if ce_hold[0]:
                        R.wait("tensor", SS, ce_hold[0])
                    R.wait("tensor", SV, cpre_sv[0])
                    R.op("tensor", "matmul", ps_ce[:, 0:4 * T], ones_f[:],
                         cp8[js][0:1, q * 4 * T: (q + 1) * 4 * T],
                         start=True, stop=True, inc=(TN, 1))
                    cmtn[0] = TN.n

                def f_evac(q=q):
                    R.wait("scalar", TN, cmtn[0])
                    if ce_usv[q]:
                        R.wait("scalar", SV, ce_usv[q])
                    R.op("scalar", "copy", ce4[q][:, 0:4 * T],
                         ps_ce[:, 0:4 * T], inc=(SS, 1))
                    ce_hold[0] = SS.n
                    ce_rd[q] = SS.n
                em(0, f_mm)
                em(0, f_evac)
            seq = []
            adds_sg = {}
            hc_sv = {}
            y_sg = {}
            for p in range(4):
                n = 2 * p
                nof = n * 8 * T
                q = p // 2
                hb = htmp2[p % 2]

                def f_pmul(n=n, nof=nof, p=p, hb=hb):
                    for h in range(2):
                        hof = h * T8
                        p4 = bass.AP(P3[:].tensor,
                                     P3[:].offset + zoff + nof + hof,
                                     [P3[:].ap[0], [T, 8], [10, nseg],
                                      [1, seglen]])
                        h4 = bass.AP(Hprev2[jp][:].tensor,
                                     Hprev2[jp][:].offset + (n + h) * 8 * SSTR,
                                     [Hprev2[jp][:].ap[0], [SSTR, 8], [1, nseg],
                                      [0, seglen]])
                        o4 = bass.AP(hb[:].tensor, hb[:].offset + hof,
                                     [hb[:].ap[0], [T, 8], [10, nseg],
                                      [1, seglen]])
                        R.op("vector", "tensor_mul", o4, p4, h4, inc=(SV, 1))
                        if tail > 0:
                            pt = bass.AP(P3[:].tensor,
                                         P3[:].offset + zoff + nof + hof
                                         + nseg * 10,
                                         [P3[:].ap[0], [T, 8], [1, tail]])
                            ht = bass.AP(Hprev2[jp][:].tensor,
                                         Hprev2[jp][:].offset
                                         + (n + h) * 8 * SSTR + nseg,
                                         [Hprev2[jp][:].ap[0], [SSTR, 8],
                                          [0, tail]])
                            ot = bass.AP(hb[:].tensor,
                                         hb[:].offset + hof + nseg * 10,
                                         [hb[:].ap[0], [T, 8], [1, tail]])
                            R.op("vector", "tensor_mul", ot, pt, ht,
                                 inc=(SV, 1))

                def f_add(nof=nof, p=p, hb=hb):
                    R.op("vector", "tensor_add", hb[:, 0:2 * T8],
                         hb[:, 0:2 * T8],
                         Z3[:, zoff + nof: zoff + nof + 2 * T8], inc=(SV, 1))

                def f_hc(n=n, q=q, p=p, hb=hb):
                    R.wait("vector", SS, ce_rd[q])
                    h_i = bass.AP(hb[:].tensor, hb[:].offset,
                                  [hb[:].ap[0], [T8, 2], [T, 8], [1, T]])
                    ce_i = bass.AP(ce4[q][:].tensor,
                                   ce4[q][:].offset + (n % 4) * T,
                                   [ce4[q][:].ap[0], [T, 2], [0, 8], [1, T]])
                    R.op("vector", "tensor_mul", h_i, h_i, ce_i, inc=(SV, 1))
                    ce_usv[q] = SV.n
                    hc_sv[p] = SV.n

                def f_y(p=p, hb=hb):
                    for h in range(2):
                        R.op("vector", "tensor_add",
                             y83[:, yoff: yoff + T8],
                             y83[:, yoff: yoff + T8],
                             hb[:, h * T8: (h + 1) * T8], inc=(SV, 1))
                seq.append((p, f_pmul, f_add, f_hc, f_y))
            em(1.6, seq[0][1])
            em(1.2, seq[0][2])
            em(1.6, seq[1][1])
            em(1.2, seq[1][2])
            for p in range(4):
                if p + 2 < 4:
                    em(0.9, seq[p][3])
                    em(1.2, seq[p][4])
                    em(1.6, seq[p + 2][1])
                    em(1.2, seq[p + 2][2])
                else:
                    em(0.9, seq[p][3])
                    em(1.2, seq[p][4])
            def f_store():
                R.wait("sync", SV)
                dsty = bass.AP(y_hbm.tensor, y_hbm.offset + t0,
                               [[S, 128], [128 * S, 8], [1, T]])
                ysrc = bass.AP(y83[:].tensor, y83[:].offset + yoff,
                               [y83[:].ap[0], [T, 8], [1, T]])
                st_y[js] = R.dma(dsty, ysrc, allow_slow_non_contiguous=(T < 8))
            em(0, f_store)
            return ops

        def chain_stage(j, filler, bigq=None):
            t0, T, nseg = TILES[j]
            tail = T - nseg * 10
            nst = nseg + (1 if tail > 0 else 0)
            seglen = 10 if T >= 10 else T
            jp, js = j % 2, j % 3
            zoff = js * 64 * TMAX

            def take(budget):
                acc = 0.0
                while filler and acc < budget:
                    c, fn = filler.popleft()
                    fn()
                    acc += c
                if acc < budget and bigq:
                    c, fn = bigq.popleft()
                    fn()

            R.op("vector", "tensor_copy", hcar_in[:], hcar[:], inc=(SV, 1))
            take(1.0)
            for k in range(nseg):
                eoff = zoff + k * 10 + seglen - 1
                gsl = bass.AP(P3[:].tensor, P3[:].offset + eoff,
                              [P3[:].ap[0], [T, 64]])
                lsl = bass.AP(Z3[:].tensor, Z3[:].offset + eoff,
                              [Z3[:].ap[0], [T, 64]])
                hck = bass.AP(Hc[:].tensor, Hc[:].offset + k,
                              [Hc[:].ap[0], [SSTR, 64]])
                if k == 0:
                    R.op("vector", "scalar_tensor_tensor", tmp64[:], hcar_in[:],
                         1.0, gsl, ALU.mult, ALU.mult, inc=(SV, 1))
                else:
                    prev = bass.AP(Hc[:].tensor, Hc[:].offset + k - 1,
                                   [Hc[:].ap[0], [SSTR, 64]])
                    R.wait("vector", TN, cst["cb"])
                    R.op("vector", "scalar_tensor_tensor", tmp64[:], prev,
                         ps_cb[:, 0:1], gsl, ALU.mult, ALU.mult, inc=(SV, 1))
                R.op("vector", "tensor_add", hck, tmp64[:], lsl, inc=(SV, 1))
                R.op("vector", "scalar_tensor_tensor", sq64[:], hck, 1.0, hck,
                     ALU.mult, ALU.mult, accum_out=nrmcol[:], inc=(SV, 1))
                sv_head = SV.n
                R.wait("tensor", SV, sv_head)
                if cst["abd"]:
                    R.wait("tensor", SS, cst["abd"])
                R.op("tensor", "matmul", ps_abd[0:1, 0:1], ones_c[:], nrmcol[:],
                     start=True, stop=True, inc=(TN, 1))
                tn1 = TN.n
                R.wait("scalar", TN, tn1)
                R.op("scalar", "activation", c_sc[0:1, 0:1], ps_abd[0:1, 0:1],
                     AF.Ln, inc=(SS, 1))
                cst["abd"] = SS.n
                R.op("scalar", "activation", c_sc[0:1, 1:2], c_sc[0:1, 0:1],
                     AF.Exp, scale=-0.5, inc=(SS, 1))
                ss_k = SS.n
                take(2.2)
                R.wait("vector", SS, ss_k)
                if k == 0 and cst["cbf"]:
                    R.wait("vector", TN, cst["cbf"])
                R.op("vector", "tensor_scalar", c_row2[jp][0:1, k:k + 1],
                     c_sc[0:1, 1:2], 20.0, 1.0, ALU.mult, ALU.min, inc=(SV, 1))
                sv_tail = SV.n
                if k + 1 < nseg:
                    R.wait("tensor", SV, sv_tail)
                    R.op("tensor", "matmul", ps_cb[:, 0:1], ones_f[:],
                         c_row2[jp][0:1, k:k + 1], start=True, stop=True,
                         inc=(TN, 1))
                    cst["cb"] = TN.n
                take(1.2)
            R.wait("tensor", SV, sv_tail)
            R.op("tensor", "matmul", ps_cb[:, 0:nseg], ones_f[:],
                 c_row2[jp][0:1, 0:nseg], start=True, stop=True, inc=(TN, 1))
            cst["cbf"] = TN.n
            R.wait("vector", TN, cst["cbf"])
            R.op("vector", "tensor_copy", cbf2[jp][:, 0:nseg], ps_cb[:, 0:nseg],
                 inc=(SV, 1))
            hl = bass.AP(Hc[:].tensor, Hc[:].offset + nseg - 1,
                         [Hc[:].ap[0], [SSTR, 64]])
            R.op("vector", "tensor_scalar", hcar[:], hl,
                 cbf2[jp][:, nseg - 1:nseg], None, ALU.mult, inc=(SV, 1))
            hp0 = bass.AP(Hprev2[jp][:].tensor, Hprev2[jp][:].offset,
                          [Hprev2[jp][:].ap[0], [SSTR, 64]])
            R.op("vector", "tensor_copy", hp0, hcar_in[:], inc=(SV, 1))
            if nst > 1:
                hpk = bass.AP(Hprev2[jp][:].tensor, Hprev2[jp][:].offset + 1,
                              [Hprev2[jp][:].ap[0], [SSTR, 64], [1, nst - 1]])
                hck2 = bass.AP(Hc[:].tensor, Hc[:].offset,
                               [Hc[:].ap[0], [SSTR, 64], [1, nst - 1]])
                cbk = bass.AP(cbf2[jp][:].tensor, cbf2[jp][:].offset,
                              [cbf2[jp][:].ap[0], [0, 64], [1, nst - 1]])
                R.op("vector", "tensor_mul", hpk, hck2, cbk, inc=(SV, 1))

        emit_loads(0)
        sB = {}
        ps0, pp0, opsB0 = s1_stage(0)
        sB[0] = opsB0
        for c, fn in ps0:
            fn()
        for pl in pp0:
            for c, fn in pl:
                fn()
        for r in range(NT + 2):
            items = []
            if r - 2 >= 0:
                items.extend(s2_stage(r - 2))
            items.extend(sB.pop(r, []))
            if r + 1 < NT:
                pre_start, pre_pairs, opsB = s1_stage(r + 1)
                sB[r + 1] = opsB
            else:
                pre_start, pre_pairs = [], [[] for _ in range(4)]
            small = deque(x for x in items if x[0] <= 2.5)
            bigs = [x for x in items if x[0] > 2.5]
            big = deque()
            nb = len(bigs)
            same_T = (r < NT and r + 1 < NT and TILES[r][1] == TILES[r + 1][1])
            if same_T and nb:
                for i in range(max(nb, 8)):
                    if i < nb:
                        big.append(bigs[i])
                    if i == 1:
                        big.extend(pre_start)
                        pre_start = []
                    if i % 2 == 1 and i // 2 < 4:
                        big.extend(pre_pairs[i // 2])
            else:
                big.extend(bigs)
                big.extend(pre_start)
                for pl in pre_pairs:
                    big.extend(pl)
            if 0 <= r - 1 < NT:
                chain_stage(r - 1, small, big)
            for dq in (small, big):
                while dq:
                    c, fn = dq.popleft()
                    fn()
        R.wait_all_dma("sync")
        R.play()

    with ExitStack() as ph:
        R = Rec(nc, dpool, AUTO)
        yf2 = [ph.enter_context(nc.sbuf_tensor(f"yfc{i}", [128, 8 * TB], FP16))
               for i in range(2)]
        g1f2 = [ph.enter_context(nc.sbuf_tensor(f"g1fc{i}", [128, 8 * TB], FP16))
                for i in range(2)]
        sgfc = ph.enter_context(nc.sbuf_tensor("sgfc", [128, 8 * TB], F32))
        ob2 = [ph.enter_context(nc.sbuf_tensor(f"oblk{i}", [128, 4 * D], F32))
               for i in range(2)]
        xg2 = [ph.enter_context(nc.sbuf_tensor(f"xgblk{i}", [128, 4 * D], F32))
               for i in range(2)]
        ps_g2 = [ph.enter_context(nc.psum_tensor(f"gt_{i}", [128, 512], F32))
                 for i in range(2)]

        ytoks = {}

        def yg_load(b_, par_):
            t0_ = b_ * TB
            ysrc = bass.AP(y_hbm.tensor, y_hbm.offset + t0_,
                           [[S, 128], [128 * S, 8], [1, TB]])
            ydst = bass.AP(yf2[par_][:].tensor, yf2[par_][:].offset,
                           [yf2[par_][:].ap[0], [TB, 8], [1, TB]])
            ty = R.dma(ydst, ysrc)
            gsrc = bass.AP(g1_hbm.tensor, g1_hbm.offset + t0_,
                           [[S, 128], [128 * S, 8], [1, TB]])
            gdst = bass.AP(g1f2[par_][:].tensor, g1f2[par_][:].offset,
                           [g1f2[par_][:].ap[0], [TB, 8], [1, TB]])
            tg = R.dma(gdst, gsrc)
            ytoks[par_] = (ty, tg)

        xgtoks = {}

        def xg_load(b_, par_):
            src = bass.AP(x1_hbm.tensor, x1_hbm.offset + b_ * TB * D,
                          [[D, 128], [128 * D, 4], [1, D]])
            xg = xg2[par_]
            xdst = bass.AP(xg[:].tensor, xg[:].offset,
                           [xg[:].ap[0], [D, 4], [1, D]])
            xgtoks[par_] = R.dma(xdst, src)

        st_o = [None, None]
        xg_load(0, 0)
        yg_load(0, 0)
        trn_g = {0: 0, 1: 0}
        for b in range(NB):
            t0 = b * TB
            bpar = b % 2
            if b + 1 < NB:
                R.wait("sync", SV)
                R.wait_tok("sync", st_o[1 - bpar])
                xg_load(b + 1, 1 - bpar)
                yg_load(b + 1, 1 - bpar)
            R.wait_tok("vector", xgtoks[bpar])
            R.wait_tok("vector", st_o[bpar])
            ty, tg = ytoks[bpar]
            R.wait_tok("vector", ty)
            R.wait_tok("vector", tg)
            R.op("vector", "tensor_mul", sgfc[:], yf2[bpar][:], g1f2[bpar][:],
                 inc=(SV, 1))
            sg_sv = SV.n
            for dp in range(8):
                pst = ps_g2[dp % 2]
                R.wait("tensor", SV, max(trn_g[dp % 2], sg_sv))
                for tc in range(4):
                    R.op("tensor", "transpose", pst[:, bass.ts(tc, 128)],
                         sgfc[:, dp * TB + tc * 128: dp * TB + tc * 128 + 128],
                         identf[:], inc=(TN, 1))
                R.wait("vector", TN)
                ob = ob2[bpar]
                o_d = bass.AP(ob[:].tensor, ob[:].offset + dp * 128,
                              [ob[:].ap[0], [D, 4], [1, 128]])
                x_i = bass.AP(xg2[bpar][:].tensor, xg2[bpar][:].offset + dp * 128,
                              [xg2[bpar][:].ap[0], [D, 4], [1, 128]])
                p_i = bass.AP(pst[:].tensor, pst[:].offset,
                              [pst[:].ap[0], [128, 4], [1, 128]])
                R.op("vector", "tensor_add", o_d, p_i, x_i, inc=(SV, 1))
                trn_g[dp % 2] = SV.n
            R.wait("sync", SV)
            dsto = bass.AP(out.tensor, out.offset + t0 * D,
                           [[D, 128], [128 * D, 4], [1, D]])
            osrc = bass.AP(ob2[bpar][:].tensor, ob2[bpar][:].offset,
                           [ob2[bpar][:].ap[0], [D, 4], [1, D]])
            st_o[bpar] = R.dma(dsto, osrc)
        R.wait_all_dma("sync")
        R.play()
    l1.close()
    return nc


def kernel(**inp):
    nc = build()
    xs = np.ascontiguousarray(np.asarray(inp["L0_attn_x"], dtype=np.float32))

    def col8(v):
        return np.ascontiguousarray(np.asarray(v, dtype=np.float32).reshape(8, 128).T)

    base = {
        "qkv_w": np.ascontiguousarray(np.asarray(inp["L0_attn_qkv_w"], dtype=np.float32)),
        "qkv_b": np.ascontiguousarray(np.asarray(inp["L0_attn_qkv_b"], dtype=np.float32).reshape(1, -1)),
        "proj_w": np.ascontiguousarray(np.asarray(inp["L0_attn_proj_w"], dtype=np.float32)),
        "proj_b": col8(inp["L0_attn_proj_b"]),
        "f0_w": np.ascontiguousarray(np.asarray(inp["L0_ffn_weight"], dtype=np.float32)),
        "f0_b": col8(inp["L0_ffn_bias"]),
        "w_dt": np.ascontiguousarray(np.asarray(inp["L1_attn_W_dt"], dtype=np.float32)),
        "b_dt": col8(inp["L1_attn_b_dt"]),
        "w_B": np.ascontiguousarray(np.asarray(inp["L1_attn_W_B"], dtype=np.float32)),
        "w_C": np.ascontiguousarray(np.asarray(inp["L1_attn_W_C"], dtype=np.float32)),
        "a_log": np.ascontiguousarray(
            np.asarray(inp["L1_attn_A_log"], dtype=np.float32)
            .reshape(8, 128, 8).transpose(1, 0, 2).reshape(128, 64)),
        "d_par": col8(inp["L1_attn_D_param"]),
        "f1_w": np.ascontiguousarray(np.asarray(inp["L1_ffn_weight"], dtype=np.float32)),
    }
    in_maps = [dict(base, x=np.ascontiguousarray(xs[b])) for b in range(8)]
    res = run_bass_kernel_spmd(nc, in_maps, core_ids=list(range(8)))
    return np.stack([np.asarray(res.results[b]["out"], dtype=np.float32)
                     for b in range(8)], axis=0)


# version 27
# speedup vs baseline: 1.0476x; 1.0052x over previous
import os
import numpy as np
import ml_dtypes
import concourse.bass as bass
import concourse.mybir as mybir
from concourse.bass_utils import run_bass_kernel_spmd
from contextlib import ExitStack

F32 = mybir.dt.float32
BF16 = mybir.dt.bfloat16
FP16 = mybir.dt.float16
FP8 = mybir.dt.float8e4
AF = mybir.ActivationFunctionType
ALU = mybir.AluOpType
AX = mybir.AxisListType

S = 4096
D = 1024
H = 16
DH = 64
TB = 512
NB = S // TB
NST = 8
MAGIC = 12582912.0

ENGS = ("sync", "vector", "scalar", "tensor", "gpsimd")
SKIP = set(os.environ.get("K1_SKIP", "").split(","))

SCAN_TILES = [(0, 1, 1)] + [(1 + 320 * j, 320, 32) for j in range(12)] + [(3841, 255, 25)]


class Sem:
    def __init__(self, h):
        self.h = h
        self.n = 0


class Rec:

    def __init__(self, nc, dpool, auto):
        self.nc = nc
        self.q = {k: [] for k in ENGS}
        self.dpool = dpool
        self.didx = 0
        self.auto = auto

    def dma(self, dst, src, **kw):
        sem = self.dpool[self.didx % len(self.dpool)]
        self.didx += 1
        if sem.n > 0:
            self.wait("sync", sem)
        self.op("sync", "dma_start", inc=(sem, 16), out=dst, in_=src, **kw)
        return (sem, sem.n)

    def wait_tok(self, eng, tok):
        if tok is not None:
            self.q[eng].append(lambda e, h=tok[0].h, v=tok[1]: e.wait_ge(h, v))

    def wait_all_dma(self, eng):
        for sem in self.dpool:
            if sem.n:
                self.wait(eng, sem)

    def op(self, eng, method, *args, inc=None, **kw):
        if eng in self.auto:
            asem = self.auto[eng]
            if asem.n > 0:
                self.wait(eng, asem)
            if inc is None:
                inc = (asem, 1)
        tgt = None
        if inc is not None:
            sem, k = inc
            sem.n += k
            tgt = (sem.h, k)

        def thunk(e, method=method, args=args, kw=kw, tgt=tgt):
            o = getattr(e, method)(*args, **kw)
            if tgt is not None:
                o.then_inc(*tgt)
        self.q[eng].append(thunk)

    def wait(self, eng, sem, val=None):
        v = sem.n if val is None else val
        self.q[eng].append(lambda e, h=sem.h, v=v: e.wait_ge(h, v))

    def play(self):
        with self.nc.Block() as blk:
            for name in ENGS:
                lst = self.q[name]
                if lst:
                    getattr(blk, name)(lambda e, lst=lst: [f(e) for f in lst])
        self.nc.all_engine_barrier()
        self.q = {k: [] for k in ENGS}


def _np_rope_tables():
    inv = 1.0 / (10000.0 ** (np.arange(0, DH, 2, dtype=np.float32) / DH))
    t = np.arange(S, dtype=np.float32)
    fr = np.einsum("t,d->td", t, inv).astype(np.float32)
    fr = np.concatenate([fr, fr], axis=-1)
    cos = np.cos(fr).astype(np.float32).T
    sin = np.sin(fr).astype(np.float32).T
    cosT = np.concatenate([cos, cos], axis=0)
    sinT = np.concatenate([sin, sin], axis=0)
    return (np.asarray(cosT, dtype=ml_dtypes.bfloat16),
            np.asarray(sinT, dtype=ml_dtypes.bfloat16))


def build(stop_after=None):
    nc = bass.Bass("TRN2", target_bir_lowering=False)

    x_in = nc.dram_tensor("x", [S, D], F32, kind="ExternalInput").ap()
    qkv_w = nc.dram_tensor("qkv_w", [3 * D, D], F32, kind="ExternalInput").ap()
    qkv_b = nc.dram_tensor("qkv_b", [1, 3 * D], F32, kind="ExternalInput").ap()
    proj_w = nc.dram_tensor("proj_w", [D, D], F32, kind="ExternalInput").ap()
    proj_b = nc.dram_tensor("proj_b", [128, 8], F32, kind="ExternalInput").ap()
    f0_w = nc.dram_tensor("f0_w", [D, D], F32, kind="ExternalInput").ap()
    f0_b = nc.dram_tensor("f0_b", [128, 8], F32, kind="ExternalInput").ap()
    w_dt = nc.dram_tensor("w_dt", [D, D], F32, kind="ExternalInput").ap()
    b_dt = nc.dram_tensor("b_dt", [128, 8], F32, kind="ExternalInput").ap()
    w_B = nc.dram_tensor("w_B", [NST, D], F32, kind="ExternalInput").ap()
    w_C = nc.dram_tensor("w_C", [NST, D], F32, kind="ExternalInput").ap()
    a_log = nc.dram_tensor("a_log", [128, 64], F32, kind="ExternalInput").ap()
    d_par = nc.dram_tensor("d_par", [128, 8], F32, kind="ExternalInput").ap()
    f1_w = nc.dram_tensor("f1_w", [D, D], F32, kind="ExternalInput").ap()
    out = nc.dram_tensor("out", [S, D], F32, kind="ExternalOutput").ap()

    ident_d = nc.inline_tensor(np.eye(128, dtype=np.float32), name="ident_f").ap()
    cosT_np, sinT_np = _np_rope_tables()
    cos_d = nc.inline_tensor(cosT_np, name="cosT").ap()
    sin_d = nc.inline_tensor(sinT_np, name="sinT").ap()
    Rm = np.zeros((64, 64), dtype=np.float32)
    Rm[0:32, 32:64] = -np.eye(32)
    Rm[32:64, 0:32] = np.eye(32)
    RT2 = np.zeros((128, 128), dtype=np.float32)
    RT2[0:64, 0:64] = Rm.T
    RT2[64:128, 64:128] = Rm.T
    rot_d = nc.inline_tensor(np.asarray(RT2, dtype=ml_dtypes.bfloat16), name="rotT").ap()
    tri_np = (np.arange(128)[:, None] <= np.arange(128)[None, :])
    tri_d = nc.inline_tensor(np.asarray(tri_np, dtype=ml_dtypes.bfloat16), name="tri").ap()

    def scratch(name, shape, dt):
        return nc.dram_tensor(name, shape, dt, kind="Internal").ap()

    wqkvT_h = scratch("wqkvT_h", [D, 3 * D], BF16)
    wprojT_h = scratch("wprojT_h", [D, D], BF16)
    wdtT_h = scratch("wdtT_h", [D, D], BF16)
    wf0T_h = scratch("wf0T_h", [D, D], FP8)
    wq1T_h = scratch("wq1T_h", [D, D], BF16)
    k_hbm = scratch("k_hbm", [4, 128, S], BF16)
    v_hbm = scratch("v_hbm", [128, 8 * 32 * 65], BF16)
    g_hbm = scratch("g_hbm", [D, S], FP16)
    ctx_hbm = scratch("ctx_hbm", [512, S], BF16)
    xnT0_hbm = scratch("xnT0_hbm", [D, S], BF16)
    x1_hbm = scratch("x1_hbm", [S, D], F32)
    xn1T_hbm = scratch("xn1T_hbm", [D, S], BF16)
    dt_hbm = scratch("dt_hbm", [D, S], F32)
    dtx_hbm = scratch("dtx_hbm", [D, S], FP16)
    g1_hbm = scratch("g1_hbm", [D, S], FP16)
    y_hbm = scratch("y_hbm", [D, S], FP16)

    top = ExitStack()
    dpool = [Sem(top.enter_context(nc.semaphore(f"DP{i}"))) for i in range(12)]
    SV = Sem(top.enter_context(nc.semaphore("SV")))
    SS = Sem(top.enter_context(nc.semaphore("SS")))
    TN = Sem(top.enter_context(nc.semaphore("TN")))
    SG = Sem(top.enter_context(nc.semaphore("SG")))
    AUTO = {"vector": SV, "scalar": SS, "gpsimd": SG}

    sbt = lambda n, s, d=F32: top.enter_context(nc.sbuf_tensor(n, s, d))
    identf = sbt("identf", [128, 128])
    rotT = sbt("rotTs", [128, 128], BF16)
    tri = sbt("tris", [128, 128], BF16)
    ones_b = sbt("ones_b", [1, TB], BF16)
    ones_f = sbt("ones_f", [1, 128])
    ones_c = sbt("ones_c", [128, 1])
    c1 = sbt("c1", [128, 1])
    cS1 = sbt("cS1", [128, 1])
    qkvb_row = sbt("qkvb_row", [1, 3 * D], BF16)
    projb_c = sbt("projb_c", [128, 8])
    f0b_c = sbt("f0b_c", [128, 8])
    bdt_c = sbt("bdt_c", [128, 8])
    dpar_c = sbt("dpar_c", [128, 8])
    a16 = sbt("a16", [128, 64])
    wBT = sbt("wBT", [128, 64])
    wCT = sbt("wCT", [128, 64])
    wsinv_c = sbt("wsinv_c", [128, 1])
    wsd127 = sbt("wsd127", [1, 1])

    with ExitStack() as ph:
        R = Rec(nc, dpool, AUTO)
        ps_t = ph.enter_context(nc.psum_tensor("ps_t", [128, 512], F32))
        wld2 = [ph.enter_context(nc.sbuf_tensor(f"wld{i}", [128, D], F32))
                for i in range(2)]
        wtmp = ph.enter_context(nc.sbuf_tensor("wtmp", [128, D], BF16))
        wtmp8 = ph.enter_context(nc.sbuf_tensor("wtmp8", [128, D], FP8))
        rowtmp = ph.enter_context(nc.sbuf_tensor("rowtmp", [1, 3 * D], F32))
        colstage = ph.enter_context(nc.sbuf_tensor("colstage", [128, 64], F32))
        wqtmp = ph.enter_context(nc.sbuf_tensor("wqtmp", [128, D], F32))
        redtmp = ph.enter_context(nc.sbuf_tensor("redtmp", [128, 16], F32))
        sc1 = ph.enter_context(nc.sbuf_tensor("sc1", [1, 16], F32))

        for dst, src in ((identf[:], ident_d), (rotT[:], rot_d), (tri[:], tri_d),
                         (rowtmp[:], qkv_b), (projb_c[:], proj_b),
                         (f0b_c[:], f0_b), (bdt_c[:], b_dt), (dpar_c[:], d_par),
                         (colstage[:], a_log)):
            R.dma(dst, src)
        R.wait_all_dma("vector")
        R.wait_all_dma("scalar")
        for t_, v_ in ((ones_b, 1.0), (ones_f, 1.0), (ones_c, 1.0), (c1, 1.0), (cS1, -0.0505 / 0.099)):
            R.op("vector", "memset", t_[:], v_)
        R.op("vector", "tensor_copy", qkvb_row[:], rowtmp[:], inc=(SV, 1))
        R.op("scalar", "activation", colstage[:], colstage[:], AF.Tanh, scale=0.1,
             inc=(SS, 1))
        R.op("scalar", "activation", colstage[:], colstage[:], AF.Exp, scale=10.0,
             inc=(SS, 1))
        R.wait("vector", SS)
        R.op("vector", "tensor_scalar", a16[:], colstage[:], -1.0 / 16.0, None,
             ALU.mult, inc=(SV, 1))

        wtoks = {}
        wtrn = {0: 0, 1: 0}
        for wi, (src, dst) in enumerate(((w_B, wBT), (w_C, wCT))):
            R.wait("sync", TN)
            wld = wld2[wi % 2]
            tok = R.dma(wld[0:8, :], src)
            for dc in range(8):
                R.wait_tok("tensor", tok)
                R.wait("tensor", SV)
                R.op("tensor", "transpose", ps_t[:, 0:8], wld[0:8, bass.ts(dc, 128)],
                     identf[0:8, 0:8], inc=(TN, 1))
                R.wait("vector", TN)
                wtrn[wi % 2] = TN.n
                R.op("vector", "tensor_copy", dst[:, bass.ts(dc, 8)], ps_t[:, 0:8],
                     inc=(SV, 1))

        def stage_weight(src_hbm, n_oc, dst_hbm, dst_cols, cast_tile, oc_off=0,
                         st_tok=[None]):
            def issue(oc):
                src_ap = bass.AP(src_hbm.tensor,
                                 src_hbm.offset + (oc_off + oc * 128) * D,
                                 [[D, 128], [1, D]])
                if wtrn[oc % 2]:
                    R.wait("sync", TN, wtrn[oc % 2])
                wtoks[oc % 2] = R.dma(wld2[oc % 2][:], src_ap)
            issue(0)
            for oc in range(n_oc):
                par = oc % 2
                if oc + 1 < n_oc:
                    issue(oc + 1)
                wld = wld2[par]
                R.wait_tok("tensor", wtoks[par])
                for half in range(2):
                    R.wait("tensor", SV)
                    for q4 in range(4):
                        R.op("tensor", "transpose", ps_t[:, bass.ts(q4, 128)],
                             wld[:, bass.ts(half * 4 + q4, 128)], identf[:],
                             inc=(TN, 1))
                    R.wait("vector", TN)
                    if half == 0:
                        R.wait_tok("vector", st_tok[0])
                    R.op("vector", "tensor_copy",
                         cast_tile[:, half * 512: half * 512 + 512],
                         ps_t[:, 0:512], inc=(SV, 1))
                wtrn[par] = TN.n
                R.wait("sync", SV)
                dst_ap = bass.AP(dst_hbm.tensor, dst_hbm.offset + oc * 128,
                                 [[dst_cols, 128], [dst_cols * 128, 8], [1, 128]])
                src_s = bass.AP(cast_tile[:].tensor, cast_tile[:].offset,
                                [[cast_tile[:].ap[0][0], 128], [128, 8], [1, 128]])
                st_tok[0] = R.dma(dst_ap, src_s)

        stage_weight(qkv_w, 24, wqkvT_h, 3 * D, wtmp)
        stage_weight(proj_w, 8, wprojT_h, D, wtmp)
        stage_weight(w_dt, 8, wdtT_h, D, wtmp)
        stage_weight(f0_w, 8, wf0T_h, D, wtmp8)

        wsv = {0: 0, 1: 0}

        def issue_f1(oc):
            src_ap = bass.AP(f1_w.tensor, f1_w.offset + oc * 128 * D,
                             [[D, 128], [1, D]])
            if wtrn[oc % 2]:
                R.wait("sync", TN, wtrn[oc % 2])
            if wsv[oc % 2]:
                R.wait("sync", SV, wsv[oc % 2])
            wtoks[oc % 2] = R.dma(wld2[oc % 2][:], src_ap)
        issue_f1(0)
        for oc in range(8):
            par = oc % 2
            if oc + 1 < 8:
                issue_f1(oc + 1)
            R.wait_tok("vector", wtoks[par])
            R.op("vector", "reduce_sum", redtmp[:, oc:oc + 1], wld2[par][:],
                 AX.X, apply_absolute_value=True, inc=(SV, 1))
            wsv[par] = SV.n
        R.wait("tensor", SV)
        R.op("tensor", "matmul", ps_t[0:1, 0:8], ones_c[:], redtmp[:, 0:8],
             start=True, stop=True, inc=(TN, 1))
        R.wait("vector", TN)
        R.op("vector", "reduce_sum", sc1[:, 0:1], ps_t[0:1, 0:8], AX.X, inc=(SV, 1))
        R.op("vector", "tensor_scalar", sc1[:, 1:2], sc1[:, 0:1],
             1.0 / (D * D), 1e-8, ALU.mult, ALU.add, inc=(SV, 1))
        R.op("vector", "reciprocal", sc1[:, 2:3], sc1[:, 1:2], inc=(SV, 1))
        R.op("vector", "tensor_scalar", wsd127[:], sc1[:, 1:2], 1.0 / 127.0, None,
             ALU.mult, inc=(SV, 1))
        R.wait("tensor", SV)
        R.op("tensor", "matmul", ps_t[:, 0:1], ones_f[:], sc1[:, 2:3],
             start=True, stop=True, inc=(TN, 1))
        R.wait("vector", TN)
        R.op("vector", "tensor_copy", wsinv_c[:], ps_t[:, 0:1], inc=(SV, 1))
        wq_st = None
        issue_f1(0)
        for oc in range(8):
            par = oc % 2
            if oc + 1 < 8:
                issue_f1(oc + 1)
            wld = wld2[par]
            R.wait_tok("vector", wtoks[par])
            R.op("vector", "tensor_scalar", wqtmp[:], wld[:], wsinv_c[:, 0:1], None,
                 ALU.mult, inc=(SV, 1))
            R.op("vector", "tensor_scalar", wld[:], wqtmp[:], 0.5, None, ALU.is_ge,
                 inc=(SV, 1))
            R.op("vector", "tensor_scalar", wqtmp[:], wqtmp[:], -0.5, None,
                 ALU.is_le, inc=(SV, 1))
            R.op("vector", "tensor_sub", wld[:], wld[:], wqtmp[:], inc=(SV, 1))
            for half in range(2):
                R.wait("tensor", SV)
                for q4 in range(4):
                    R.op("tensor", "transpose", ps_t[:, bass.ts(q4, 128)],
                         wld[:, bass.ts(half * 4 + q4, 128)], identf[:],
                         inc=(TN, 1))
                R.wait("vector", TN)
                if half == 0:
                    R.wait_tok("vector", wq_st)
                R.op("vector", "tensor_copy",
                     wtmp[:, half * 512: half * 512 + 512],
                     ps_t[:, 0:512], inc=(SV, 1))
            wtrn[par] = TN.n
            wsv[par] = SV.n
            R.wait("sync", SV)
            dst_ap = bass.AP(wq1T_h.tensor, wq1T_h.offset + oc * 128,
                             [[D, 128], [D * 128, 8], [1, 128]])
            src_s = bass.AP(wtmp[:].tensor, wtmp[:].offset,
                            [[wtmp[:].ap[0][0], 128], [128, 8], [1, 128]])
            wq_st = R.dma(dst_ap, src_s)
        R.wait_all_dma("sync")
        R.play()
    if stop_after == "p0":
        return nc

    l0 = ExitStack()
    s0 = lambda n, s, d=F32: l0.enter_context(nc.sbuf_tensor(n, s, d))
    cos_blk2 = [s0(f"cos_blk{i}", [128, TB], BF16) for i in range(2)]
    sin_blk2 = [s0(f"sin_blk{i}", [128, TB], BF16) for i in range(2)]
    kv_res = s0("kv_res", [128, 4 * S], BF16)
    v_res = s0("v_res", [128, 8 * 32 * 65], BF16)
    xblk2 = [s0(f"xblk{i}", [128, 4 * D]) for i in range(2)]
    xnT2 = [s0(f"xnT2_{i}", [128, 8 * TB], BF16) for i in range(2)]

    def emit_ln4(R, src_tile, dst_tile, sq, s2, st):
        m4, var4, rstd4 = st["m"], st["var"], st["rstd"]
        y24, u4, w4 = st["y2"], st["u"], st["w"]
        for tc in range(4):
            R.op("vector", "reduce_sum", m4[:, tc:tc + 1],
                 src_tile[:, bass.ts(tc, D)], AX.X)
            R.wait("scalar", SV)
            R.op("scalar", "activation", sq[:], src_tile[:, bass.ts(tc, D)],
                 AF.Square, accum_out=s2[:, tc:tc + 1], inc=(SS, 1))
        R.wait("vector", SS)
        R.op("vector", "tensor_scalar", m4[:], m4[:], 1.0 / D, None, ALU.mult)
        R.op("vector", "tensor_scalar", var4[:], s2[:], 1.0 / D, None, ALU.mult)
        R.op("vector", "tensor_mul", y24[:], m4[:], m4[:])
        R.op("vector", "tensor_sub", var4[:], var4[:], y24[:])
        R.op("vector", "tensor_scalar", var4[:], var4[:], 1.0, 1e-5, ALU.mult,
             ALU.add)
        R.op("vector", "reciprocal", rstd4[:], var4[:])
        for _ in range(4):
            R.op("vector", "tensor_mul", y24[:], rstd4[:], rstd4[:])
            R.op("vector", "tensor_mul", u4[:], var4[:], y24[:])
            R.op("vector", "tensor_scalar", w4[:], u4[:], -0.5, 1.5, ALU.mult,
                 ALU.add)
            R.op("vector", "tensor_mul", rstd4[:], rstd4[:], w4[:])
        for tc in range(4):
            R.op("vector", "tensor_scalar", dst_tile[:, bass.ts(tc, D)],
                 src_tile[:, bass.ts(tc, D)], m4[:, tc:tc + 1],
                 rstd4[:, tc:tc + 1], ALU.subtract, ALU.mult, inc=(SV, 1))

    xtoks = {}
    xnt_toks = {}

    def emit_xload(R, blkidx, par, want_x=True, want_xnt=False):
        t0_ = blkidx * TB
        toks = []
        if want_x:
            src = bass.AP(x_in.tensor, x_in.offset + blkidx * TB * D,
                          [[D, 128], [128 * D, 4], [1, D]])
            xb = xblk2[par]
            xdst = bass.AP(xb[:].tensor, xb[:].offset,
                           [xb[:].ap[0], [D, 4], [1, D]])
            toks.append(R.dma(xdst, src))
        cs = bass.AP(cos_d.tensor, cos_d.offset + t0_, [[S, 128], [1, TB]])
        sn = bass.AP(sin_d.tensor, sin_d.offset + t0_, [[S, 128], [1, TB]])
        toks.append(R.dma(cos_blk2[par][:], cs))
        toks.append(R.dma(sin_blk2[par][:], sn))
        xtoks[par] = tuple(toks)
        if want_xnt:
            src = bass.AP(xnT0_hbm.tensor, xnT0_hbm.offset + t0_,
                          [[S, 128], [S * 128, 8], [1, TB]])
            dst = bass.AP(xnT2[par][:].tensor, xnT2[par][:].offset,
                          [xnT2[par][:].ap[0], [TB, 8], [1, TB]])
            xnt_toks[par] = R.dma(dst, src)

    with ExitStack() as ph:
        R = Rec(nc, dpool, AUTO)
        wkvT = ph.enter_context(nc.sbuf_tensor("wkvT", [128, 8 * 2048], BF16))
        wf0T = ph.enter_context(nc.sbuf_tensor("wf0T", [128, 8 * D], FP8))
        xnT_f8 = ph.enter_context(nc.sbuf_tensor("xnT_f8", [128, 8 * TB], FP8))
        k_raw = ph.enter_context(nc.sbuf_tensor("k_raw", [128, TB], BF16))
        k_t1 = ph.enter_context(nc.sbuf_tensor("k_t1", [128, TB], BF16))
        k_t2 = ph.enter_context(nc.sbuf_tensor("k_t2", [128, TB], BF16))
        kstage = ph.enter_context(nc.sbuf_tensor("kstage", [128, TB], BF16))
        vstage = ph.enter_context(nc.sbuf_tensor("vstage", [128, TB], BF16))
        gstage = ph.enter_context(nc.sbuf_tensor("gstage", [128, TB], FP16))
        xnblk = ph.enter_context(nc.sbuf_tensor("xnblk", [128, 4 * D], F32))
        sqdump = ph.enter_context(nc.sbuf_tensor("sqdump", [128, D], F32))
        s2t = ph.enter_context(nc.sbuf_tensor("s2t", [128, 4], F32))
        stat = {k: ph.enter_context(nc.sbuf_tensor("st_" + k, [128, 4], F32))
                for k in ("m", "var", "rstd", "y2", "u", "w")}
        ps_a = ph.enter_context(nc.psum_tensor("p1_a", [128, 512], F32))
        ps_k = [ph.enter_context(nc.psum_tensor(f"p1_k{i}", [128, 512], F32))
                for i in range(2)]
        ps_r = ph.enter_context(nc.psum_tensor("p1_r", [128, 512], F32))
        ps_v = [ph.enter_context(nc.psum_tensor(f"p1_v{i}", [128, 512], F32))
                for i in range(2)]
        ps_g = [ph.enter_context(nc.psum_tensor(f"p1_g{i}", [128, 512], F32))
                for i in range(2)]

        if "ones" not in SKIP:
            ones_ap = bass.AP(v_res[:].tensor, v_res[:].offset + 64,
                              [v_res[:].ap[0], [65, 8 * 32]])
            R.op("vector", "memset", ones_ap, 1.0, inc=(SV, 1))
        wkv_toks = []
        for ic in range(8):
            wsrc = bass.AP(wqkvT_h.tensor, wqkvT_h.offset + ic * 3 * D * 128 + 1024,
                           [[3 * D, 128], [1, 2048]])
            wkv_toks.append(R.dma(wkvT[:, ic * 2048:(ic + 1) * 2048], wsrc))
        wf0_toks = []
        for ic in range(8):
            w8src = bass.AP(wf0T_h.tensor, wf0T_h.offset + ic * D * 128,
                            [[D, 128], [1, D]])
            wf0_toks.append(R.dma(wf0T[:, ic * D:(ic + 1) * D], w8src))
        st_k = None
        st_v = None
        st_g = None
        st_xnt = [None, None]
        emit_xload(R, 0, 0)
        for b in range(NB):
            t0 = b * TB
            par = b % 2
            cos_blk, sin_blk = cos_blk2[par], sin_blk2[par]
            xnT_bf = xnT2[par]
            xblk = xblk2[par]
            for tk in xtoks[par]:
                R.wait_tok("vector", tk)
            R.wait_tok("scalar", xtoks[par][0])
            emit_ln4(R, xblk[:], xnblk[:], sqdump, s2t, stat)
            R.wait_tok("vector", st_xnt[par])
            p1tr = {0: 0, 1: 0}
            R.wait("tensor", SV)
            R.wait("tensor", SS)
            for ic in range(8):
                pst = (ps_a, ps_r)[ic % 2]
                if p1tr[ic % 2]:
                    R.wait("tensor", SV, p1tr[ic % 2])
                for tc in range(4):
                    R.op("tensor", "transpose", pst[:, bass.ts(tc, 128)],
                         xnblk[:, tc * D + ic * 128: tc * D + ic * 128 + 128],
                         identf[:], inc=(TN, 1))
                R.wait("vector", TN)
                R.op("vector", "tensor_copy", xnT_bf[:, bass.ts(ic, TB)],
                     pst[:, 0:512], inc=(SV, 1))
                R.op("vector", "tensor_copy", xnT_f8[:, bass.ts(ic, TB)],
                     pst[:, 0:512], inc=(SV, 1))
                p1tr[ic % 2] = SV.n
            R.wait("sync", SV)
            xts = bass.AP(xnT_bf[:].tensor, xnT_bf[:].offset,
                          [xnT_bf[:].ap[0], [TB, 8], [1, TB]])
            xtd = bass.AP(xnT0_hbm.tensor, xnT0_hbm.offset + t0,
                          [[S, 128], [S * 128, 8], [1, TB]])
            st_xnt[par] = R.dma(xtd, xts)
            if b + 1 < NB:
                R.wait("sync", SV)
                emit_xload(R, b + 1, 1 - par)
            def emit_k(kc):
                nonlocal st_k
                pk = ps_k[kc % 2]
                R.wait("tensor", SS)
                R.wait("tensor", SV)
                for ic in range(8):
                    if b == 0:
                        R.wait_tok("tensor", wkv_toks[ic])
                    R.op("tensor", "matmul", pk[:],
                         wkvT[:, ic * 2048 + kc * 128: ic * 2048 + kc * 128 + 128],
                         xnT_bf[:, bass.ts(ic, TB)],
                         start=(ic == 0), stop=False, inc=(TN, 1))
                R.op("tensor", "matmul", pk[:],
                     qkvb_row[0:1, 1024 + kc * 128: 1024 + kc * 128 + 128],
                     ones_b[:], start=False, stop=True, inc=(TN, 1))
                R.wait("scalar", TN)
                R.wait("scalar", SV)
                R.op("scalar", "copy", k_raw[:], pk[:], inc=(SS, 1))
                R.wait("tensor", SS)
                R.wait("tensor", SV)
                R.op("tensor", "matmul", ps_r[:], rotT[:], k_raw[:],
                     start=True, stop=True, inc=(TN, 1))
                R.wait("vector", SS)
                R.op("vector", "tensor_mul", k_t1[:], k_raw[:],
                     cos_blk[:], inc=(SV, 1))
                R.wait("vector", TN)
                R.op("vector", "tensor_mul", k_t2[:], ps_r[:],
                     sin_blk[:], inc=(SV, 1))
                if kc < 4:
                    R.op("vector", "tensor_add",
                         kv_res[:, kc * S + t0: kc * S + t0 + TB],
                         k_t1[:], k_t2[:], inc=(SV, 1))
                else:
                    R.wait_tok("vector", st_k)
                    R.op("vector", "tensor_add", kstage[:], k_t1[:], k_t2[:],
                         inc=(SV, 1))
                    R.wait("sync", SV)
                    dst = bass.AP(k_hbm.tensor,
                                  k_hbm.offset + (kc - 4) * 128 * S + t0,
                                  [[S, 128], [1, TB]])
                    st_k = R.dma(dst, kstage[:])

            def emit_v(vi):
                nonlocal st_v
                tc, dvc = vi // 2, vi % 2
                pv = ps_v[dvc]
                R.wait("tensor", SV)
                for ic in range(8):
                    if b == 0:
                        R.wait_tok("tensor", wkv_toks[ic])
                    R.op("tensor", "matmul", pv[:],
                         xnT_bf[:, ic * TB + tc * 128: ic * TB + tc * 128 + 128],
                         wkvT[:, ic * 2048 + 1024 + dvc * 512:
                              ic * 2048 + 1024 + dvc * 512 + 512],
                         start=(ic == 0), stop=False, inc=(TN, 1))
                R.op("tensor", "matmul", pv[:], ones_b[0:1, 0:128],
                     qkvb_row[0:1, 2048 + dvc * 512: 2048 + dvc * 512 + 512],
                     start=False, stop=True, inc=(TN, 1))
                R.wait("vector", TN)
                kcg = b * 4 + tc
                if dvc == 0:
                    dsts = bass.AP(v_res[:].tensor, v_res[:].offset + kcg * 65,
                                   [v_res[:].ap[0], [32 * 65, 8], [1, 64]])
                    srcs = bass.AP(pv[:].tensor, pv[:].offset,
                                   [pv[:].ap[0], [64, 8], [1, 64]])
                    R.op("vector", "tensor_copy", dsts, srcs, inc=(SV, 1))
                else:
                    R.wait_tok("vector", st_v)
                    R.op("vector", "tensor_copy", vstage[:], pv[:], inc=(SV, 1))
                    R.wait("sync", SV)
                    dst = bass.AP(v_hbm.tensor, v_hbm.offset + kcg * 65,
                                  [[8 * 32 * 65, 128], [32 * 65, 8], [1, 64]])
                    vsrc = bass.AP(vstage[:].tensor, vstage[:].offset,
                                   [vstage[:].ap[0], [64, 8], [1, 64]])
                    st_v = R.dma(dst, vsrc)

            def emit_f(fc):
                nonlocal st_g
                pg = ps_g[fc % 2]
                R.wait("tensor", SS)
                for ic in range(8):
                    if b == 0:
                        R.wait_tok("tensor", wf0_toks[ic])
                    R.op("tensor", "matmul", pg[:],
                         wf0T[:, ic * D + fc * 128: ic * D + fc * 128 + 128],
                         xnT_f8[:, bass.ts(ic, TB)],
                         start=(ic == 0), stop=(ic == 7), inc=(TN, 1))
                R.wait("scalar", TN)
                R.wait_tok("scalar", st_g)
                R.op("scalar", "activation", gstage[:], pg[:], AF.Silu,
                     bias=f0b_c[:, fc:fc + 1], inc=(SS, 1))
                R.wait("sync", SS)
                dst = bass.AP(g_hbm.tensor, g_hbm.offset + fc * 128 * S + t0,
                              [[S, 128], [1, TB]])
                st_g = R.dma(dst, gstage[:])

            for i in range(8):
                if 'k' not in SKIP:
                    emit_k(i)
                if 'v' not in SKIP:
                    emit_v(i)
                if 'ffn0' not in SKIP:
                    emit_f(i)
        R.wait_all_dma("sync")
        R.play()

    def attn_phase(group):
        G = f"g{group}_"
        with ExitStack() as ph:
            R = Rec(nc, dpool, AUTO)
            wqT = ph.enter_context(nc.sbuf_tensor(G + "wqT", [128, 8 * 512], BF16))
            q_res2 = [ph.enter_context(nc.sbuf_tensor(G + f"q_res{i}", [128, 4 * TB], BF16))
                      for i in range(2)]
            q_raw = ph.enter_context(nc.sbuf_tensor(G + "q_raw", [128, TB], BF16))
            q_t1 = ph.enter_context(nc.sbuf_tensor(G + "q_t1", [128, TB], BF16))
            pTAB = ph.enter_context(nc.sbuf_tensor(G + "pTAB", [128, 4096], BF16))
            zrow = ph.enter_context(nc.sbuf_tensor(G + "zrow", [1, 512], F32))
            zrow2 = ph.enter_context(nc.sbuf_tensor(G + "zrow2", [1, 512], F32))
            zbf = ph.enter_context(nc.sbuf_tensor(G + "zbf", [128, 512], F32))
            ctxst = ph.enter_context(nc.sbuf_tensor(G + "ctxst", [64, 512], BF16))
            ctxst2 = ph.enter_context(nc.sbuf_tensor(G + "ctxst2", [64, 512], BF16))
            ps_AB = ph.enter_context(nc.psum_tensor(G + "p2_AB", [128, 2048], F32))
            ps_C = ph.enter_context(nc.psum_tensor(G + "p2_C", [128, 512], F32))
            ps_cA = ph.enter_context(nc.psum_tensor(G + "p2_cA", [65, 512], F32))
            ps_cB = ph.enter_context(nc.psum_tensor(G + "p2_cB", [65, 512], F32))
            ps_q = ph.enter_context(nc.psum_tensor(G + "p2_q", [128, 512], F32))
            if group == 1:
                wpT = ph.enter_context(nc.sbuf_tensor(G + "wpT", [128, 8 * D], BF16))
                ctxa_ld = ph.enter_context(nc.sbuf_tensor(G + "ctxa_ld", [128, 2048], BF16))
                ctxb_res = ph.enter_context(nc.sbuf_tensor(G + "ctxb_res", [128, 2048], BF16))
                g_fc = ph.enter_context(nc.sbuf_tensor(G + "g_fc", [128, 512], FP16))
                sg_fc = ph.enter_context(nc.sbuf_tensor(G + "sg_fc", [128, 512], F32))

            wsrc = bass.AP(wqkvT_h.tensor, wqkvT_h.offset + group * 512,
                           [[3 * D, 128], [3 * D * 128, 8], [1, 512]])
            wdstq = bass.AP(wqT[:].tensor, wqT[:].offset,
                            [wqT[:].ap[0], [512, 8], [1, 512]])
            wq_tok = R.dma(wdstq, wsrc)
            k_toks = [None] * 4
            v_toks = [None] * 4
            wp_tok = None
            if group == 1:
                for hp in range(4):
                    src = bass.AP(k_hbm.tensor, k_hbm.offset + hp * 128 * S,
                                  [[S, 128], [1, S]])
                    k_toks[hp] = R.dma(kv_res[:, hp * S:(hp + 1) * S], src)
                    vsw = bass.AP(v_hbm.tensor,
                                  v_hbm.offset + hp * 2 * 32 * 65,
                                  [[8 * 32 * 65, 128], [1, 2 * 32 * 65]])
                    v_toks[hp] = R.dma(
                        v_res[:, hp * 2 * 32 * 65:(hp + 1) * 2 * 32 * 65], vsw)
                wps = bass.AP(wprojT_h.tensor, wprojT_h.offset,
                              [[D, 128], [D * 128, 8], [1, D]])
                wpdst = bass.AP(wpT[:].tensor, wpT[:].offset,
                                [wpT[:].ap[0], [D, 8], [1, D]])
                wp_tok = R.dma(wpdst, wps)
                ones_ap2 = bass.AP(v_res[:].tensor, v_res[:].offset + 64,
                                   [v_res[:].ap[0], [65, 8 * 32]])
                for hp in range(4):
                    R.wait_tok("vector", v_toks[hp])
                R.op("vector", "memset", ones_ap2, 1.0, inc=(SV, 1))
                ones_sv = SV.n
            R.wait_tok("tensor", wq_tok)
            st_ctx_h = [None, None]
            st_x1 = None
            st_out = None
            slot_ss = [0, 0]
            ctx_sv = [0]
            q_ss = [0]
            q_sv = [0, 0]
            q_psv = [0]
            gate_sv = [0]
            pend_nt = [None]
            znt_tn = [0]

            def norm_tail():
                if pend_nt[0] is None:
                    return
                hA_, t0_, rsv = pend_nt[0]
                pend_nt[0] = None
                R.wait("tensor", SV, rsv)
                R.op("tensor", "matmul", ps_C[0:64, :], ones_f[0:1, 0:64],
                     zrow[:], start=True, stop=True, inc=(TN, 1))
                R.op("tensor", "matmul", ps_C[64:128, :], ones_f[0:1, 0:64],
                     zrow2[:], start=True, stop=True, inc=(TN, 1))
                znt_tn[0] = TN.n
                R.wait("vector", TN, znt_tn[0])
                R.op("vector", "tensor_copy", zbf[:], ps_C[:], inc=(SV, 1))
                if group == 0:
                    R.wait_tok("vector", st_ctx_h[0])
                    R.op("vector", "tensor_mul", ctxst[:], ps_cA[0:64, :],
                         zbf[0:64, :], inc=(SV, 1))
                    R.wait_tok("vector", st_ctx_h[1])
                    R.op("vector", "tensor_mul", ctxst2[:], ps_cB[0:64, :],
                         zbf[64:128, :], inc=(SV, 1))
                    R.wait("sync", SV)
                    dstA = bass.AP(ctx_hbm.tensor,
                                   ctx_hbm.offset + (hA_ * 64) * S + t0_,
                                   [[S, 64], [1, TB]])
                    st_ctx_h[0] = R.dma(dstA, ctxst[:])
                    dstB = bass.AP(ctx_hbm.tensor,
                                   ctx_hbm.offset + ((hA_ + 1) * 64) * S + t0_,
                                   [[S, 64], [1, TB]])
                    st_ctx_h[1] = R.dma(dstB, ctxst2[:])
                else:
                    R.op("vector", "tensor_mul",
                         ctxb_res[0:64, bass.ts(hA_ // 2, TB)], ps_cA[0:64, :],
                         zbf[0:64, :], inc=(SV, 1))
                    R.op("vector", "tensor_mul",
                         ctxb_res[64:128, bass.ts(hA_ // 2, TB)], ps_cB[0:64, :],
                         zbf[64:128, :], inc=(SV, 1))
                ctx_sv[0] = SV.n

            def emit_qblock(b_, par_):
                q_res = q_res2[par_]
                cosb, sinb = cos_blk2[par_], sin_blk2[par_]
                xnt = xnT2[par_]
                R.wait_tok("tensor", xnt_toks[par_])
                for tk in xtoks[par_]:
                    R.wait_tok("vector", tk)
                for qc in range(4):
                    if q_ss[0]:
                        R.wait("tensor", SS, q_ss[0])
                    if q_psv[0]:
                        R.wait("tensor", SV, q_psv[0])
                    for ic in range(8):
                        R.op("tensor", "matmul", ps_q[:],
                             wqT[:, ic * 512 + qc * 128: ic * 512 + qc * 128 + 128],
                             xnt[:, bass.ts(ic, TB)],
                             start=(ic == 0), stop=False, inc=(TN, 1))
                    R.op("tensor", "matmul", ps_q[:],
                         qkvb_row[0:1, group * 512 + qc * 128:
                                  group * 512 + qc * 128 + 128],
                         ones_b[:], start=False, stop=True, inc=(TN, 1))
                    R.wait("scalar", TN)
                    R.wait("scalar", SV)
                    R.op("scalar", "copy", q_raw[:], ps_q[:], inc=(SS, 1))
                    q_ss[0] = SS.n
                    R.wait("tensor", SS, q_ss[0])
                    R.op("tensor", "matmul", ps_q[:], rotT[:], q_raw[:],
                         start=True, stop=True, inc=(TN, 1))
                    rot_tn = TN.n
                    R.wait("vector", SS, q_ss[0])
                    R.op("vector", "tensor_mul", q_t1[:], q_raw[:],
                         cosb[:], inc=(SV, 1))
                    R.wait("vector", TN, rot_tn)
                    R.op("vector", "tensor_mul", q_res[:, bass.ts(qc, TB)],
                         ps_q[:], sinb[:], inc=(SV, 1))
                    q_psv[0] = SV.n
                    R.op("vector", "tensor_add", q_res[:, bass.ts(qc, TB)],
                         q_res[:, bass.ts(qc, TB)], q_t1[:], inc=(SV, 1))
                q_sv[par_] = SV.n
            emit_xload(R, 0, 0, want_x=(group == 1), want_xnt=True)
            for b in range(NB):
                t0 = b * TB
                par = b % 2
                cos_blk, sin_blk = cos_blk2[par], sin_blk2[par]
                xblk = xblk2[par]
                xnT_bf = xnT2[par]
                for tk in xtoks[par]:
                    R.wait_tok("vector", tk)
                R.wait_tok("tensor", xnt_toks[par])
                if b + 1 < NB:
                    R.wait("sync", SV)
                    R.wait("sync", TN)
                    if group == 1:
                        R.wait_tok("sync", st_x1)
                        R.wait_tok("sync", st_out)
                    emit_xload(R, b + 1, 1 - par, want_x=(group == 1),
                               want_xnt=True)
                if b == 0:
                    emit_qblock(0, 0)
                q_res = q_res2[par]
                R.wait("tensor", SV, q_sv[par])
                if gate_sv[0]:
                    R.wait("tensor", SV, gate_sv[0])
                for hp in range(4):
                    hA = 2 * hp
                    n_kc = b + 1
                    qsl = slice(hp * TB, hp * TB + TB)
                    if group == 1 and b == 0:
                        R.wait_tok("tensor", k_toks[hp])
                        R.wait_tok("tensor", v_toks[hp])
                        R.wait("tensor", SV, ones_sv)
                    exp_ss = [0] * 4
                    mask_sv = [0] * 4

                    def emit_av(kc, kk, diag):
                        R.wait("tensor", SS, exp_ss[kk])
                        if diag:
                            R.wait("tensor", SV, mask_sv[kk])
                        qs = kk * 128 if diag else 0
                        kcg = kc * 4 + kk
                        first = (kc == 0 and kk == 0)
                        last = (diag and kk == 3)
                        if first:
                            R.wait("tensor", SV, ctx_sv[0])
                        vA = bass.AP(v_res[:].tensor,
                                     v_res[:].offset + ((hA) * 32 + kcg) * 65,
                                     [v_res[:].ap[0], [1, 65]])
                        vB = bass.AP(v_res[:].tensor,
                                     v_res[:].offset + ((hA + 1) * 32 + kcg) * 65,
                                     [v_res[:].ap[0], [1, 65]])
                        R.op("tensor", "matmul", ps_cA[:, qs:512], vA,
                             pTAB[:, kk * 1024 + qs: kk * 1024 + 512],
                             start=first, stop=last, inc=(TN, 1),
                             skip_group_check=True)
                        R.op("tensor", "matmul", ps_cB[:, qs:512], vB,
                             pTAB[:, kk * 1024 + 512 + qs: kk * 1024 + 1024],
                             start=first, stop=last, inc=(TN, 1),
                             skip_group_check=True)

                    pend = []
                    for kc in range(n_kc):
                        diag = (kc == b)
                        for kk in range(4):
                            kcol = kc * 512 + kk * 128
                            sl = kk % 2
                            if slot_ss[sl]:
                                R.wait("tensor", SS, slot_ss[sl])
                            R.op("tensor", "matmul",
                                 ps_AB[:, sl * 1024: sl * 1024 + 512],
                                 kv_res[0:64, hp * S + kcol: hp * S + kcol + 128],
                                 q_res[0:64, qsl], start=True, stop=True,
                                 tile_position=(0, 0), inc=(TN, 1))
                            R.op("tensor", "matmul",
                                 ps_AB[:, sl * 1024 + 512: sl * 1024 + 1024],
                                 kv_res[64:128, hp * S + kcol: hp * S + kcol + 128],
                                 q_res[64:128, qsl], start=True, stop=True,
                                 tile_position=(64, 0), inc=(TN, 1))
                            R.wait("scalar", TN)
                            R.op("scalar", "activation",
                                 pTAB[:, kk * 1024: kk * 1024 + 1024],
                                 ps_AB[:, sl * 1024: sl * 1024 + 1024],
                                 AF.Exp, scale=0.125, inc=(SS, 1))
                            exp_ss[kk] = SS.n
                            slot_ss[sl] = SS.n
                            if diag:
                                R.wait("vector", SS, exp_ss[kk])
                                for half in range(2):
                                    col = kk * 1024 + half * 512 + kk * 128
                                    R.op("vector", "tensor_mul",
                                         pTAB[:, col:col + 128],
                                         pTAB[:, col:col + 128], tri[:],
                                         inc=(SV, 1))
                                mask_sv[kk] = SV.n
                            pend.append((kc, kk, diag))
                            if kc == 0 and kk == 1:
                                norm_tail()
                            if len(pend) > 2:
                                emit_av(*pend.pop(0))
                    for args_ in pend:
                        emit_av(*args_)
                    pend = []
                    if hp == 0 and b + 1 < NB:
                        emit_qblock(b + 1, 1 - par)
                    R.wait("vector", TN)
                    if znt_tn[0]:
                        R.wait("vector", TN, znt_tn[0])
                    R.op("vector", "reciprocal", zrow[:], ps_cA[64:65, :],
                         inc=(SV, 1))
                    R.op("vector", "reciprocal", zrow2[:], ps_cB[64:65, :],
                         inc=(SV, 1))
                    pend_nt[0] = (hA, t0, SV.n)
                norm_tail()
                if group == 1:
                    ctx_toks = []
                    for ic in range(4):
                        src = bass.AP(ctx_hbm.tensor,
                                      ctx_hbm.offset + ic * 128 * S + t0,
                                      [[S, 128], [1, TB]])
                        ctx_toks.append(R.dma(ctxa_ld[:, bass.ts(ic, TB)], src))
                    for dp in range(8):
                        if dp == 0:
                            R.wait_tok("tensor", wp_tok)
                            for tk in ctx_toks:
                                R.wait_tok("tensor", tk)
                        R.wait("tensor", SV)
                        R.wait("tensor", SS)
                        for hd in range(8):
                            lhs = wpT[:, hd * D + dp * 128: hd * D + dp * 128 + 128]
                            rhs = (ctxa_ld[:, bass.ts(hd, TB)] if hd < 4
                                   else ctxb_res[:, bass.ts(hd - 4, TB)])
                            R.op("tensor", "matmul", ps_C[:], lhs, rhs,
                                 start=(hd == 0), stop=(hd == 7), inc=(TN, 1))
                        src = bass.AP(g_hbm.tensor, g_hbm.offset + dp * 128 * S + t0,
                                      [[S, 128], [1, TB]])
                        R.wait("sync", SV)
                        tok_g = R.dma(g_fc[:], src)
                        R.wait_tok("vector", tok_g)
                        R.wait("vector", TN)
                        R.op("vector", "scalar_tensor_tensor", sg_fc[:], ps_C[:],
                             projb_c[:, dp:dp + 1], g_fc[:], ALU.add, ALU.mult,
                             inc=(SV, 1))
                        R.wait("tensor", SV)
                        R.wait("tensor", SS)
                        for tc in range(4):
                            R.op("tensor", "transpose",
                                 ps_AB[:, bass.ts(tc, 128)],
                                 sg_fc[:, bass.ts(tc, 128)], identf[:],
                                 inc=(TN, 1))
                        R.wait("vector", TN)
                        o_d = bass.AP(xblk[:].tensor, xblk[:].offset + dp * 128,
                                      [xblk[:].ap[0], [D, 4], [1, 128]])
                        p_i = bass.AP(ps_AB[:].tensor, ps_AB[:].offset,
                                      [ps_AB[:].ap[0], [128, 4], [1, 128]])
                        R.op("vector", "tensor_add", o_d, p_i, o_d, inc=(SV, 1))
                    gate_sv[0] = SV.n
                    R.wait("sync", SV)
                    dst = bass.AP(x1_hbm.tensor, x1_hbm.offset + t0 * D,
                                  [[D, 128], [128 * D, 4], [1, D]])
                    x1src = bass.AP(xblk[:].tensor, xblk[:].offset,
                                    [xblk[:].ap[0], [D, 4], [1, D]])
                    st_x1 = R.dma(dst, x1src)
                    if stop_after == "l0":
                        dst2 = bass.AP(out.tensor, out.offset + t0 * D,
                                       [[D, 128], [128 * D, 4], [1, D]])
                        st_out = R.dma(dst2, x1src)
            R.wait_all_dma("sync")
            R.play()

    if stop_after == "p1":
        l0.close()
        return nc
    attn_phase(0)
    if stop_after == "p2a":
        l0.close()
        return nc
    attn_phase(1)
    l0.close()
    nc.all_engine_barrier()
    if stop_after == "l0":
        return nc

    bp_hbm = scratch("bp_hbm", [NST, S], F32)
    cp_hbm = scratch("cp_hbm", [NST, S], F32)

    l1 = ExitStack()

    with ExitStack() as ph:
        R = Rec(nc, dpool, AUTO)
        xb2 = [ph.enter_context(nc.sbuf_tensor(f"xblk1{i}", [128, 4 * D], F32))
               for i in range(2)]
        xnblk1 = ph.enter_context(nc.sbuf_tensor("xnblk1", [128, 4 * D], F32))
        sqdump1 = ph.enter_context(nc.sbuf_tensor("sqdump1", [128, D], F32))
        s2t1 = ph.enter_context(nc.sbuf_tensor("s2t1", [128, 4], F32))
        stat1 = {k: ph.enter_context(nc.sbuf_tensor("s1_" + k, [128, 4], F32))
                 for k in ("m", "var", "rstd", "y2", "u", "w")}
        wdtT = ph.enter_context(nc.sbuf_tensor("wdtT", [128, 8 * D], BF16))
        wq1T = ph.enter_context(nc.sbuf_tensor("wq1T", [128, 8 * D], BF16))
        xn1T_bf = ph.enter_context(nc.sbuf_tensor("xn1T_bf", [128, 8 * TB], BF16))
        xn1T_f = ph.enter_context(nc.sbuf_tensor("xn1T_f", [128, 8 * TB], F32))
        aqT = ph.enter_context(nc.sbuf_tensor("aqT", [128, 8 * TB], BF16))
        dtblk = ph.enter_context(nc.sbuf_tensor("dtblk", [128, 8 * TB], F32))
        dtxblk = ph.enter_context(nc.sbuf_tensor("dtxblk", [128, 8 * TB], FP16))
        t1t = ph.enter_context(nc.sbuf_tensor("t1t", [128, 8 * TB], F32))
        aqtmp = ph.enter_context(nc.sbuf_tensor("aqtmp", [128, 4 * D], F32))
        rmax4 = ph.enter_context(nc.sbuf_tensor("rmax4", [128, 4], F32))
        arcp = ph.enter_context(nc.sbuf_tensor("arcp", [128, 1], F32))
        sclrow = ph.enter_context(nc.sbuf_tensor("sclrow", [1, TB], F32))
        sclb = ph.enter_context(nc.sbuf_tensor("sclb", [128, TB], F32))
        g1stage = ph.enter_context(nc.sbuf_tensor("g1stage", [128, TB], FP16))
        bpstage = ph.enter_context(nc.sbuf_tensor("bpstage", [8, TB], F32))
        cpstage = ph.enter_context(nc.sbuf_tensor("cpstage", [8, TB], F32))
        ps_tr = ph.enter_context(nc.psum_tensor("l1_tr", [128, 512], F32))
        ps_dt = [ph.enter_context(nc.psum_tensor(f"l1_dt{i}", [128, 512], F32))
                 for i in range(2)]
        ps_bp = ph.enter_context(nc.psum_tensor("l1_bp", [8, 512], F32))
        ps_bn = [ph.enter_context(nc.psum_tensor(f"l1_bn{i}", [128, 512], F32))
                 for i in range(2)]
        ps_msc = ph.enter_context(nc.psum_tensor("l1_msc", [128, 512], F32))

        wdts = bass.AP(wdtT_h.tensor, wdtT_h.offset,
                       [[D, 128], [D * 128, 8], [1, D]])
        wdtd = bass.AP(wdtT[:].tensor, wdtT[:].offset,
                       [wdtT[:].ap[0], [D, 8], [1, D]])
        tok_a = R.dma(wdtd, wdts)
        wqs = bass.AP(wq1T_h.tensor, wq1T_h.offset,
                      [[D, 128], [D * 128, 8], [1, D]])
        wqd = bass.AP(wq1T[:].tensor, wq1T[:].offset,
                      [wq1T[:].ap[0], [D, 8], [1, D]])
        tok_b = R.dma(wqd, wqs)
        R.wait_tok("tensor", tok_a)
        R.wait_tok("tensor", tok_b)
        st_dt = None
        st_dtx = None
        aq_tn = [0]
        bp_sv = [0]
        st_g1 = None
        st_bp = None
        st_cp = None
        st_xt = None

        x1toks = {}

        def xload1(b_, par_):
            src = bass.AP(x1_hbm.tensor, x1_hbm.offset + b_ * TB * D,
                          [[D, 128], [128 * D, 4], [1, D]])
            xb = xb2[par_]
            xdst = bass.AP(xb[:].tensor, xb[:].offset,
                           [xb[:].ap[0], [D, 4], [1, D]])
            x1toks[par_] = R.dma(xdst, src)

        def emit_ln14(R, src_tile, dst_tile):
            m4, var4, rstd4 = stat1["m"], stat1["var"], stat1["rstd"]
            y24, u4, w4 = stat1["y2"], stat1["u"], stat1["w"]
            for tc in range(4):
                R.op("vector", "reduce_sum", m4[:, tc:tc + 1],
                     src_tile[:, bass.ts(tc, D)], AX.X)
                R.wait("scalar", SV)
                R.op("scalar", "activation", sqdump1[:],
                     src_tile[:, bass.ts(tc, D)], AF.Square,
                     accum_out=s2t1[:, tc:tc + 1], inc=(SS, 1))
            R.wait("vector", SS)
            R.op("vector", "tensor_scalar", m4[:], m4[:], 1.0 / D, None, ALU.mult)
            R.op("vector", "tensor_scalar", var4[:], s2t1[:], 1.0 / D, None,
                 ALU.mult)
            R.op("vector", "tensor_mul", y24[:], m4[:], m4[:])
            R.op("vector", "tensor_sub", var4[:], var4[:], y24[:])
            R.op("vector", "tensor_scalar", var4[:], var4[:], 1.0, 1e-5,
                 ALU.mult, ALU.add)
            R.op("vector", "reciprocal", rstd4[:], var4[:])
            for _ in range(4):
                R.op("vector", "tensor_mul", y24[:], rstd4[:], rstd4[:])
                R.op("vector", "tensor_mul", u4[:], var4[:], y24[:])
                R.op("vector", "tensor_scalar", w4[:], u4[:], -0.5, 1.5,
                     ALU.mult, ALU.add)
                R.op("vector", "tensor_mul", rstd4[:], rstd4[:], w4[:])
            for tc in range(4):
                R.op("vector", "tensor_scalar", dst_tile[:, bass.ts(tc, D)],
                     src_tile[:, bass.ts(tc, D)], m4[:, tc:tc + 1],
                     rstd4[:, tc:tc + 1], ALU.subtract, ALU.mult, inc=(SV, 1))

        xload1(0, 0)
        for b in range(NB):
            t0 = b * TB
            par = b % 2
            xblk1 = xb2[par]
            R.wait_tok("vector", x1toks[par])
            R.wait_tok("scalar", x1toks[par])
            if b + 1 < NB:
                R.wait("sync", SV)
                R.wait("sync", SS)
                xload1(b + 1, 1 - par)
            emit_ln14(R, xblk1[:], xnblk1[:])
            R.wait_tok("vector", st_xt)
            xtsv = {0: 0, 1: 0}
            R.wait("tensor", SV)
            for ic in range(8):
                pst = (ps_tr, ps_msc)[ic % 2]
                if xtsv[ic % 2]:
                    R.wait("tensor", SV, xtsv[ic % 2])
                for tc in range(4):
                    R.op("tensor", "transpose", pst[:, bass.ts(tc, 128)],
                         xnblk1[:, tc * D + ic * 128: tc * D + ic * 128 + 128],
                         identf[:], inc=(TN, 1))
                R.wait("vector", TN)
                R.op("vector", "tensor_copy", xn1T_bf[:, bass.ts(ic, TB)],
                     pst[:, 0:512])
                R.op("vector", "tensor_copy", xn1T_f[:, bass.ts(ic, TB)],
                     pst[:, 0:512])
                xtsv[ic % 2] = SV.n
            xnev_sv = SV.n
            R.wait("sync", SV)
            R.wait_tok("sync", st_xt)
            xts = bass.AP(xn1T_bf[:].tensor, xn1T_bf[:].offset,
                          [xn1T_bf[:].ap[0], [TB, 8], [1, TB]])
            xtd = bass.AP(xn1T_hbm.tensor, xn1T_hbm.offset + t0,
                          [[S, 128], [S * 128, 8], [1, TB]])
            st_xt = R.dma(xtd, xts)
            for tc in range(4):
                xc = xnblk1[:, bass.ts(tc, D)]
                R.op("vector", "reduce_max", rmax4[:, tc:tc + 1], xc, AX.X,
                     apply_absolute_value=True, inc=(SV, 1))
            R.op("vector", "tensor_scalar", stat1["u"][:], rmax4[:],
                 1.0, 1e-8, ALU.mult, ALU.add, inc=(SV, 1))
            R.op("vector", "reciprocal", stat1["u"][:], stat1["u"][:],
                 inc=(SV, 1))
            if aq_tn[0]:
                R.wait("vector", TN, aq_tn[0])
            for tc in range(4):
                xc = xnblk1[:, bass.ts(tc, D)]
                R.op("vector", "tensor_scalar", aqtmp[:, bass.ts(tc, D)], xc,
                     stat1["u"][:, tc:tc + 1], 127.0, ALU.mult, ALU.mult,
                     inc=(SV, 1))
            R.op("vector", "tensor_scalar", aqtmp[:], aqtmp[:], MAGIC, MAGIC,
                 ALU.add, ALU.subtract, inc=(SV, 1))
            aqc_sv = SV.n
            for dc in range(8):
                pd = ps_dt[dc % 2]
                R.wait("tensor", SS)
                R.wait("tensor", SV, xnev_sv)
                for ic in range(8):
                    R.op("tensor", "matmul", pd[:],
                         wdtT[:, ic * D + dc * 128: ic * D + dc * 128 + 128],
                         xn1T_bf[:, bass.ts(ic, TB)],
                         start=(ic == 0), stop=(ic == 7), inc=(TN, 1))
                R.wait("scalar", TN)
                if dc == 0:
                    R.wait("scalar", SV)
                R.op("scalar", "activation", t1t[:, bass.ts(dc, TB)], pd[:],
                     AF.Exp, bias=bdt_c[:, dc:dc + 1], inc=(SS, 1))
            R.op("scalar", "activation", t1t[:], t1t[:], AF.Ln,
                 bias=c1[:, 0:1], inc=(SS, 1))
            R.op("scalar", "activation", t1t[:], t1t[:], AF.Tanh,
                 scale=1.0 / 0.099, bias=cS1[:, 0:1], inc=(SS, 1))
            R.wait("vector", SS)
            R.wait_tok("vector", st_dt)
            R.op("vector", "tensor_scalar", dtblk[:], t1t[:], 0.099, 0.0505,
                 ALU.mult, ALU.add, inc=(SV, 1))
            R.wait_tok("vector", st_dtx)
            R.op("vector", "tensor_mul", dtxblk[:], dtblk[:], xn1T_f[:],
                 inc=(SV, 1))
            R.wait("sync", SV)
            dstd = bass.AP(dt_hbm.tensor, dt_hbm.offset + t0,
                           [[S, 128], [128 * S, 8], [1, TB]])
            dsrc = bass.AP(dtblk[:].tensor, dtblk[:].offset,
                           [dtblk[:].ap[0], [TB, 8], [1, TB]])
            st_dt = R.dma(dstd, dsrc)
            dstx = bass.AP(dtx_hbm.tensor, dtx_hbm.offset + t0,
                           [[S, 128], [128 * S, 8], [1, TB]])
            xsrc = bass.AP(dtxblk[:].tensor, dtxblk[:].offset,
                           [dtxblk[:].ap[0], [TB, 8], [1, TB]])
            st_dtx = R.dma(dstx, xsrc)
            trsv = {0: 0, 1: 0}
            R.wait("tensor", SV, aqc_sv)
            for ic in range(8):
                pst = (ps_tr, ps_msc)[ic % 2]
                if trsv[ic % 2]:
                    R.wait("tensor", SV, trsv[ic % 2])
                for tc in range(4):
                    R.op("tensor", "transpose", pst[:, bass.ts(tc, 128)],
                         aqtmp[:, tc * D + ic * 128: tc * D + ic * 128 + 128],
                         identf[:], inc=(TN, 1))
                R.wait("vector", TN)
                dsta = bass.AP(aqT[:].tensor, aqT[:].offset + ic * TB,
                               [aqT[:].ap[0], [128, 4], [1, 128]])
                srca = bass.AP(pst[:].tensor, pst[:].offset,
                               [pst[:].ap[0], [128, 4], [1, 128]])
                R.op("vector", "tensor_copy", dsta, srca, inc=(SV, 1))
                trsv[ic % 2] = SV.n
                aq_tn[0] = TN.n
            R.wait("tensor", SV)
            for tc in range(4):
                R.op("tensor", "transpose", ps_msc[0:1, bass.ts(tc, 128)],
                     rmax4[:, tc:tc + 1], identf[:], inc=(TN, 1))
            R.wait("scalar", TN)
            R.op("scalar", "activation", sclrow[:], ps_msc[0:1, 0:512], AF.Identity,
                 scale=wsd127[0:1, 0:1], inc=(SS, 1))
            R.wait("tensor", SS)
            R.op("tensor", "matmul", ps_msc[:, 0:512], ones_f[:], sclrow[:],
                 start=True, stop=True, inc=(TN, 1))
            R.wait("vector", TN)
            R.op("vector", "tensor_copy", sclb[:], ps_msc[:, 0:512], inc=(SV, 1))
            for wt, stage, hbm, sttok in ((wBT, bpstage, bp_hbm, "bp"),
                                          (wCT, cpstage, cp_hbm, "cp")):
                if bp_sv[0]:
                    R.wait("tensor", SV, bp_sv[0])
                for ic in range(8):
                    R.op("tensor", "matmul", ps_bp[:],
                         wt[:, ic * 8: ic * 8 + 8],
                         xn1T_f[:, bass.ts(ic, TB)],
                         start=(ic == 0), stop=(ic == 7), inc=(TN, 1))
                R.wait("vector", TN)
                R.wait_tok("vector", st_bp if sttok == "bp" else st_cp)
                R.op("vector", "tensor_copy", stage[:], ps_bp[:], inc=(SV, 1))
                bp_sv[0] = SV.n
                R.wait("sync", SV)
                dstb = bass.AP(hbm.tensor, hbm.offset + t0, [[S, 8], [1, TB]])
                if sttok == "bp":
                    st_bp = R.dma(dstb, stage[:])
                else:
                    st_cp = R.dma(dstb, stage[:])
            for fc in range(8):
                pb = ps_bn[fc % 2]
                R.wait("tensor", SV)
                for ic in range(8):
                    R.op("tensor", "matmul", pb[:],
                         wq1T[:, ic * D + fc * 128: ic * D + fc * 128 + 128],
                         aqT[:, bass.ts(ic, TB)],
                         start=(ic == 0), stop=(ic == 7), inc=(TN, 1))
                R.wait("vector", TN)
                R.wait("vector", SS)
                R.wait_tok("vector", st_g1)
                R.op("vector", "tensor_mul", g1stage[:], pb[:], sclb[:],
                     inc=(SV, 1))
                R.wait("scalar", SV)
                R.op("scalar", "activation", g1stage[:], g1stage[:], AF.Silu,
                     inc=(SS, 1))
                R.wait("sync", SS)
                dst = bass.AP(g1_hbm.tensor, g1_hbm.offset + fc * 128 * S + t0,
                              [[S, 128], [1, TB]])
                st_g1 = R.dma(dst, g1stage[:])
        R.wait_all_dma("sync")
        R.play()

    if stop_after == "l1a":
        l1.close()
        return nc

    TMAX = 120
    SSTR = 12
    from collections import deque
    with ExitStack() as ph:
        R = Rec(nc, dpool, AUTO)
        sb = lambda nm, sh, d=F32: ph.enter_context(nc.sbuf_tensor(nm, sh, d))
        dA_all = sb("dA_all", [128, 64 * TMAX], FP16)
        u_all = sb("u_all", [128, 64 * TMAX], FP16)
        Z3 = sb("Z3", [128, 3 * 64 * TMAX], FP16)
        P3 = sb("P3", [128, 3 * 64 * TMAX], FP16)
        dt8 = [sb(f"dt8_{i}", [128, 8 * TMAX]) for i in range(2)]
        dtx8 = [sb(f"dtx8_{i}", [128, 8 * TMAX], FP16) for i in range(2)]
        xn1T8 = [sb(f"xn1T8_{i}", [128, 8 * TMAX], BF16) for i in range(2)]
        bp8 = [sb(f"bp8_{i}", [1, 8 * TMAX]) for i in range(2)]
        cp8 = [sb(f"cp8_{i}", [1, 8 * TMAX]) for i in range(3)]
        y83 = sb("y83", [128, 3 * 8 * TMAX], FP16)
        etp = [sb("etp_0", [128, 16 * TMAX])]
        uptmp = sb("uptmp", [128, 16 * TMAX], FP16)
        htmp2 = [sb(f"htmp8_{i}", [128, 16 * TMAX], FP16) for i in range(2)]
        be4 = [sb(f"be4_{i}", [128, 4 * TMAX], FP16) for i in range(2)]
        ce4 = [sb(f"ce4_{i}", [128, 4 * TMAX], FP16) for i in range(2)]
        Hc = sb("Hc", [128, 64 * SSTR])
        Hprev2 = [sb(f"Hprev{i}", [128, 64 * SSTR]) for i in range(2)]
        cbf2 = [sb(f"cbf{i}", [128, SSTR]) for i in range(2)]
        tmp64 = sb("tmp64", [128, 64])
        sq64 = sb("sq64", [128, 64])
        nrmcol = sb("nrmcol", [128, 1])
        c_row2 = [sb(f"c_row{i}", [1, SSTR]) for i in range(2)]
        c_sc = sb("c_sc", [1, 8])
        hcar = sb("hcar", [128, 64])
        hcar_in = sb("hcar_in", [128, 64])
        ps_be = ph.enter_context(nc.psum_tensor("sc_be", [128, 4 * TMAX], F32))
        ps_ce = ph.enter_context(nc.psum_tensor("sc_ce", [128, 4 * TMAX], F32))
        ps_abd = ph.enter_context(nc.psum_tensor("sc_abd", [1, 8], F32))
        ps_cb = ph.enter_context(nc.psum_tensor("sc_cb", [128, SSTR], F32))

        TILES = ([(0, 1, 1)] + [(1 + 120 * j, 120, 12) for j in range(34)]
                 + [(4081, 15, 1)])
        NT = len(TILES)
        R.op("vector", "memset", hcar[:], 0.0, inc=(SV, 1))

        ld_toks = [None, None]
        cp_toks = [None, None, None]
        st_y = [None, None, None]
        ess = [0]
        essn = {}
        argsv = [0]
        be_hold = [0]
        be_rd = [0, 0]
        be_usv = [0, 0]
        bmtn = [0]
        ce_hold = [0]
        ce_rd = [0, 0]
        ce_usv = [0, 0]
        cmtn = [0]
        cpre_sv = [0]
        ht_sg = [0, 0]
        up_T = [0]
        cst = {"cb": 0, "abd": 0, "cbf": 0}

        def emit_loads(j):
            t0, T, _ = TILES[j]
            jl, js = j % 2, j % 3
            R.wait("sync", SV)
            R.wait("sync", TN)
            toks = []
            for hbm_t, dstl in ((dt_hbm, dt8), (dtx_hbm, dtx8),
                                (xn1T_hbm, xn1T8)):
                dst = dstl[jl]
                srcp = bass.AP(hbm_t.tensor, hbm_t.offset + t0,
                               [[S, 128], [S * 128, 8], [1, T]])
                dstp = bass.AP(dst[:].tensor, dst[:].offset,
                               [dst[:].ap[0], [T, 8], [1, T]])
                toks.append(R.dma(dstp, srcp,
                                  allow_slow_non_contiguous=(T < 8)))
            bdst = bass.AP(bp8[jl][:].tensor, bp8[jl][:].offset,
                           [bp8[jl][:].ap[0], [T, 8], [1, T]])
            toks.append(R.dma(bdst,
                              bass.AP(bp_hbm.tensor, bp_hbm.offset + t0,
                                      [[S * 8, 1], [S, 8], [1, T]]),
                              allow_slow_non_contiguous=(T < 8)))
            ld_toks[jl] = toks
            cdst = bass.AP(cp8[js][:].tensor, cp8[js][:].offset,
                           [cp8[js][:].ap[0], [T, 8], [1, T]])
            cp_toks[js] = R.dma(cdst,
                                bass.AP(cp_hbm.tensor, cp_hbm.offset + t0,
                                        [[S * 8, 1], [S, 8], [1, T]]),
                                allow_slow_non_contiguous=(T < 8))

        def s1_stage(j):
            t0, T, nseg = TILES[j]
            tail = T - nseg * 10
            nst = nseg + (1 if tail > 0 else 0)
            seglen = 10 if T >= 10 else T
            T8 = 8 * T
            jl, js = j % 2, j % 3
            zoff = js * 64 * TMAX
            yoff = js * 8 * TMAX
            pre_start = []
            pre_pairs = [[] for _ in range(4)]
            ops = []
            em = lambda c, f: ops.append((c, f))
            emp = lambda c, f: pre_start.append((c, f))

            def fstart():
                for tk in ld_toks[jl]:
                    R.wait_tok("vector", tk)
                R.wait_tok("tensor", ld_toks[jl][3])
            emp(0, fstart)
            for q in range(2):
                def f_mm(q=q):
                    if be_hold[0]:
                        R.wait("tensor", SS, be_hold[0])
                    R.op("tensor", "matmul", ps_be[:, 0:4 * T], ones_f[:],
                         bp8[jl][0:1, q * 4 * T: (q + 1) * 4 * T],
                         start=True, stop=True, inc=(TN, 1))
                    bmtn[0] = TN.n

                def f_evac(q=q):
                    R.wait("scalar", TN, bmtn[0])
                    if be_usv[q]:
                        R.wait("scalar", SV, be_usv[q])
                    R.op("scalar", "copy", be4[q][:, 0:4 * T],
                         ps_be[:, 0:4 * T], inc=(SS, 1))
                    be_hold[0] = SS.n
                    be_rd[q] = SS.n
                emp(0, f_mm)
                emp(0, f_evac)
            for p in range(4):
                n = 2 * p
                nof = n * 8 * T
                ep = p % 2

                def f_arg(n=n, ep=ep):
                    if ess[0]:
                        R.wait("vector", SS, ess[0])
                    et = etp[0]
                    arg_o = bass.AP(et[:].tensor, et[:].offset,
                                    [et[:].ap[0], [T8, 2], [T, 8], [1, T]])
                    dt_i = bass.AP(dt8[jl][:].tensor, dt8[jl][:].offset,
                                   [dt8[jl][:].ap[0], [0, 2], [T, 8], [1, T]])
                    a_i = bass.AP(a16[:].tensor, a16[:].offset + n,
                                  [a16[:].ap[0], [1, 2], [8, 8], [0, T]])
                    R.op("vector", "tensor_mul", arg_o, dt_i, a_i, inc=(SV, 1))
                    argsv[0] = SV.n

                def f_tanh(n=n, nof=nof, ep=ep, p=p):
                    R.wait("scalar", SV, argsv[0])
                    et = etp[0]
                    R.op("scalar", "activation", et[:, 0:2 * T8], et[:, 0:2 * T8],
                         AF.Tanh, inc=(SS, 1))
                    R.op("scalar", "activation", dA_all[:, nof: nof + 2 * T8],
                         et[:, 0:2 * T8], AF.Exp, scale=16.0, inc=(SS, 1))
                    ess[0] = SS.n
                    essn[(j, p)] = SS.n
                pre_pairs[p].append((2.2, f_arg))
                pre_pairs[p].append((0, f_tanh))
            for p in range(4):
                n = 2 * p
                nof = n * 8 * T
                q = p // 2

                def f_u(n=n, nof=nof, q=q):
                    R.wait("vector", SS, be_rd[q])
                    u_o = bass.AP(u_all[:].tensor, u_all[:].offset + nof,
                                  [u_all[:].ap[0], [T8, 2], [T, 8], [1, T]])
                    dtx_i = bass.AP(dtx8[jl][:].tensor, dtx8[jl][:].offset,
                                    [dtx8[jl][:].ap[0], [0, 2], [T, 8], [1, T]])
                    be_i = bass.AP(be4[q][:].tensor,
                                   be4[q][:].offset + (n % 4) * T,
                                   [be4[q][:].ap[0], [T, 2], [0, 8], [1, T]])
                    R.op("vector", "tensor_mul", u_o, dtx_i, be_i, inc=(SV, 1))
                    be_usv[q] = SV.n

                def f_up(n=n, nof=nof, p=p):
                    R.wait("vector", SS, essn[(j, p)])
                    if up_T[0] != T:
                        R.op("vector", "memset", uptmp[:, 0:16 * TMAX], 0.0,
                             inc=(SV, 1))
                        up_T[0] = T
                    upd = bass.AP(uptmp[:].tensor, uptmp[:].offset,
                                  [uptmp[:].ap[0], [T8, 2], [T, 8], [10, nst]])
                    das = bass.AP(dA_all[:].tensor, dA_all[:].offset + nof,
                                  [dA_all[:].ap[0], [T8, 2], [T, 8], [10, nst]])
                    R.op("vector", "tensor_copy", upd, das, inc=(SV, 1))
                    R.op("vector", "memset", das, 0.0, inc=(SV, 1))

                def f_zscan(nof=nof):
                    R.op("vector", "tensor_tensor_scan",
                         Z3[:, zoff + nof: zoff + nof + 2 * T8],
                         dA_all[:, nof: nof + 2 * T8],
                         u_all[:, nof: nof + 2 * T8],
                         0.0, ALU.mult, ALU.add, inc=(SV, 1))

                def f_pscan(nof=nof):
                    R.op("vector", "tensor_tensor_scan",
                         P3[:, zoff + nof: zoff + nof + 2 * T8],
                         dA_all[:, nof: nof + 2 * T8], uptmp[:, 0:2 * T8],
                         0.0, ALU.mult, ALU.add, inc=(SV, 1))

                em(1.4, f_u)

                def f_upz(f1=f_up, f2=f_zscan):
                    f1()
                    f2()
                em(5.8, f_upz)
                em(4.6, f_pscan)

            def f_yinit():
                R.wait_tok("vector", st_y[js])
                y_o = bass.AP(y83[:].tensor, y83[:].offset + yoff,
                              [y83[:].ap[0], [T, 8], [1, T]])
                x_i = bass.AP(xn1T8[jl][:].tensor, xn1T8[jl][:].offset,
                              [xn1T8[jl][:].ap[0], [T, 8], [1, T]])
                d_i = bass.AP(dpar_c[:].tensor, dpar_c[:].offset,
                              [dpar_c[:].ap[0], [1, 8], [0, T]])
                R.op("vector", "tensor_mul", y_o, x_i, d_i, inc=(SV, 1))
            em(0.7, f_yinit)
            if j + 1 < NT:
                em(0, lambda j=j: emit_loads(j + 1))
            return pre_start, pre_pairs, ops

        def s2_stage(j):
            t0, T, nseg = TILES[j]
            tail = T - nseg * 10
            seglen = 10 if T >= 10 else T
            T8 = 8 * T
            js, jp = j % 3, j % 2
            zoff = js * 64 * TMAX
            yoff = js * 8 * TMAX
            ops = []
            em = lambda c, f: ops.append((c, f))
            def f_cpre():
                R.wait_tok("vector", cp_toks[js])
                ce_d = bass.AP(cp8[js][:].tensor, cp8[js][:].offset + seglen - 1,
                               [cp8[js][:].ap[0], [T, 8], [10, nseg]])
                cr_i = bass.AP(c_row2[jp][:].tensor, c_row2[jp][:].offset,
                               [c_row2[jp][:].ap[0], [0, 8], [1, nseg]])
                R.op("vector", "tensor_mul", ce_d, ce_d, cr_i, inc=(SV, 1))
                cpre_sv[0] = SV.n
            em(0.3, f_cpre)
            for q in range(2):
                def f_mm(q=q):
                    if ce_hold[0]:
                        R.wait("tensor", SS, ce_hold[0])
                    R.wait("tensor", SV, cpre_sv[0])
                    R.op("tensor", "matmul", ps_ce[:, 0:4 * T], ones_f[:],
                         cp8[js][0:1, q * 4 * T: (q + 1) * 4 * T],
                         start=True, stop=True, inc=(TN, 1))
                    cmtn[0] = TN.n

                def f_evac(q=q):
                    R.wait("scalar", TN, cmtn[0])
                    if ce_usv[q]:
                        R.wait("scalar", SV, ce_usv[q])
                    R.op("scalar", "copy", ce4[q][:, 0:4 * T],
                         ps_ce[:, 0:4 * T], inc=(SS, 1))
                    ce_hold[0] = SS.n
                    ce_rd[q] = SS.n
                em(0, f_mm)
                em(0, f_evac)
            seq = []
            adds_sg = {}
            hc_sv = {}
            y_sg = {}
            for p in range(4):
                n = 2 * p
                nof = n * 8 * T
                q = p // 2
                hb = htmp2[p % 2]

                def f_pmul(n=n, nof=nof, p=p, hb=hb):
                    for h in range(2):
                        hof = h * T8
                        p4 = bass.AP(P3[:].tensor,
                                     P3[:].offset + zoff + nof + hof,
                                     [P3[:].ap[0], [T, 8], [10, nseg],
                                      [1, seglen]])
                        h4 = bass.AP(Hprev2[jp][:].tensor,
                                     Hprev2[jp][:].offset + (n + h) * 8 * SSTR,
                                     [Hprev2[jp][:].ap[0], [SSTR, 8], [1, nseg],
                                      [0, seglen]])
                        o4 = bass.AP(hb[:].tensor, hb[:].offset + hof,
                                     [hb[:].ap[0], [T, 8], [10, nseg],
                                      [1, seglen]])
                        R.op("vector", "tensor_mul", o4, p4, h4, inc=(SV, 1))
                        if tail > 0:
                            pt = bass.AP(P3[:].tensor,
                                         P3[:].offset + zoff + nof + hof
                                         + nseg * 10,
                                         [P3[:].ap[0], [T, 8], [1, tail]])
                            ht = bass.AP(Hprev2[jp][:].tensor,
                                         Hprev2[jp][:].offset
                                         + (n + h) * 8 * SSTR + nseg,
                                         [Hprev2[jp][:].ap[0], [SSTR, 8],
                                          [0, tail]])
                            ot = bass.AP(hb[:].tensor,
                                         hb[:].offset + hof + nseg * 10,
                                         [hb[:].ap[0], [T, 8], [1, tail]])
                            R.op("vector", "tensor_mul", ot, pt, ht,
                                 inc=(SV, 1))

                def f_add(nof=nof, p=p, hb=hb):
                    R.op("vector", "tensor_add", hb[:, 0:2 * T8],
                         hb[:, 0:2 * T8],
                         Z3[:, zoff + nof: zoff + nof + 2 * T8], inc=(SV, 1))

                def f_hc(n=n, q=q, p=p, hb=hb):
                    R.wait("vector", SS, ce_rd[q])
                    h_i = bass.AP(hb[:].tensor, hb[:].offset,
                                  [hb[:].ap[0], [T8, 2], [T, 8], [1, T]])
                    ce_i = bass.AP(ce4[q][:].tensor,
                                   ce4[q][:].offset + (n % 4) * T,
                                   [ce4[q][:].ap[0], [T, 2], [0, 8], [1, T]])
                    R.op("vector", "tensor_mul", h_i, h_i, ce_i, inc=(SV, 1))
                    ce_usv[q] = SV.n
                    hc_sv[p] = SV.n

                def f_y(p=p, hb=hb):
                    for h in range(2):
                        R.op("vector", "tensor_add",
                             y83[:, yoff: yoff + T8],
                             y83[:, yoff: yoff + T8],
                             hb[:, h * T8: (h + 1) * T8], inc=(SV, 1))
                seq.append((p, f_pmul, f_add, f_hc, f_y))
            em(1.6, seq[0][1])
            em(1.2, seq[0][2])
            em(1.6, seq[1][1])
            em(1.2, seq[1][2])
            for p in range(4):
                if p + 2 < 4:
                    em(0.9, seq[p][3])
                    em(1.2, seq[p][4])
                    em(1.6, seq[p + 2][1])
                    em(1.2, seq[p + 2][2])
                else:
                    em(0.9, seq[p][3])
                    em(1.2, seq[p][4])
            def f_store():
                R.wait("sync", SV)
                dsty = bass.AP(y_hbm.tensor, y_hbm.offset + t0,
                               [[S, 128], [128 * S, 8], [1, T]])
                ysrc = bass.AP(y83[:].tensor, y83[:].offset + yoff,
                               [y83[:].ap[0], [T, 8], [1, T]])
                st_y[js] = R.dma(dsty, ysrc, allow_slow_non_contiguous=(T < 8))
            em(0, f_store)
            return ops

        def chain_stage(j, filler, bigq=None):
            t0, T, nseg = TILES[j]
            tail = T - nseg * 10
            nst = nseg + (1 if tail > 0 else 0)
            seglen = 10 if T >= 10 else T
            jp, js = j % 2, j % 3
            zoff = js * 64 * TMAX

            def take(budget):
                acc = 0.0
                while filler and acc < budget:
                    c, fn = filler.popleft()
                    fn()
                    acc += c
                if acc < budget and bigq:
                    c, fn = bigq.popleft()
                    fn()

            R.op("vector", "tensor_copy", hcar_in[:], hcar[:], inc=(SV, 1))
            take(1.0)
            for k in range(nseg):
                eoff = zoff + k * 10 + seglen - 1
                gsl = bass.AP(P3[:].tensor, P3[:].offset + eoff,
                              [P3[:].ap[0], [T, 64]])
                lsl = bass.AP(Z3[:].tensor, Z3[:].offset + eoff,
                              [Z3[:].ap[0], [T, 64]])
                hck = bass.AP(Hc[:].tensor, Hc[:].offset + k,
                              [Hc[:].ap[0], [SSTR, 64]])
                if k == 0:
                    R.op("vector", "scalar_tensor_tensor", tmp64[:], hcar_in[:],
                         1.0, gsl, ALU.mult, ALU.mult, inc=(SV, 1))
                else:
                    prev = bass.AP(Hc[:].tensor, Hc[:].offset + k - 1,
                                   [Hc[:].ap[0], [SSTR, 64]])
                    R.wait("vector", TN, cst["cb"])
                    R.op("vector", "scalar_tensor_tensor", tmp64[:], prev,
                         ps_cb[:, 0:1], gsl, ALU.mult, ALU.mult, inc=(SV, 1))
                R.op("vector", "tensor_add", hck, tmp64[:], lsl, inc=(SV, 1))
                R.op("vector", "scalar_tensor_tensor", sq64[:], hck, 1.0, hck,
                     ALU.mult, ALU.mult, accum_out=nrmcol[:], inc=(SV, 1))
                sv_head = SV.n
                R.wait("tensor", SV, sv_head)
                if cst["abd"]:
                    R.wait("tensor", SS, cst["abd"])
                R.op("tensor", "matmul", ps_abd[0:1, 0:1], ones_c[:], nrmcol[:],
                     start=True, stop=True, inc=(TN, 1))
                tn1 = TN.n
                R.wait("scalar", TN, tn1)
                R.op("scalar", "activation", c_sc[0:1, 0:1], ps_abd[0:1, 0:1],
                     AF.Ln, inc=(SS, 1))
                cst["abd"] = SS.n
                R.op("scalar", "activation", c_sc[0:1, 1:2], c_sc[0:1, 0:1],
                     AF.Exp, scale=-0.5, inc=(SS, 1))
                ss_k = SS.n
                take(2.2)
                R.wait("vector", SS, ss_k)
                if k == 0 and cst["cbf"]:
                    R.wait("vector", TN, cst["cbf"])
                R.op("vector", "tensor_scalar", c_row2[jp][0:1, k:k + 1],
                     c_sc[0:1, 1:2], 20.0, 1.0, ALU.mult, ALU.min, inc=(SV, 1))
                sv_tail = SV.n
                if k + 1 < nseg:
                    R.wait("tensor", SV, sv_tail)
                    R.op("tensor", "matmul", ps_cb[:, 0:1], ones_f[:],
                         c_row2[jp][0:1, k:k + 1], start=True, stop=True,
                         inc=(TN, 1))
                    cst["cb"] = TN.n
                take(1.2)
            R.wait("tensor", SV, sv_tail)
            R.op("tensor", "matmul", ps_cb[:, 0:nseg], ones_f[:],
                 c_row2[jp][0:1, 0:nseg], start=True, stop=True, inc=(TN, 1))
            cst["cbf"] = TN.n
            R.wait("vector", TN, cst["cbf"])
            R.op("vector", "tensor_copy", cbf2[jp][:, 0:nseg], ps_cb[:, 0:nseg],
                 inc=(SV, 1))
            hl = bass.AP(Hc[:].tensor, Hc[:].offset + nseg - 1,
                         [Hc[:].ap[0], [SSTR, 64]])
            R.op("vector", "tensor_scalar", hcar[:], hl,
                 cbf2[jp][:, nseg - 1:nseg], None, ALU.mult, inc=(SV, 1))
            hp0 = bass.AP(Hprev2[jp][:].tensor, Hprev2[jp][:].offset,
                          [Hprev2[jp][:].ap[0], [SSTR, 64]])
            R.op("vector", "tensor_copy", hp0, hcar_in[:], inc=(SV, 1))
            if nst > 1:
                hpk = bass.AP(Hprev2[jp][:].tensor, Hprev2[jp][:].offset + 1,
                              [Hprev2[jp][:].ap[0], [SSTR, 64], [1, nst - 1]])
                hck2 = bass.AP(Hc[:].tensor, Hc[:].offset,
                               [Hc[:].ap[0], [SSTR, 64], [1, nst - 1]])
                cbk = bass.AP(cbf2[jp][:].tensor, cbf2[jp][:].offset,
                              [cbf2[jp][:].ap[0], [0, 64], [1, nst - 1]])
                R.op("vector", "tensor_mul", hpk, hck2, cbk, inc=(SV, 1))

        emit_loads(0)
        sB = {}
        ps0, pp0, opsB0 = s1_stage(0)
        sB[0] = opsB0
        for c, fn in ps0:
            fn()
        for pl in pp0:
            for c, fn in pl:
                fn()
        for r in range(NT + 2):
            items = []
            if r - 2 >= 0:
                items.extend(s2_stage(r - 2))
            items.extend(sB.pop(r, []))
            if r + 1 < NT:
                pre_start, pre_pairs, opsB = s1_stage(r + 1)
                sB[r + 1] = opsB
            else:
                pre_start, pre_pairs = [], [[] for _ in range(4)]
            small = deque(x for x in items if x[0] <= 2.5)
            bigs = [x for x in items if x[0] > 2.5]
            big = deque()
            nb = len(bigs)
            same_T = (r < NT and r + 1 < NT and TILES[r][1] == TILES[r + 1][1])
            if same_T and nb:
                for i in range(max(nb, 8)):
                    if i < nb:
                        big.append(bigs[i])
                    if i == 1:
                        big.extend(pre_start)
                        pre_start = []
                    if i % 2 == 1 and i // 2 < 4:
                        big.extend(pre_pairs[i // 2])
            else:
                big.extend(bigs)
                big.extend(pre_start)
                for pl in pre_pairs:
                    big.extend(pl)
            if 0 <= r - 1 < NT:
                chain_stage(r - 1, small, big)
            for dq in (small, big):
                while dq:
                    c, fn = dq.popleft()
                    fn()
        R.wait_all_dma("sync")
        R.play()

    with ExitStack() as ph:
        R = Rec(nc, dpool, AUTO)
        yf2 = [ph.enter_context(nc.sbuf_tensor(f"yfc{i}", [128, 8 * TB], FP16))
               for i in range(2)]
        g1f2 = [ph.enter_context(nc.sbuf_tensor(f"g1fc{i}", [128, 8 * TB], FP16))
                for i in range(2)]
        sgfc = ph.enter_context(nc.sbuf_tensor("sgfc", [128, 8 * TB], F32))
        ob2 = [ph.enter_context(nc.sbuf_tensor(f"oblk{i}", [128, 4 * D], F32))
               for i in range(2)]
        xg2 = [ph.enter_context(nc.sbuf_tensor(f"xgblk{i}", [128, 4 * D], F32))
               for i in range(2)]
        ps_g2 = [ph.enter_context(nc.psum_tensor(f"gt_{i}", [128, 512], F32))
                 for i in range(2)]

        ytoks = {}

        def yg_load(b_, par_):
            t0_ = b_ * TB
            ysrc = bass.AP(y_hbm.tensor, y_hbm.offset + t0_,
                           [[S, 128], [128 * S, 8], [1, TB]])
            ydst = bass.AP(yf2[par_][:].tensor, yf2[par_][:].offset,
                           [yf2[par_][:].ap[0], [TB, 8], [1, TB]])
            ty = R.dma(ydst, ysrc)
            gsrc = bass.AP(g1_hbm.tensor, g1_hbm.offset + t0_,
                           [[S, 128], [128 * S, 8], [1, TB]])
            gdst = bass.AP(g1f2[par_][:].tensor, g1f2[par_][:].offset,
                           [g1f2[par_][:].ap[0], [TB, 8], [1, TB]])
            tg = R.dma(gdst, gsrc)
            ytoks[par_] = (ty, tg)

        xgtoks = {}

        def xg_load(b_, par_):
            src = bass.AP(x1_hbm.tensor, x1_hbm.offset + b_ * TB * D,
                          [[D, 128], [128 * D, 4], [1, D]])
            xg = xg2[par_]
            xdst = bass.AP(xg[:].tensor, xg[:].offset,
                           [xg[:].ap[0], [D, 4], [1, D]])
            xgtoks[par_] = R.dma(xdst, src)

        st_o = [None, None]
        xg_load(0, 0)
        yg_load(0, 0)
        trn_g = {0: 0, 1: 0}
        for b in range(NB):
            t0 = b * TB
            bpar = b % 2
            if b + 1 < NB:
                R.wait("sync", SV)
                R.wait_tok("sync", st_o[1 - bpar])
                xg_load(b + 1, 1 - bpar)
                yg_load(b + 1, 1 - bpar)
            R.wait_tok("vector", xgtoks[bpar])
            R.wait_tok("vector", st_o[bpar])
            ty, tg = ytoks[bpar]
            R.wait_tok("vector", ty)
            R.wait_tok("vector", tg)
            R.op("vector", "tensor_mul", sgfc[:], yf2[bpar][:], g1f2[bpar][:],
                 inc=(SV, 1))
            sg_sv = SV.n
            for dp in range(8):
                pst = ps_g2[dp % 2]
                R.wait("tensor", SV, max(trn_g[dp % 2], sg_sv))
                for tc in range(4):
                    R.op("tensor", "transpose", pst[:, bass.ts(tc, 128)],
                         sgfc[:, dp * TB + tc * 128: dp * TB + tc * 128 + 128],
                         identf[:], inc=(TN, 1))
                R.wait("vector", TN)
                ob = ob2[bpar]
                o_d = bass.AP(ob[:].tensor, ob[:].offset + dp * 128,
                              [ob[:].ap[0], [D, 4], [1, 128]])
                x_i = bass.AP(xg2[bpar][:].tensor, xg2[bpar][:].offset + dp * 128,
                              [xg2[bpar][:].ap[0], [D, 4], [1, 128]])
                p_i = bass.AP(pst[:].tensor, pst[:].offset,
                              [pst[:].ap[0], [128, 4], [1, 128]])
                R.op("vector", "tensor_add", o_d, p_i, x_i, inc=(SV, 1))
                trn_g[dp % 2] = SV.n
            R.wait("sync", SV)
            dsto = bass.AP(out.tensor, out.offset + t0 * D,
                           [[D, 128], [128 * D, 4], [1, D]])
            osrc = bass.AP(ob2[bpar][:].tensor, ob2[bpar][:].offset,
                           [ob2[bpar][:].ap[0], [D, 4], [1, D]])
            st_o[bpar] = R.dma(dsto, osrc)
        R.wait_all_dma("sync")
        R.play()
    l1.close()
    return nc


def kernel(**inp):
    nc = build()
    xs = np.ascontiguousarray(np.asarray(inp["L0_attn_x"], dtype=np.float32))

    def col8(v):
        return np.ascontiguousarray(np.asarray(v, dtype=np.float32).reshape(8, 128).T)

    base = {
        "qkv_w": np.ascontiguousarray(np.asarray(inp["L0_attn_qkv_w"], dtype=np.float32)),
        "qkv_b": np.ascontiguousarray(np.asarray(inp["L0_attn_qkv_b"], dtype=np.float32).reshape(1, -1)),
        "proj_w": np.ascontiguousarray(np.asarray(inp["L0_attn_proj_w"], dtype=np.float32)),
        "proj_b": col8(inp["L0_attn_proj_b"]),
        "f0_w": np.ascontiguousarray(np.asarray(inp["L0_ffn_weight"], dtype=np.float32)),
        "f0_b": col8(inp["L0_ffn_bias"]),
        "w_dt": np.ascontiguousarray(np.asarray(inp["L1_attn_W_dt"], dtype=np.float32)),
        "b_dt": col8(inp["L1_attn_b_dt"]),
        "w_B": np.ascontiguousarray(np.asarray(inp["L1_attn_W_B"], dtype=np.float32)),
        "w_C": np.ascontiguousarray(np.asarray(inp["L1_attn_W_C"], dtype=np.float32)),
        "a_log": np.ascontiguousarray(
            np.asarray(inp["L1_attn_A_log"], dtype=np.float32)
            .reshape(8, 128, 8).transpose(1, 0, 2).reshape(128, 64)),
        "d_par": col8(inp["L1_attn_D_param"]),
        "f1_w": np.ascontiguousarray(np.asarray(inp["L1_ffn_weight"], dtype=np.float32)),
    }
    in_maps = [dict(base, x=np.ascontiguousarray(xs[b])) for b in range(8)]
    res = run_bass_kernel_spmd(nc, in_maps, core_ids=list(range(8)))
    return np.stack([np.asarray(res.results[b]["out"], dtype=np.float32)
                     for b in range(8)], axis=0)


# version 28
# speedup vs baseline: 1.0481x; 1.0005x over previous
import os
import numpy as np
import ml_dtypes
import concourse.bass as bass
import concourse.mybir as mybir
from concourse.bass_utils import run_bass_kernel_spmd
from contextlib import ExitStack

F32 = mybir.dt.float32
BF16 = mybir.dt.bfloat16
FP16 = mybir.dt.float16
FP8 = mybir.dt.float8e4
AF = mybir.ActivationFunctionType
ALU = mybir.AluOpType
AX = mybir.AxisListType

S = 4096
D = 1024
H = 16
DH = 64
TB = 512
NB = S // TB
NST = 8
MAGIC = 12582912.0

ENGS = ("sync", "vector", "scalar", "tensor", "gpsimd")
SKIP = set(os.environ.get("K1_SKIP", "").split(","))

SCAN_TILES = [(0, 1, 1)] + [(1 + 320 * j, 320, 32) for j in range(12)] + [(3841, 255, 25)]


class Sem:
    def __init__(self, h):
        self.h = h
        self.n = 0


class Rec:

    def __init__(self, nc, dpool, auto):
        self.nc = nc
        self.q = {k: [] for k in ENGS}
        self.dpool = dpool
        self.didx = 0
        self.auto = auto

    def dma(self, dst, src, **kw):
        sem = self.dpool[self.didx % len(self.dpool)]
        self.didx += 1
        if sem.n > 0:
            self.wait("sync", sem)
        self.op("sync", "dma_start", inc=(sem, 16), out=dst, in_=src, **kw)
        return (sem, sem.n)

    def wait_tok(self, eng, tok):
        if tok is not None:
            self.q[eng].append(lambda e, h=tok[0].h, v=tok[1]: e.wait_ge(h, v))

    def wait_all_dma(self, eng):
        for sem in self.dpool:
            if sem.n:
                self.wait(eng, sem)

    def op(self, eng, method, *args, inc=None, **kw):
        if eng in self.auto:
            asem = self.auto[eng]
            if asem.n > 0:
                self.wait(eng, asem)
            if inc is None:
                inc = (asem, 1)
        tgt = None
        if inc is not None:
            sem, k = inc
            sem.n += k
            tgt = (sem.h, k)

        def thunk(e, method=method, args=args, kw=kw, tgt=tgt):
            o = getattr(e, method)(*args, **kw)
            if tgt is not None:
                o.then_inc(*tgt)
        self.q[eng].append(thunk)

    def wait(self, eng, sem, val=None):
        v = sem.n if val is None else val
        self.q[eng].append(lambda e, h=sem.h, v=v: e.wait_ge(h, v))

    def play(self):
        with self.nc.Block() as blk:
            for name in ENGS:
                lst = self.q[name]
                if lst:
                    getattr(blk, name)(lambda e, lst=lst: [f(e) for f in lst])
        self.nc.all_engine_barrier()
        self.q = {k: [] for k in ENGS}


def _np_rope_tables():
    inv = 1.0 / (10000.0 ** (np.arange(0, DH, 2, dtype=np.float32) / DH))
    t = np.arange(S, dtype=np.float32)
    fr = np.einsum("t,d->td", t, inv).astype(np.float32)
    fr = np.concatenate([fr, fr], axis=-1)
    cos = np.cos(fr).astype(np.float32).T
    sin = np.sin(fr).astype(np.float32).T
    cosT = np.concatenate([cos, cos], axis=0)
    sinT = np.concatenate([sin, sin], axis=0)
    return (np.asarray(cosT, dtype=ml_dtypes.bfloat16),
            np.asarray(sinT, dtype=ml_dtypes.bfloat16))


def build(stop_after=None):
    nc = bass.Bass("TRN2", target_bir_lowering=False)

    x_in = nc.dram_tensor("x", [S, D], F32, kind="ExternalInput").ap()
    qkv_w = nc.dram_tensor("qkv_w", [3 * D, D], F32, kind="ExternalInput").ap()
    qkv_b = nc.dram_tensor("qkv_b", [1, 3 * D], F32, kind="ExternalInput").ap()
    proj_w = nc.dram_tensor("proj_w", [D, D], F32, kind="ExternalInput").ap()
    proj_b = nc.dram_tensor("proj_b", [128, 8], F32, kind="ExternalInput").ap()
    f0_w = nc.dram_tensor("f0_w", [D, D], F32, kind="ExternalInput").ap()
    f0_b = nc.dram_tensor("f0_b", [128, 8], F32, kind="ExternalInput").ap()
    w_dt = nc.dram_tensor("w_dt", [D, D], F32, kind="ExternalInput").ap()
    b_dt = nc.dram_tensor("b_dt", [128, 8], F32, kind="ExternalInput").ap()
    w_B = nc.dram_tensor("w_B", [NST, D], F32, kind="ExternalInput").ap()
    w_C = nc.dram_tensor("w_C", [NST, D], F32, kind="ExternalInput").ap()
    a_log = nc.dram_tensor("a_log", [128, 64], F32, kind="ExternalInput").ap()
    d_par = nc.dram_tensor("d_par", [128, 8], F32, kind="ExternalInput").ap()
    f1_w = nc.dram_tensor("f1_w", [D, D], F32, kind="ExternalInput").ap()
    out = nc.dram_tensor("out", [S, D], F32, kind="ExternalOutput").ap()

    ident_d = nc.inline_tensor(np.eye(128, dtype=np.float32), name="ident_f").ap()
    cosT_np, sinT_np = _np_rope_tables()
    cos_d = nc.inline_tensor(cosT_np, name="cosT").ap()
    sin_d = nc.inline_tensor(sinT_np, name="sinT").ap()
    Rm = np.zeros((64, 64), dtype=np.float32)
    Rm[0:32, 32:64] = -np.eye(32)
    Rm[32:64, 0:32] = np.eye(32)
    RT2 = np.zeros((128, 128), dtype=np.float32)
    RT2[0:64, 0:64] = Rm.T
    RT2[64:128, 64:128] = Rm.T
    rot_d = nc.inline_tensor(np.asarray(RT2, dtype=ml_dtypes.bfloat16), name="rotT").ap()
    tri_np = (np.arange(128)[:, None] <= np.arange(128)[None, :])
    tri_d = nc.inline_tensor(np.asarray(tri_np, dtype=ml_dtypes.bfloat16), name="tri").ap()

    def scratch(name, shape, dt):
        return nc.dram_tensor(name, shape, dt, kind="Internal").ap()

    wqkvT_h = scratch("wqkvT_h", [D, 3 * D], BF16)
    wprojT_h = scratch("wprojT_h", [D, D], BF16)
    wdtT_h = scratch("wdtT_h", [D, D], BF16)
    wf0T_h = scratch("wf0T_h", [D, D], FP8)
    wq1T_h = scratch("wq1T_h", [D, D], BF16)
    k_hbm = scratch("k_hbm", [4, 128, S], BF16)
    v_hbm = scratch("v_hbm", [128, 8 * 32 * 65], BF16)
    g_hbm = scratch("g_hbm", [D, S], FP16)
    ctx_hbm = scratch("ctx_hbm", [512, S], BF16)
    xnT0_hbm = scratch("xnT0_hbm", [D, S], BF16)
    x1_hbm = scratch("x1_hbm", [S, D], F32)
    xn1T_hbm = scratch("xn1T_hbm", [D, S], BF16)
    dt_hbm = scratch("dt_hbm", [D, S], F32)
    dtx_hbm = scratch("dtx_hbm", [D, S], FP16)
    g1_hbm = scratch("g1_hbm", [D, S], FP16)
    y_hbm = scratch("y_hbm", [D, S], FP16)

    top = ExitStack()
    dpool = [Sem(top.enter_context(nc.semaphore(f"DP{i}"))) for i in range(12)]
    SV = Sem(top.enter_context(nc.semaphore("SV")))
    SS = Sem(top.enter_context(nc.semaphore("SS")))
    TN = Sem(top.enter_context(nc.semaphore("TN")))
    SG = Sem(top.enter_context(nc.semaphore("SG")))
    AUTO = {"vector": SV, "scalar": SS, "gpsimd": SG}

    sbt = lambda n, s, d=F32: top.enter_context(nc.sbuf_tensor(n, s, d))
    identf = sbt("identf", [128, 128])
    rotT = sbt("rotTs", [128, 128], BF16)
    tri = sbt("tris", [128, 128], BF16)
    ones_b = sbt("ones_b", [1, TB], BF16)
    ones_f = sbt("ones_f", [1, 128])
    ones_c = sbt("ones_c", [128, 1])
    c1 = sbt("c1", [128, 1])
    cS1 = sbt("cS1", [128, 1])
    qkvb_row = sbt("qkvb_row", [1, 3 * D], BF16)
    projb_c = sbt("projb_c", [128, 8])
    f0b_c = sbt("f0b_c", [128, 8])
    bdt_c = sbt("bdt_c", [128, 8])
    dpar_c = sbt("dpar_c", [128, 8])
    a16 = sbt("a16", [128, 64])
    wBT = sbt("wBT", [128, 64])
    wCT = sbt("wCT", [128, 64])
    wsinv_c = sbt("wsinv_c", [128, 1])
    wsd127 = sbt("wsd127", [1, 1])

    with ExitStack() as ph:
        R = Rec(nc, dpool, AUTO)
        ps_t = ph.enter_context(nc.psum_tensor("ps_t", [128, 512], F32))
        wld2 = [ph.enter_context(nc.sbuf_tensor(f"wld{i}", [128, D], F32))
                for i in range(2)]
        wtmp = ph.enter_context(nc.sbuf_tensor("wtmp", [128, D], BF16))
        wtmp8 = ph.enter_context(nc.sbuf_tensor("wtmp8", [128, D], FP8))
        rowtmp = ph.enter_context(nc.sbuf_tensor("rowtmp", [1, 3 * D], F32))
        colstage = ph.enter_context(nc.sbuf_tensor("colstage", [128, 64], F32))
        wqtmp = ph.enter_context(nc.sbuf_tensor("wqtmp", [128, D], F32))
        redtmp = ph.enter_context(nc.sbuf_tensor("redtmp", [128, 16], F32))
        sc1 = ph.enter_context(nc.sbuf_tensor("sc1", [1, 16], F32))

        for dst, src in ((identf[:], ident_d), (rotT[:], rot_d), (tri[:], tri_d),
                         (rowtmp[:], qkv_b), (projb_c[:], proj_b),
                         (f0b_c[:], f0_b), (bdt_c[:], b_dt), (dpar_c[:], d_par),
                         (colstage[:], a_log)):
            R.dma(dst, src)
        R.wait_all_dma("vector")
        R.wait_all_dma("scalar")
        for t_, v_ in ((ones_b, 1.0), (ones_f, 1.0), (ones_c, 1.0), (c1, 1.0), (cS1, -0.0505 / 0.099)):
            R.op("vector", "memset", t_[:], v_)
        R.op("vector", "tensor_copy", qkvb_row[:], rowtmp[:], inc=(SV, 1))
        R.op("scalar", "activation", colstage[:], colstage[:], AF.Tanh, scale=0.1,
             inc=(SS, 1))
        R.op("scalar", "activation", colstage[:], colstage[:], AF.Exp, scale=10.0,
             inc=(SS, 1))
        R.wait("vector", SS)
        R.op("vector", "tensor_scalar", a16[:], colstage[:], -1.0 / 16.0, None,
             ALU.mult, inc=(SV, 1))

        wtoks = {}
        wtrn = {0: 0, 1: 0}
        for wi, (src, dst) in enumerate(((w_B, wBT), (w_C, wCT))):
            R.wait("sync", TN)
            wld = wld2[wi % 2]
            tok = R.dma(wld[0:8, :], src)
            for dc in range(8):
                R.wait_tok("tensor", tok)
                R.wait("tensor", SV)
                R.op("tensor", "transpose", ps_t[:, 0:8], wld[0:8, bass.ts(dc, 128)],
                     identf[0:8, 0:8], inc=(TN, 1))
                R.wait("vector", TN)
                wtrn[wi % 2] = TN.n
                R.op("vector", "tensor_copy", dst[:, bass.ts(dc, 8)], ps_t[:, 0:8],
                     inc=(SV, 1))

        def stage_weight(src_hbm, n_oc, dst_hbm, dst_cols, cast_tile, oc_off=0,
                         st_tok=[None]):
            def issue(oc):
                src_ap = bass.AP(src_hbm.tensor,
                                 src_hbm.offset + (oc_off + oc * 128) * D,
                                 [[D, 128], [1, D]])
                if wtrn[oc % 2]:
                    R.wait("sync", TN, wtrn[oc % 2])
                wtoks[oc % 2] = R.dma(wld2[oc % 2][:], src_ap)
            issue(0)
            for oc in range(n_oc):
                par = oc % 2
                if oc + 1 < n_oc:
                    issue(oc + 1)
                wld = wld2[par]
                R.wait_tok("tensor", wtoks[par])
                for half in range(2):
                    R.wait("tensor", SV)
                    for q4 in range(4):
                        R.op("tensor", "transpose", ps_t[:, bass.ts(q4, 128)],
                             wld[:, bass.ts(half * 4 + q4, 128)], identf[:],
                             inc=(TN, 1))
                    R.wait("vector", TN)
                    if half == 0:
                        R.wait_tok("vector", st_tok[0])
                    R.op("vector", "tensor_copy",
                         cast_tile[:, half * 512: half * 512 + 512],
                         ps_t[:, 0:512], inc=(SV, 1))
                wtrn[par] = TN.n
                R.wait("sync", SV)
                dst_ap = bass.AP(dst_hbm.tensor, dst_hbm.offset + oc * 128,
                                 [[dst_cols, 128], [dst_cols * 128, 8], [1, 128]])
                src_s = bass.AP(cast_tile[:].tensor, cast_tile[:].offset,
                                [[cast_tile[:].ap[0][0], 128], [128, 8], [1, 128]])
                st_tok[0] = R.dma(dst_ap, src_s)

        stage_weight(qkv_w, 24, wqkvT_h, 3 * D, wtmp)
        stage_weight(proj_w, 8, wprojT_h, D, wtmp)
        stage_weight(w_dt, 8, wdtT_h, D, wtmp)
        stage_weight(f0_w, 8, wf0T_h, D, wtmp8)

        wsv = {0: 0, 1: 0}

        def issue_f1(oc):
            src_ap = bass.AP(f1_w.tensor, f1_w.offset + oc * 128 * D,
                             [[D, 128], [1, D]])
            if wtrn[oc % 2]:
                R.wait("sync", TN, wtrn[oc % 2])
            if wsv[oc % 2]:
                R.wait("sync", SV, wsv[oc % 2])
            wtoks[oc % 2] = R.dma(wld2[oc % 2][:], src_ap)
        issue_f1(0)
        for oc in range(8):
            par = oc % 2
            if oc + 1 < 8:
                issue_f1(oc + 1)
            R.wait_tok("vector", wtoks[par])
            R.op("vector", "reduce_sum", redtmp[:, oc:oc + 1], wld2[par][:],
                 AX.X, apply_absolute_value=True, inc=(SV, 1))
            wsv[par] = SV.n
        R.wait("tensor", SV)
        R.op("tensor", "matmul", ps_t[0:1, 0:8], ones_c[:], redtmp[:, 0:8],
             start=True, stop=True, inc=(TN, 1))
        R.wait("vector", TN)
        R.op("vector", "reduce_sum", sc1[:, 0:1], ps_t[0:1, 0:8], AX.X, inc=(SV, 1))
        R.op("vector", "tensor_scalar", sc1[:, 1:2], sc1[:, 0:1],
             1.0 / (D * D), 1e-8, ALU.mult, ALU.add, inc=(SV, 1))
        R.op("vector", "reciprocal", sc1[:, 2:3], sc1[:, 1:2], inc=(SV, 1))
        R.op("vector", "tensor_scalar", wsd127[:], sc1[:, 1:2], 1.0 / 127.0, None,
             ALU.mult, inc=(SV, 1))
        R.wait("tensor", SV)
        R.op("tensor", "matmul", ps_t[:, 0:1], ones_f[:], sc1[:, 2:3],
             start=True, stop=True, inc=(TN, 1))
        R.wait("vector", TN)
        R.op("vector", "tensor_copy", wsinv_c[:], ps_t[:, 0:1], inc=(SV, 1))
        wq_st = None
        issue_f1(0)
        for oc in range(8):
            par = oc % 2
            if oc + 1 < 8:
                issue_f1(oc + 1)
            wld = wld2[par]
            R.wait_tok("vector", wtoks[par])
            R.op("vector", "tensor_scalar", wqtmp[:], wld[:], wsinv_c[:, 0:1], None,
                 ALU.mult, inc=(SV, 1))
            R.op("vector", "tensor_scalar", wld[:], wqtmp[:], 0.5, None, ALU.is_ge,
                 inc=(SV, 1))
            R.op("vector", "tensor_scalar", wqtmp[:], wqtmp[:], -0.5, None,
                 ALU.is_le, inc=(SV, 1))
            R.op("vector", "tensor_sub", wld[:], wld[:], wqtmp[:], inc=(SV, 1))
            for half in range(2):
                R.wait("tensor", SV)
                for q4 in range(4):
                    R.op("tensor", "transpose", ps_t[:, bass.ts(q4, 128)],
                         wld[:, bass.ts(half * 4 + q4, 128)], identf[:],
                         inc=(TN, 1))
                R.wait("vector", TN)
                if half == 0:
                    R.wait_tok("vector", wq_st)
                R.op("vector", "tensor_copy",
                     wtmp[:, half * 512: half * 512 + 512],
                     ps_t[:, 0:512], inc=(SV, 1))
            wtrn[par] = TN.n
            wsv[par] = SV.n
            R.wait("sync", SV)
            dst_ap = bass.AP(wq1T_h.tensor, wq1T_h.offset + oc * 128,
                             [[D, 128], [D * 128, 8], [1, 128]])
            src_s = bass.AP(wtmp[:].tensor, wtmp[:].offset,
                            [[wtmp[:].ap[0][0], 128], [128, 8], [1, 128]])
            wq_st = R.dma(dst_ap, src_s)
        R.wait_all_dma("sync")
        R.play()
    if stop_after == "p0":
        return nc

    l0 = ExitStack()
    s0 = lambda n, s, d=F32: l0.enter_context(nc.sbuf_tensor(n, s, d))
    cos_blk2 = [s0(f"cos_blk{i}", [128, TB], BF16) for i in range(2)]
    sin_blk2 = [s0(f"sin_blk{i}", [128, TB], BF16) for i in range(2)]
    kv_res = s0("kv_res", [128, 4 * S], BF16)
    v_res = s0("v_res", [128, 8 * 32 * 65], BF16)
    xblk2 = [s0(f"xblk{i}", [128, 4 * D]) for i in range(2)]
    xnT2 = [s0(f"xnT2_{i}", [128, 8 * TB], BF16) for i in range(2)]

    def emit_ln4(R, src_tile, dst_tile, sq, s2, st):
        m4, var4, rstd4 = st["m"], st["var"], st["rstd"]
        y24, u4, w4 = st["y2"], st["u"], st["w"]
        for tc in range(4):
            R.op("vector", "reduce_sum", m4[:, tc:tc + 1],
                 src_tile[:, bass.ts(tc, D)], AX.X)
            R.wait("scalar", SV)
            R.op("scalar", "activation", sq[:], src_tile[:, bass.ts(tc, D)],
                 AF.Square, accum_out=s2[:, tc:tc + 1], inc=(SS, 1))
        R.wait("vector", SS)
        R.op("vector", "tensor_scalar", m4[:], m4[:], 1.0 / D, None, ALU.mult)
        R.op("vector", "tensor_scalar", var4[:], s2[:], 1.0 / D, None, ALU.mult)
        R.op("vector", "tensor_mul", y24[:], m4[:], m4[:])
        R.op("vector", "tensor_sub", var4[:], var4[:], y24[:])
        R.op("vector", "tensor_scalar", var4[:], var4[:], 1.0, 1e-5, ALU.mult,
             ALU.add)
        R.op("vector", "reciprocal", rstd4[:], var4[:])
        for _ in range(4):
            R.op("vector", "tensor_mul", y24[:], rstd4[:], rstd4[:])
            R.op("vector", "tensor_mul", u4[:], var4[:], y24[:])
            R.op("vector", "tensor_scalar", w4[:], u4[:], -0.5, 1.5, ALU.mult,
                 ALU.add)
            R.op("vector", "tensor_mul", rstd4[:], rstd4[:], w4[:])
        for tc in range(4):
            R.op("vector", "tensor_scalar", dst_tile[:, bass.ts(tc, D)],
                 src_tile[:, bass.ts(tc, D)], m4[:, tc:tc + 1],
                 rstd4[:, tc:tc + 1], ALU.subtract, ALU.mult, inc=(SV, 1))

    xtoks = {}
    xnt_toks = {}

    def emit_xload(R, blkidx, par, want_x=True, want_xnt=False):
        t0_ = blkidx * TB
        toks = []
        if want_x:
            src = bass.AP(x_in.tensor, x_in.offset + blkidx * TB * D,
                          [[D, 128], [128 * D, 4], [1, D]])
            xb = xblk2[par]
            xdst = bass.AP(xb[:].tensor, xb[:].offset,
                           [xb[:].ap[0], [D, 4], [1, D]])
            toks.append(R.dma(xdst, src))
        cs = bass.AP(cos_d.tensor, cos_d.offset + t0_, [[S, 128], [1, TB]])
        sn = bass.AP(sin_d.tensor, sin_d.offset + t0_, [[S, 128], [1, TB]])
        toks.append(R.dma(cos_blk2[par][:], cs))
        toks.append(R.dma(sin_blk2[par][:], sn))
        xtoks[par] = tuple(toks)
        if want_xnt:
            src = bass.AP(xnT0_hbm.tensor, xnT0_hbm.offset + t0_,
                          [[S, 128], [S * 128, 8], [1, TB]])
            dst = bass.AP(xnT2[par][:].tensor, xnT2[par][:].offset,
                          [xnT2[par][:].ap[0], [TB, 8], [1, TB]])
            xnt_toks[par] = R.dma(dst, src)

    with ExitStack() as ph:
        R = Rec(nc, dpool, AUTO)
        wkvT = ph.enter_context(nc.sbuf_tensor("wkvT", [128, 8 * 2048], BF16))
        wf0T = ph.enter_context(nc.sbuf_tensor("wf0T", [128, 8 * D], FP8))
        xnT_f8 = ph.enter_context(nc.sbuf_tensor("xnT_f8", [128, 8 * TB], FP8))
        k_raw = ph.enter_context(nc.sbuf_tensor("k_raw", [128, TB], BF16))
        k_t1 = ph.enter_context(nc.sbuf_tensor("k_t1", [128, TB], BF16))
        k_t2 = ph.enter_context(nc.sbuf_tensor("k_t2", [128, TB], BF16))
        kstage = ph.enter_context(nc.sbuf_tensor("kstage", [128, TB], BF16))
        vstage = ph.enter_context(nc.sbuf_tensor("vstage", [128, TB], BF16))
        gstage = ph.enter_context(nc.sbuf_tensor("gstage", [128, TB], FP16))
        xnblk = ph.enter_context(nc.sbuf_tensor("xnblk", [128, 4 * D], F32))
        sqdump = ph.enter_context(nc.sbuf_tensor("sqdump", [128, D], F32))
        s2t = ph.enter_context(nc.sbuf_tensor("s2t", [128, 4], F32))
        stat = {k: ph.enter_context(nc.sbuf_tensor("st_" + k, [128, 4], F32))
                for k in ("m", "var", "rstd", "y2", "u", "w")}
        ps_a = ph.enter_context(nc.psum_tensor("p1_a", [128, 512], F32))
        ps_k = [ph.enter_context(nc.psum_tensor(f"p1_k{i}", [128, 512], F32))
                for i in range(2)]
        ps_r = ph.enter_context(nc.psum_tensor("p1_r", [128, 512], F32))
        ps_v = [ph.enter_context(nc.psum_tensor(f"p1_v{i}", [128, 512], F32))
                for i in range(2)]
        ps_g = [ph.enter_context(nc.psum_tensor(f"p1_g{i}", [128, 512], F32))
                for i in range(2)]

        if "ones" not in SKIP:
            ones_ap = bass.AP(v_res[:].tensor, v_res[:].offset + 64,
                              [v_res[:].ap[0], [65, 8 * 32]])
            R.op("vector", "memset", ones_ap, 1.0, inc=(SV, 1))
        wkv_toks = []
        for ic in range(8):
            wsrc = bass.AP(wqkvT_h.tensor, wqkvT_h.offset + ic * 3 * D * 128 + 1024,
                           [[3 * D, 128], [1, 2048]])
            wkv_toks.append(R.dma(wkvT[:, ic * 2048:(ic + 1) * 2048], wsrc))
        wf0_toks = []
        for ic in range(8):
            w8src = bass.AP(wf0T_h.tensor, wf0T_h.offset + ic * D * 128,
                            [[D, 128], [1, D]])
            wf0_toks.append(R.dma(wf0T[:, ic * D:(ic + 1) * D], w8src))
        st_k = None
        st_v = None
        st_g = None
        st_xnt = [None, None]
        emit_xload(R, 0, 0)
        for b in range(NB):
            t0 = b * TB
            par = b % 2
            cos_blk, sin_blk = cos_blk2[par], sin_blk2[par]
            xnT_bf = xnT2[par]
            xblk = xblk2[par]
            for tk in xtoks[par]:
                R.wait_tok("vector", tk)
            R.wait_tok("scalar", xtoks[par][0])
            emit_ln4(R, xblk[:], xnblk[:], sqdump, s2t, stat)
            R.wait_tok("vector", st_xnt[par])
            p1tr = {0: 0, 1: 0}
            R.wait("tensor", SV)
            R.wait("tensor", SS)
            for ic in range(8):
                pst = (ps_a, ps_r)[ic % 2]
                if p1tr[ic % 2]:
                    R.wait("tensor", SV, p1tr[ic % 2])
                for tc in range(4):
                    R.op("tensor", "transpose", pst[:, bass.ts(tc, 128)],
                         xnblk[:, tc * D + ic * 128: tc * D + ic * 128 + 128],
                         identf[:], inc=(TN, 1))
                R.wait("vector", TN)
                R.op("vector", "tensor_copy", xnT_bf[:, bass.ts(ic, TB)],
                     pst[:, 0:512], inc=(SV, 1))
                R.op("vector", "tensor_copy", xnT_f8[:, bass.ts(ic, TB)],
                     pst[:, 0:512], inc=(SV, 1))
                p1tr[ic % 2] = SV.n
            R.wait("sync", SV)
            xts = bass.AP(xnT_bf[:].tensor, xnT_bf[:].offset,
                          [xnT_bf[:].ap[0], [TB, 8], [1, TB]])
            xtd = bass.AP(xnT0_hbm.tensor, xnT0_hbm.offset + t0,
                          [[S, 128], [S * 128, 8], [1, TB]])
            st_xnt[par] = R.dma(xtd, xts)
            if b + 1 < NB:
                R.wait("sync", SV)
                emit_xload(R, b + 1, 1 - par)
            def emit_k(kc):
                nonlocal st_k
                pk = ps_k[kc % 2]
                R.wait("tensor", SS)
                R.wait("tensor", SV)
                for ic in range(8):
                    if b == 0:
                        R.wait_tok("tensor", wkv_toks[ic])
                    R.op("tensor", "matmul", pk[:],
                         wkvT[:, ic * 2048 + kc * 128: ic * 2048 + kc * 128 + 128],
                         xnT_bf[:, bass.ts(ic, TB)],
                         start=(ic == 0), stop=False, inc=(TN, 1))
                R.op("tensor", "matmul", pk[:],
                     qkvb_row[0:1, 1024 + kc * 128: 1024 + kc * 128 + 128],
                     ones_b[:], start=False, stop=True, inc=(TN, 1))
                R.wait("scalar", TN)
                R.wait("scalar", SV)
                R.op("scalar", "copy", k_raw[:], pk[:], inc=(SS, 1))
                R.wait("tensor", SS)
                R.wait("tensor", SV)
                R.op("tensor", "matmul", ps_r[:], rotT[:], k_raw[:],
                     start=True, stop=True, inc=(TN, 1))
                R.wait("vector", SS)
                R.op("vector", "tensor_mul", k_t1[:], k_raw[:],
                     cos_blk[:], inc=(SV, 1))
                R.wait("vector", TN)
                R.op("vector", "tensor_mul", k_t2[:], ps_r[:],
                     sin_blk[:], inc=(SV, 1))
                if kc < 4:
                    R.op("vector", "tensor_add",
                         kv_res[:, kc * S + t0: kc * S + t0 + TB],
                         k_t1[:], k_t2[:], inc=(SV, 1))
                else:
                    R.wait_tok("vector", st_k)
                    R.op("vector", "tensor_add", kstage[:], k_t1[:], k_t2[:],
                         inc=(SV, 1))
                    R.wait("sync", SV)
                    dst = bass.AP(k_hbm.tensor,
                                  k_hbm.offset + (kc - 4) * 128 * S + t0,
                                  [[S, 128], [1, TB]])
                    st_k = R.dma(dst, kstage[:])

            def emit_v(vi):
                nonlocal st_v
                tc, dvc = vi // 2, vi % 2
                pv = ps_v[dvc]
                R.wait("tensor", SV)
                for ic in range(8):
                    if b == 0:
                        R.wait_tok("tensor", wkv_toks[ic])
                    R.op("tensor", "matmul", pv[:],
                         xnT_bf[:, ic * TB + tc * 128: ic * TB + tc * 128 + 128],
                         wkvT[:, ic * 2048 + 1024 + dvc * 512:
                              ic * 2048 + 1024 + dvc * 512 + 512],
                         start=(ic == 0), stop=False, inc=(TN, 1))
                R.op("tensor", "matmul", pv[:], ones_b[0:1, 0:128],
                     qkvb_row[0:1, 2048 + dvc * 512: 2048 + dvc * 512 + 512],
                     start=False, stop=True, inc=(TN, 1))
                R.wait("vector", TN)
                kcg = b * 4 + tc
                if dvc == 0:
                    dsts = bass.AP(v_res[:].tensor, v_res[:].offset + kcg * 65,
                                   [v_res[:].ap[0], [32 * 65, 8], [1, 64]])
                    srcs = bass.AP(pv[:].tensor, pv[:].offset,
                                   [pv[:].ap[0], [64, 8], [1, 64]])
                    R.op("vector", "tensor_copy", dsts, srcs, inc=(SV, 1))
                else:
                    R.wait_tok("vector", st_v)
                    R.op("vector", "tensor_copy", vstage[:], pv[:], inc=(SV, 1))
                    R.wait("sync", SV)
                    dst = bass.AP(v_hbm.tensor, v_hbm.offset + kcg * 65,
                                  [[8 * 32 * 65, 128], [32 * 65, 8], [1, 64]])
                    vsrc = bass.AP(vstage[:].tensor, vstage[:].offset,
                                   [vstage[:].ap[0], [64, 8], [1, 64]])
                    st_v = R.dma(dst, vsrc)

            def emit_f(fc):
                nonlocal st_g
                pg = ps_g[fc % 2]
                R.wait("tensor", SS)
                for ic in range(8):
                    if b == 0:
                        R.wait_tok("tensor", wf0_toks[ic])
                    R.op("tensor", "matmul", pg[:],
                         wf0T[:, ic * D + fc * 128: ic * D + fc * 128 + 128],
                         xnT_f8[:, bass.ts(ic, TB)],
                         start=(ic == 0), stop=(ic == 7), inc=(TN, 1))
                R.wait("scalar", TN)
                R.wait_tok("scalar", st_g)
                R.op("scalar", "activation", gstage[:], pg[:], AF.Silu,
                     bias=f0b_c[:, fc:fc + 1], inc=(SS, 1))
                R.wait("sync", SS)
                dst = bass.AP(g_hbm.tensor, g_hbm.offset + fc * 128 * S + t0,
                              [[S, 128], [1, TB]])
                st_g = R.dma(dst, gstage[:])

            for i in range(8):
                if 'k' not in SKIP:
                    emit_k(i)
                if 'v' not in SKIP:
                    emit_v(i)
                if 'ffn0' not in SKIP:
                    emit_f(i)
        R.wait_all_dma("sync")
        R.play()

    def attn_phase(group):
        G = f"g{group}_"
        with ExitStack() as ph:
            R = Rec(nc, dpool, AUTO)
            wqT = ph.enter_context(nc.sbuf_tensor(G + "wqT", [128, 8 * 512], BF16))
            q_res2 = [ph.enter_context(nc.sbuf_tensor(G + f"q_res{i}", [128, 4 * TB], BF16))
                      for i in range(2)]
            q_raw = ph.enter_context(nc.sbuf_tensor(G + "q_raw", [128, TB], BF16))
            q_t1 = ph.enter_context(nc.sbuf_tensor(G + "q_t1", [128, TB], BF16))
            pTAB = ph.enter_context(nc.sbuf_tensor(G + "pTAB", [128, 4096], BF16))
            zrow = ph.enter_context(nc.sbuf_tensor(G + "zrow", [1, 512], F32))
            zrow2 = ph.enter_context(nc.sbuf_tensor(G + "zrow2", [1, 512], F32))
            zbf = ph.enter_context(nc.sbuf_tensor(G + "zbf", [128, 512], F32))
            ctxst = ph.enter_context(nc.sbuf_tensor(G + "ctxst", [64, 512], BF16))
            ctxst2 = ph.enter_context(nc.sbuf_tensor(G + "ctxst2", [64, 512], BF16))
            ps_AB = ph.enter_context(nc.psum_tensor(G + "p2_AB", [128, 2048], F32))
            ps_C = ph.enter_context(nc.psum_tensor(G + "p2_C", [128, 512], F32))
            ps_cA = ph.enter_context(nc.psum_tensor(G + "p2_cA", [65, 512], F32))
            ps_cB = ph.enter_context(nc.psum_tensor(G + "p2_cB", [65, 512], F32))
            ps_q = ph.enter_context(nc.psum_tensor(G + "p2_q", [128, 512], F32))
            if group == 1:
                wpT = ph.enter_context(nc.sbuf_tensor(G + "wpT", [128, 8 * D], BF16))
                ctxa_ld = ph.enter_context(nc.sbuf_tensor(G + "ctxa_ld", [128, 2048], BF16))
                ctxb_res = ph.enter_context(nc.sbuf_tensor(G + "ctxb_res", [128, 2048], BF16))
                g_fc = ph.enter_context(nc.sbuf_tensor(G + "g_fc", [128, 512], FP16))
                sg_fc = ph.enter_context(nc.sbuf_tensor(G + "sg_fc", [128, 512], F32))

            wsrc = bass.AP(wqkvT_h.tensor, wqkvT_h.offset + group * 512,
                           [[3 * D, 128], [3 * D * 128, 8], [1, 512]])
            wdstq = bass.AP(wqT[:].tensor, wqT[:].offset,
                            [wqT[:].ap[0], [512, 8], [1, 512]])
            wq_tok = R.dma(wdstq, wsrc)
            k_toks = [None] * 4
            v_toks = [None] * 4
            wp_tok = None
            if group == 1:
                for hp in range(4):
                    src = bass.AP(k_hbm.tensor, k_hbm.offset + hp * 128 * S,
                                  [[S, 128], [1, S]])
                    k_toks[hp] = R.dma(kv_res[:, hp * S:(hp + 1) * S], src)
                    vsw = bass.AP(v_hbm.tensor,
                                  v_hbm.offset + hp * 2 * 32 * 65,
                                  [[8 * 32 * 65, 128], [1, 2 * 32 * 65]])
                    v_toks[hp] = R.dma(
                        v_res[:, hp * 2 * 32 * 65:(hp + 1) * 2 * 32 * 65], vsw)
                wps = bass.AP(wprojT_h.tensor, wprojT_h.offset,
                              [[D, 128], [D * 128, 8], [1, D]])
                wpdst = bass.AP(wpT[:].tensor, wpT[:].offset,
                                [wpT[:].ap[0], [D, 8], [1, D]])
                wp_tok = R.dma(wpdst, wps)
                ones_ap2 = bass.AP(v_res[:].tensor, v_res[:].offset + 64,
                                   [v_res[:].ap[0], [65, 8 * 32]])
                for hp in range(4):
                    R.wait_tok("vector", v_toks[hp])
                R.op("vector", "memset", ones_ap2, 1.0, inc=(SV, 1))
                ones_sv = SV.n
            R.wait_tok("tensor", wq_tok)
            st_ctx_h = [None, None]
            st_x1 = None
            st_out = None
            slot_ss = [0, 0]
            ctx_sv = [0]
            q_ss = [0]
            q_sv = [0, 0]
            q_psv = [0]
            gate_sv = [0]
            pend_nt = [None]
            znt_tn = [0]

            def norm_tail():
                if pend_nt[0] is None:
                    return
                hA_, t0_, rsv = pend_nt[0]
                pend_nt[0] = None
                R.wait("tensor", SV, rsv)
                R.op("tensor", "matmul", ps_C[0:64, :], ones_f[0:1, 0:64],
                     zrow[:], start=True, stop=True, inc=(TN, 1))
                R.op("tensor", "matmul", ps_C[64:128, :], ones_f[0:1, 0:64],
                     zrow2[:], start=True, stop=True, inc=(TN, 1))
                znt_tn[0] = TN.n
                R.wait("vector", TN, znt_tn[0])
                R.op("vector", "tensor_copy", zbf[:], ps_C[:], inc=(SV, 1))
                if group == 0:
                    R.wait_tok("vector", st_ctx_h[0])
                    R.op("vector", "tensor_mul", ctxst[:], ps_cA[0:64, :],
                         zbf[0:64, :], inc=(SV, 1))
                    R.wait_tok("vector", st_ctx_h[1])
                    R.op("vector", "tensor_mul", ctxst2[:], ps_cB[0:64, :],
                         zbf[64:128, :], inc=(SV, 1))
                    R.wait("sync", SV)
                    dstA = bass.AP(ctx_hbm.tensor,
                                   ctx_hbm.offset + (hA_ * 64) * S + t0_,
                                   [[S, 64], [1, TB]])
                    st_ctx_h[0] = R.dma(dstA, ctxst[:])
                    dstB = bass.AP(ctx_hbm.tensor,
                                   ctx_hbm.offset + ((hA_ + 1) * 64) * S + t0_,
                                   [[S, 64], [1, TB]])
                    st_ctx_h[1] = R.dma(dstB, ctxst2[:])
                else:
                    R.op("vector", "tensor_mul",
                         ctxb_res[0:64, bass.ts(hA_ // 2, TB)], ps_cA[0:64, :],
                         zbf[0:64, :], inc=(SV, 1))
                    R.op("vector", "tensor_mul",
                         ctxb_res[64:128, bass.ts(hA_ // 2, TB)], ps_cB[0:64, :],
                         zbf[64:128, :], inc=(SV, 1))
                ctx_sv[0] = SV.n

            def emit_qblock(b_, par_):
                q_res = q_res2[par_]
                cosb, sinb = cos_blk2[par_], sin_blk2[par_]
                xnt = xnT2[par_]
                R.wait_tok("tensor", xnt_toks[par_])
                for tk in xtoks[par_]:
                    R.wait_tok("vector", tk)
                for qc in range(4):
                    if q_ss[0]:
                        R.wait("tensor", SS, q_ss[0])
                    if q_psv[0]:
                        R.wait("tensor", SV, q_psv[0])
                    for ic in range(8):
                        R.op("tensor", "matmul", ps_q[:],
                             wqT[:, ic * 512 + qc * 128: ic * 512 + qc * 128 + 128],
                             xnt[:, bass.ts(ic, TB)],
                             start=(ic == 0), stop=False, inc=(TN, 1))
                    R.op("tensor", "matmul", ps_q[:],
                         qkvb_row[0:1, group * 512 + qc * 128:
                                  group * 512 + qc * 128 + 128],
                         ones_b[:], start=False, stop=True, inc=(TN, 1))
                    R.wait("scalar", TN)
                    R.wait("scalar", SV)
                    R.op("scalar", "copy", q_raw[:], ps_q[:], inc=(SS, 1))
                    q_ss[0] = SS.n
                    R.wait("tensor", SS, q_ss[0])
                    R.op("tensor", "matmul", ps_q[:], rotT[:], q_raw[:],
                         start=True, stop=True, inc=(TN, 1))
                    rot_tn = TN.n
                    R.wait("vector", SS, q_ss[0])
                    R.op("vector", "tensor_mul", q_t1[:], q_raw[:],
                         cosb[:], inc=(SV, 1))
                    R.wait("vector", TN, rot_tn)
                    R.op("vector", "tensor_mul", q_res[:, bass.ts(qc, TB)],
                         ps_q[:], sinb[:], inc=(SV, 1))
                    q_psv[0] = SV.n
                    R.op("vector", "tensor_add", q_res[:, bass.ts(qc, TB)],
                         q_res[:, bass.ts(qc, TB)], q_t1[:], inc=(SV, 1))
                q_sv[par_] = SV.n
            emit_xload(R, 0, 0, want_x=(group == 1), want_xnt=True)
            for b in range(NB):
                t0 = b * TB
                par = b % 2
                cos_blk, sin_blk = cos_blk2[par], sin_blk2[par]
                xblk = xblk2[par]
                xnT_bf = xnT2[par]
                for tk in xtoks[par]:
                    R.wait_tok("vector", tk)
                R.wait_tok("tensor", xnt_toks[par])
                if b + 1 < NB:
                    R.wait("sync", SV)
                    R.wait("sync", TN)
                    if group == 1:
                        R.wait_tok("sync", st_x1)
                        R.wait_tok("sync", st_out)
                    emit_xload(R, b + 1, 1 - par, want_x=(group == 1),
                               want_xnt=True)
                if b == 0:
                    emit_qblock(0, 0)
                if group == 1:
                    ctx_toks = []
                    R.wait("sync", TN)
                    for ic in range(4):
                        src = bass.AP(ctx_hbm.tensor,
                                      ctx_hbm.offset + ic * 128 * S + t0,
                                      [[S, 128], [1, TB]])
                        ctx_toks.append(R.dma(ctxa_ld[:, bass.ts(ic, TB)], src))
                q_res = q_res2[par]
                R.wait("tensor", SV, q_sv[par])
                if gate_sv[0]:
                    R.wait("tensor", SV, gate_sv[0])
                for hp in range(4):
                    hA = 2 * hp
                    n_kc = b + 1
                    qsl = slice(hp * TB, hp * TB + TB)
                    if group == 1 and b == 0:
                        R.wait_tok("tensor", k_toks[hp])
                        R.wait_tok("tensor", v_toks[hp])
                        R.wait("tensor", SV, ones_sv)
                    exp_ss = [0] * 4
                    mask_sv = [0] * 4

                    def emit_av(kc, kk, diag):
                        R.wait("tensor", SS, exp_ss[kk])
                        if diag:
                            R.wait("tensor", SV, mask_sv[kk])
                        qs = kk * 128 if diag else 0
                        kcg = kc * 4 + kk
                        first = (kc == 0 and kk == 0)
                        last = (diag and kk == 3)
                        if first:
                            R.wait("tensor", SV, ctx_sv[0])
                        vA = bass.AP(v_res[:].tensor,
                                     v_res[:].offset + ((hA) * 32 + kcg) * 65,
                                     [v_res[:].ap[0], [1, 65]])
                        vB = bass.AP(v_res[:].tensor,
                                     v_res[:].offset + ((hA + 1) * 32 + kcg) * 65,
                                     [v_res[:].ap[0], [1, 65]])
                        R.op("tensor", "matmul", ps_cA[:, qs:512], vA,
                             pTAB[:, kk * 1024 + qs: kk * 1024 + 512],
                             start=first, stop=last, inc=(TN, 1),
                             skip_group_check=True)
                        R.op("tensor", "matmul", ps_cB[:, qs:512], vB,
                             pTAB[:, kk * 1024 + 512 + qs: kk * 1024 + 1024],
                             start=first, stop=last, inc=(TN, 1),
                             skip_group_check=True)

                    pend = []
                    for kc in range(n_kc):
                        diag = (kc == b)
                        for kk in range(4):
                            kcol = kc * 512 + kk * 128
                            sl = kk % 2
                            if slot_ss[sl]:
                                R.wait("tensor", SS, slot_ss[sl])
                            R.op("tensor", "matmul",
                                 ps_AB[:, sl * 1024: sl * 1024 + 512],
                                 kv_res[0:64, hp * S + kcol: hp * S + kcol + 128],
                                 q_res[0:64, qsl], start=True, stop=True,
                                 tile_position=(0, 0), inc=(TN, 1))
                            R.op("tensor", "matmul",
                                 ps_AB[:, sl * 1024 + 512: sl * 1024 + 1024],
                                 kv_res[64:128, hp * S + kcol: hp * S + kcol + 128],
                                 q_res[64:128, qsl], start=True, stop=True,
                                 tile_position=(64, 0), inc=(TN, 1))
                            R.wait("scalar", TN)
                            R.op("scalar", "activation",
                                 pTAB[:, kk * 1024: kk * 1024 + 1024],
                                 ps_AB[:, sl * 1024: sl * 1024 + 1024],
                                 AF.Exp, scale=0.125, inc=(SS, 1))
                            exp_ss[kk] = SS.n
                            slot_ss[sl] = SS.n
                            if diag:
                                R.wait("vector", SS, exp_ss[kk])
                                for half in range(2):
                                    col = kk * 1024 + half * 512 + kk * 128
                                    R.op("vector", "tensor_mul",
                                         pTAB[:, col:col + 128],
                                         pTAB[:, col:col + 128], tri[:],
                                         inc=(SV, 1))
                                mask_sv[kk] = SV.n
                            pend.append((kc, kk, diag))
                            if kc == 0 and kk == 1:
                                norm_tail()
                            if len(pend) > 2:
                                emit_av(*pend.pop(0))
                    for args_ in pend:
                        emit_av(*args_)
                    pend = []
                    if hp == 0 and b + 1 < NB:
                        emit_qblock(b + 1, 1 - par)
                    R.wait("vector", TN)
                    if znt_tn[0]:
                        R.wait("vector", TN, znt_tn[0])
                    R.op("vector", "reciprocal", zrow[:], ps_cA[64:65, :],
                         inc=(SV, 1))
                    R.op("vector", "reciprocal", zrow2[:], ps_cB[64:65, :],
                         inc=(SV, 1))
                    pend_nt[0] = (hA, t0, SV.n)
                norm_tail()
                if group == 1:
                    for dp in range(8):
                        if dp == 0:
                            R.wait_tok("tensor", wp_tok)
                            for tk in ctx_toks:
                                R.wait_tok("tensor", tk)
                        R.wait("tensor", SV)
                        R.wait("tensor", SS)
                        for hd in range(8):
                            lhs = wpT[:, hd * D + dp * 128: hd * D + dp * 128 + 128]
                            rhs = (ctxa_ld[:, bass.ts(hd, TB)] if hd < 4
                                   else ctxb_res[:, bass.ts(hd - 4, TB)])
                            R.op("tensor", "matmul", ps_C[:], lhs, rhs,
                                 start=(hd == 0), stop=(hd == 7), inc=(TN, 1))
                        src = bass.AP(g_hbm.tensor, g_hbm.offset + dp * 128 * S + t0,
                                      [[S, 128], [1, TB]])
                        R.wait("sync", SV)
                        tok_g = R.dma(g_fc[:], src)
                        R.wait_tok("vector", tok_g)
                        R.wait("vector", TN)
                        R.op("vector", "scalar_tensor_tensor", sg_fc[:], ps_C[:],
                             projb_c[:, dp:dp + 1], g_fc[:], ALU.add, ALU.mult,
                             inc=(SV, 1))
                        R.wait("tensor", SV)
                        R.wait("tensor", SS)
                        for tc in range(4):
                            R.op("tensor", "transpose",
                                 ps_AB[:, bass.ts(tc, 128)],
                                 sg_fc[:, bass.ts(tc, 128)], identf[:],
                                 inc=(TN, 1))
                        R.wait("vector", TN)
                        o_d = bass.AP(xblk[:].tensor, xblk[:].offset + dp * 128,
                                      [xblk[:].ap[0], [D, 4], [1, 128]])
                        p_i = bass.AP(ps_AB[:].tensor, ps_AB[:].offset,
                                      [ps_AB[:].ap[0], [128, 4], [1, 128]])
                        R.op("vector", "tensor_add", o_d, p_i, o_d, inc=(SV, 1))
                    gate_sv[0] = SV.n
                    R.wait("sync", SV)
                    dst = bass.AP(x1_hbm.tensor, x1_hbm.offset + t0 * D,
                                  [[D, 128], [128 * D, 4], [1, D]])
                    x1src = bass.AP(xblk[:].tensor, xblk[:].offset,
                                    [xblk[:].ap[0], [D, 4], [1, D]])
                    st_x1 = R.dma(dst, x1src)
                    if stop_after == "l0":
                        dst2 = bass.AP(out.tensor, out.offset + t0 * D,
                                       [[D, 128], [128 * D, 4], [1, D]])
                        st_out = R.dma(dst2, x1src)
            R.wait_all_dma("sync")
            R.play()

    if stop_after == "p1":
        l0.close()
        return nc
    attn_phase(0)
    if stop_after == "p2a":
        l0.close()
        return nc
    attn_phase(1)
    l0.close()
    nc.all_engine_barrier()
    if stop_after == "l0":
        return nc

    bp_hbm = scratch("bp_hbm", [NST, S], F32)
    cp_hbm = scratch("cp_hbm", [NST, S], F32)

    l1 = ExitStack()

    with ExitStack() as ph:
        R = Rec(nc, dpool, AUTO)
        xb2 = [ph.enter_context(nc.sbuf_tensor(f"xblk1{i}", [128, 4 * D], F32))
               for i in range(2)]
        xnblk1 = ph.enter_context(nc.sbuf_tensor("xnblk1", [128, 4 * D], F32))
        sqdump1 = ph.enter_context(nc.sbuf_tensor("sqdump1", [128, D], F32))
        s2t1 = ph.enter_context(nc.sbuf_tensor("s2t1", [128, 4], F32))
        stat1 = {k: ph.enter_context(nc.sbuf_tensor("s1_" + k, [128, 4], F32))
                 for k in ("m", "var", "rstd", "y2", "u", "w")}
        wdtT = ph.enter_context(nc.sbuf_tensor("wdtT", [128, 8 * D], BF16))
        wq1T = ph.enter_context(nc.sbuf_tensor("wq1T", [128, 8 * D], BF16))
        xn1T_bf = ph.enter_context(nc.sbuf_tensor("xn1T_bf", [128, 8 * TB], BF16))
        xn1T_f = ph.enter_context(nc.sbuf_tensor("xn1T_f", [128, 8 * TB], F32))
        aqT = ph.enter_context(nc.sbuf_tensor("aqT", [128, 8 * TB], BF16))
        dtblk = ph.enter_context(nc.sbuf_tensor("dtblk", [128, 8 * TB], F32))
        dtxblk = ph.enter_context(nc.sbuf_tensor("dtxblk", [128, 8 * TB], FP16))
        t1t = ph.enter_context(nc.sbuf_tensor("t1t", [128, 8 * TB], F32))
        aqtmp = ph.enter_context(nc.sbuf_tensor("aqtmp", [128, 4 * D], F32))
        rmax4 = ph.enter_context(nc.sbuf_tensor("rmax4", [128, 4], F32))
        arcp = ph.enter_context(nc.sbuf_tensor("arcp", [128, 1], F32))
        sclrow = ph.enter_context(nc.sbuf_tensor("sclrow", [1, TB], F32))
        sclb = ph.enter_context(nc.sbuf_tensor("sclb", [128, TB], F32))
        g1stage = ph.enter_context(nc.sbuf_tensor("g1stage", [128, TB], FP16))
        bpstage = ph.enter_context(nc.sbuf_tensor("bpstage", [8, TB], F32))
        cpstage = ph.enter_context(nc.sbuf_tensor("cpstage", [8, TB], F32))
        ps_tr = ph.enter_context(nc.psum_tensor("l1_tr", [128, 512], F32))
        ps_dt = [ph.enter_context(nc.psum_tensor(f"l1_dt{i}", [128, 512], F32))
                 for i in range(2)]
        ps_bp = ph.enter_context(nc.psum_tensor("l1_bp", [8, 512], F32))
        ps_bn = [ph.enter_context(nc.psum_tensor(f"l1_bn{i}", [128, 512], F32))
                 for i in range(2)]
        ps_msc = ph.enter_context(nc.psum_tensor("l1_msc", [128, 512], F32))

        wdts = bass.AP(wdtT_h.tensor, wdtT_h.offset,
                       [[D, 128], [D * 128, 8], [1, D]])
        wdtd = bass.AP(wdtT[:].tensor, wdtT[:].offset,
                       [wdtT[:].ap[0], [D, 8], [1, D]])
        tok_a = R.dma(wdtd, wdts)
        wqs = bass.AP(wq1T_h.tensor, wq1T_h.offset,
                      [[D, 128], [D * 128, 8], [1, D]])
        wqd = bass.AP(wq1T[:].tensor, wq1T[:].offset,
                      [wq1T[:].ap[0], [D, 8], [1, D]])
        tok_b = R.dma(wqd, wqs)
        R.wait_tok("tensor", tok_a)
        R.wait_tok("tensor", tok_b)
        st_dt = None
        st_dtx = None
        aq_tn = [0]
        bp_sv = [0]
        st_g1 = None
        st_bp = None
        st_cp = None
        st_xt = None

        x1toks = {}

        def xload1(b_, par_):
            src = bass.AP(x1_hbm.tensor, x1_hbm.offset + b_ * TB * D,
                          [[D, 128], [128 * D, 4], [1, D]])
            xb = xb2[par_]
            xdst = bass.AP(xb[:].tensor, xb[:].offset,
                           [xb[:].ap[0], [D, 4], [1, D]])
            x1toks[par_] = R.dma(xdst, src)

        def emit_ln14(R, src_tile, dst_tile):
            m4, var4, rstd4 = stat1["m"], stat1["var"], stat1["rstd"]
            y24, u4, w4 = stat1["y2"], stat1["u"], stat1["w"]
            for tc in range(4):
                R.op("vector", "reduce_sum", m4[:, tc:tc + 1],
                     src_tile[:, bass.ts(tc, D)], AX.X)
                R.wait("scalar", SV)
                R.op("scalar", "activation", sqdump1[:],
                     src_tile[:, bass.ts(tc, D)], AF.Square,
                     accum_out=s2t1[:, tc:tc + 1], inc=(SS, 1))
            R.wait("vector", SS)
            R.op("vector", "tensor_scalar", m4[:], m4[:], 1.0 / D, None, ALU.mult)
            R.op("vector", "tensor_scalar", var4[:], s2t1[:], 1.0 / D, None,
                 ALU.mult)
            R.op("vector", "tensor_mul", y24[:], m4[:], m4[:])
            R.op("vector", "tensor_sub", var4[:], var4[:], y24[:])
            R.op("vector", "tensor_scalar", var4[:], var4[:], 1.0, 1e-5,
                 ALU.mult, ALU.add)
            R.op("vector", "reciprocal", rstd4[:], var4[:])
            for _ in range(4):
                R.op("vector", "tensor_mul", y24[:], rstd4[:], rstd4[:])
                R.op("vector", "tensor_mul", u4[:], var4[:], y24[:])
                R.op("vector", "tensor_scalar", w4[:], u4[:], -0.5, 1.5,
                     ALU.mult, ALU.add)
                R.op("vector", "tensor_mul", rstd4[:], rstd4[:], w4[:])
            for tc in range(4):
                R.op("vector", "tensor_scalar", dst_tile[:, bass.ts(tc, D)],
                     src_tile[:, bass.ts(tc, D)], m4[:, tc:tc + 1],
                     rstd4[:, tc:tc + 1], ALU.subtract, ALU.mult, inc=(SV, 1))

        xload1(0, 0)
        for b in range(NB):
            t0 = b * TB
            par = b % 2
            xblk1 = xb2[par]
            R.wait_tok("vector", x1toks[par])
            R.wait_tok("scalar", x1toks[par])
            if b + 1 < NB:
                R.wait("sync", SV)
                R.wait("sync", SS)
                xload1(b + 1, 1 - par)
            emit_ln14(R, xblk1[:], xnblk1[:])
            R.wait_tok("vector", st_xt)
            xtsv = {0: 0, 1: 0}
            R.wait("tensor", SV)
            for ic in range(8):
                pst = (ps_tr, ps_msc)[ic % 2]
                if xtsv[ic % 2]:
                    R.wait("tensor", SV, xtsv[ic % 2])
                for tc in range(4):
                    R.op("tensor", "transpose", pst[:, bass.ts(tc, 128)],
                         xnblk1[:, tc * D + ic * 128: tc * D + ic * 128 + 128],
                         identf[:], inc=(TN, 1))
                R.wait("vector", TN)
                R.op("vector", "tensor_copy", xn1T_bf[:, bass.ts(ic, TB)],
                     pst[:, 0:512])
                R.op("vector", "tensor_copy", xn1T_f[:, bass.ts(ic, TB)],
                     pst[:, 0:512])
                xtsv[ic % 2] = SV.n
            xnev_sv = SV.n
            R.wait("sync", SV)
            R.wait_tok("sync", st_xt)
            xts = bass.AP(xn1T_bf[:].tensor, xn1T_bf[:].offset,
                          [xn1T_bf[:].ap[0], [TB, 8], [1, TB]])
            xtd = bass.AP(xn1T_hbm.tensor, xn1T_hbm.offset + t0,
                          [[S, 128], [S * 128, 8], [1, TB]])
            st_xt = R.dma(xtd, xts)
            for tc in range(4):
                xc = xnblk1[:, bass.ts(tc, D)]
                R.op("vector", "reduce_max", rmax4[:, tc:tc + 1], xc, AX.X,
                     apply_absolute_value=True, inc=(SV, 1))
            R.op("vector", "tensor_scalar", stat1["u"][:], rmax4[:],
                 1.0, 1e-8, ALU.mult, ALU.add, inc=(SV, 1))
            R.op("vector", "reciprocal", stat1["u"][:], stat1["u"][:],
                 inc=(SV, 1))
            if aq_tn[0]:
                R.wait("vector", TN, aq_tn[0])
            for tc in range(4):
                xc = xnblk1[:, bass.ts(tc, D)]
                R.op("vector", "tensor_scalar", aqtmp[:, bass.ts(tc, D)], xc,
                     stat1["u"][:, tc:tc + 1], 127.0, ALU.mult, ALU.mult,
                     inc=(SV, 1))
            R.op("vector", "tensor_scalar", aqtmp[:], aqtmp[:], MAGIC, MAGIC,
                 ALU.add, ALU.subtract, inc=(SV, 1))
            aqc_sv = SV.n
            for dc in range(8):
                pd = ps_dt[dc % 2]
                R.wait("tensor", SS)
                R.wait("tensor", SV, xnev_sv)
                for ic in range(8):
                    R.op("tensor", "matmul", pd[:],
                         wdtT[:, ic * D + dc * 128: ic * D + dc * 128 + 128],
                         xn1T_bf[:, bass.ts(ic, TB)],
                         start=(ic == 0), stop=(ic == 7), inc=(TN, 1))
                R.wait("scalar", TN)
                if dc == 0:
                    R.wait("scalar", SV)
                R.op("scalar", "activation", t1t[:, bass.ts(dc, TB)], pd[:],
                     AF.Exp, bias=bdt_c[:, dc:dc + 1], inc=(SS, 1))
            R.op("scalar", "activation", t1t[:], t1t[:], AF.Ln,
                 bias=c1[:, 0:1], inc=(SS, 1))
            R.op("scalar", "activation", t1t[:], t1t[:], AF.Tanh,
                 scale=1.0 / 0.099, bias=cS1[:, 0:1], inc=(SS, 1))
            R.wait("vector", SS)
            R.wait_tok("vector", st_dt)
            R.op("vector", "tensor_scalar", dtblk[:], t1t[:], 0.099, 0.0505,
                 ALU.mult, ALU.add, inc=(SV, 1))
            R.wait_tok("vector", st_dtx)
            R.op("vector", "tensor_mul", dtxblk[:], dtblk[:], xn1T_f[:],
                 inc=(SV, 1))
            R.wait("sync", SV)
            dstd = bass.AP(dt_hbm.tensor, dt_hbm.offset + t0,
                           [[S, 128], [128 * S, 8], [1, TB]])
            dsrc = bass.AP(dtblk[:].tensor, dtblk[:].offset,
                           [dtblk[:].ap[0], [TB, 8], [1, TB]])
            st_dt = R.dma(dstd, dsrc)
            dstx = bass.AP(dtx_hbm.tensor, dtx_hbm.offset + t0,
                           [[S, 128], [128 * S, 8], [1, TB]])
            xsrc = bass.AP(dtxblk[:].tensor, dtxblk[:].offset,
                           [dtxblk[:].ap[0], [TB, 8], [1, TB]])
            st_dtx = R.dma(dstx, xsrc)
            trsv = {0: 0, 1: 0}
            R.wait("tensor", SV, aqc_sv)
            for ic in range(8):
                pst = (ps_tr, ps_msc)[ic % 2]
                if trsv[ic % 2]:
                    R.wait("tensor", SV, trsv[ic % 2])
                for tc in range(4):
                    R.op("tensor", "transpose", pst[:, bass.ts(tc, 128)],
                         aqtmp[:, tc * D + ic * 128: tc * D + ic * 128 + 128],
                         identf[:], inc=(TN, 1))
                R.wait("vector", TN)
                dsta = bass.AP(aqT[:].tensor, aqT[:].offset + ic * TB,
                               [aqT[:].ap[0], [128, 4], [1, 128]])
                srca = bass.AP(pst[:].tensor, pst[:].offset,
                               [pst[:].ap[0], [128, 4], [1, 128]])
                R.op("vector", "tensor_copy", dsta, srca, inc=(SV, 1))
                trsv[ic % 2] = SV.n
                aq_tn[0] = TN.n
            R.wait("tensor", SV)
            for tc in range(4):
                R.op("tensor", "transpose", ps_msc[0:1, bass.ts(tc, 128)],
                     rmax4[:, tc:tc + 1], identf[:], inc=(TN, 1))
            R.wait("scalar", TN)
            R.op("scalar", "activation", sclrow[:], ps_msc[0:1, 0:512], AF.Identity,
                 scale=wsd127[0:1, 0:1], inc=(SS, 1))
            R.wait("tensor", SS)
            R.op("tensor", "matmul", ps_msc[:, 0:512], ones_f[:], sclrow[:],
                 start=True, stop=True, inc=(TN, 1))
            R.wait("vector", TN)
            R.op("vector", "tensor_copy", sclb[:], ps_msc[:, 0:512], inc=(SV, 1))
            for wt, stage, hbm, sttok in ((wBT, bpstage, bp_hbm, "bp"),
                                          (wCT, cpstage, cp_hbm, "cp")):
                if bp_sv[0]:
                    R.wait("tensor", SV, bp_sv[0])
                for ic in range(8):
                    R.op("tensor", "matmul", ps_bp[:],
                         wt[:, ic * 8: ic * 8 + 8],
                         xn1T_f[:, bass.ts(ic, TB)],
                         start=(ic == 0), stop=(ic == 7), inc=(TN, 1))
                R.wait("vector", TN)
                R.wait_tok("vector", st_bp if sttok == "bp" else st_cp)
                R.op("vector", "tensor_copy", stage[:], ps_bp[:], inc=(SV, 1))
                bp_sv[0] = SV.n
                R.wait("sync", SV)
                dstb = bass.AP(hbm.tensor, hbm.offset + t0, [[S, 8], [1, TB]])
                if sttok == "bp":
                    st_bp = R.dma(dstb, stage[:])
                else:
                    st_cp = R.dma(dstb, stage[:])
            for fc in range(8):
                pb = ps_bn[fc % 2]
                R.wait("tensor", SV)
                for ic in range(8):
                    R.op("tensor", "matmul", pb[:],
                         wq1T[:, ic * D + fc * 128: ic * D + fc * 128 + 128],
                         aqT[:, bass.ts(ic, TB)],
                         start=(ic == 0), stop=(ic == 7), inc=(TN, 1))
                R.wait("vector", TN)
                R.wait("vector", SS)
                R.wait_tok("vector", st_g1)
                R.op("vector", "tensor_mul", g1stage[:], pb[:], sclb[:],
                     inc=(SV, 1))
                R.wait("scalar", SV)
                R.op("scalar", "activation", g1stage[:], g1stage[:], AF.Silu,
                     inc=(SS, 1))
                R.wait("sync", SS)
                dst = bass.AP(g1_hbm.tensor, g1_hbm.offset + fc * 128 * S + t0,
                              [[S, 128], [1, TB]])
                st_g1 = R.dma(dst, g1stage[:])
        R.wait_all_dma("sync")
        R.play()

    if stop_after == "l1a":
        l1.close()
        return nc

    TMAX = 120
    SSTR = 12
    from collections import deque
    with ExitStack() as ph:
        R = Rec(nc, dpool, AUTO)
        sb = lambda nm, sh, d=F32: ph.enter_context(nc.sbuf_tensor(nm, sh, d))
        dA_all = sb("dA_all", [128, 64 * TMAX], FP16)
        u_all = sb("u_all", [128, 64 * TMAX], FP16)
        Z3 = sb("Z3", [128, 3 * 64 * TMAX], FP16)
        P3 = sb("P3", [128, 3 * 64 * TMAX], FP16)
        dt8 = [sb(f"dt8_{i}", [128, 8 * TMAX]) for i in range(2)]
        dtx8 = [sb(f"dtx8_{i}", [128, 8 * TMAX], FP16) for i in range(2)]
        xn1T8 = [sb(f"xn1T8_{i}", [128, 8 * TMAX], BF16) for i in range(2)]
        bp8 = [sb(f"bp8_{i}", [1, 8 * TMAX]) for i in range(2)]
        cp8 = [sb(f"cp8_{i}", [1, 8 * TMAX]) for i in range(3)]
        y83 = sb("y83", [128, 3 * 8 * TMAX], FP16)
        etp = [sb("etp_0", [128, 16 * TMAX])]
        uptmp = sb("uptmp", [128, 16 * TMAX], FP16)
        htmp2 = [sb(f"htmp8_{i}", [128, 16 * TMAX], FP16) for i in range(2)]
        be4 = [sb(f"be4_{i}", [128, 4 * TMAX], FP16) for i in range(2)]
        ce4 = [sb(f"ce4_{i}", [128, 4 * TMAX], FP16) for i in range(2)]
        Hc = sb("Hc", [128, 64 * SSTR])
        Hprev2 = [sb(f"Hprev{i}", [128, 64 * SSTR]) for i in range(2)]
        cbf2 = [sb(f"cbf{i}", [128, SSTR]) for i in range(2)]
        tmp64 = sb("tmp64", [128, 64])
        sq64 = sb("sq64", [128, 64])
        nrmcol = sb("nrmcol", [128, 1])
        c_row2 = [sb(f"c_row{i}", [1, SSTR]) for i in range(2)]
        c_sc = sb("c_sc", [1, 8])
        hcar = sb("hcar", [128, 64])
        hcar_in = sb("hcar_in", [128, 64])
        ps_be = ph.enter_context(nc.psum_tensor("sc_be", [128, 4 * TMAX], F32))
        ps_ce = ph.enter_context(nc.psum_tensor("sc_ce", [128, 4 * TMAX], F32))
        ps_abd = ph.enter_context(nc.psum_tensor("sc_abd", [1, 8], F32))
        ps_cb = ph.enter_context(nc.psum_tensor("sc_cb", [128, SSTR], F32))

        TILES = ([(0, 1, 1)] + [(1 + 120 * j, 120, 12) for j in range(34)]
                 + [(4081, 15, 1)])
        NT = len(TILES)
        R.op("vector", "memset", hcar[:], 0.0, inc=(SV, 1))

        ld_toks = [None, None]
        cp_toks = [None, None, None]
        st_y = [None, None, None]
        ess = [0]
        essn = {}
        argsv = [0]
        be_hold = [0]
        be_rd = [0, 0]
        be_usv = [0, 0]
        bmtn = [0]
        ce_hold = [0]
        ce_rd = [0, 0]
        ce_usv = [0, 0]
        cmtn = [0]
        cpre_sv = [0]
        ht_sg = [0, 0]
        up_T = [0]
        cst = {"cb": 0, "abd": 0, "cbf": 0}

        def emit_loads(j):
            t0, T, _ = TILES[j]
            jl, js = j % 2, j % 3
            R.wait("sync", SV)
            R.wait("sync", TN)
            toks = []
            for hbm_t, dstl in ((dt_hbm, dt8), (dtx_hbm, dtx8),
                                (xn1T_hbm, xn1T8)):
                dst = dstl[jl]
                srcp = bass.AP(hbm_t.tensor, hbm_t.offset + t0,
                               [[S, 128], [S * 128, 8], [1, T]])
                dstp = bass.AP(dst[:].tensor, dst[:].offset,
                               [dst[:].ap[0], [T, 8], [1, T]])
                toks.append(R.dma(dstp, srcp,
                                  allow_slow_non_contiguous=(T < 8)))
            bdst = bass.AP(bp8[jl][:].tensor, bp8[jl][:].offset,
                           [bp8[jl][:].ap[0], [T, 8], [1, T]])
            toks.append(R.dma(bdst,
                              bass.AP(bp_hbm.tensor, bp_hbm.offset + t0,
                                      [[S * 8, 1], [S, 8], [1, T]]),
                              allow_slow_non_contiguous=(T < 8)))
            ld_toks[jl] = toks
            cdst = bass.AP(cp8[js][:].tensor, cp8[js][:].offset,
                           [cp8[js][:].ap[0], [T, 8], [1, T]])
            cp_toks[js] = R.dma(cdst,
                                bass.AP(cp_hbm.tensor, cp_hbm.offset + t0,
                                        [[S * 8, 1], [S, 8], [1, T]]),
                                allow_slow_non_contiguous=(T < 8))

        def s1_stage(j):
            t0, T, nseg = TILES[j]
            tail = T - nseg * 10
            nst = nseg + (1 if tail > 0 else 0)
            seglen = 10 if T >= 10 else T
            T8 = 8 * T
            jl, js = j % 2, j % 3
            zoff = js * 64 * TMAX
            yoff = js * 8 * TMAX
            pre_start = []
            pre_pairs = [[] for _ in range(4)]
            ops = []
            em = lambda c, f: ops.append((c, f))
            emp = lambda c, f: pre_start.append((c, f))

            def fstart():
                for tk in ld_toks[jl]:
                    R.wait_tok("vector", tk)
                R.wait_tok("tensor", ld_toks[jl][3])
            emp(0, fstart)
            for q in range(2):
                def f_mm(q=q):
                    if be_hold[0]:
                        R.wait("tensor", SS, be_hold[0])
                    R.op("tensor", "matmul", ps_be[:, 0:4 * T], ones_f[:],
                         bp8[jl][0:1, q * 4 * T: (q + 1) * 4 * T],
                         start=True, stop=True, inc=(TN, 1))
                    bmtn[0] = TN.n

                def f_evac(q=q):
                    R.wait("scalar", TN, bmtn[0])
                    if be_usv[q]:
                        R.wait("scalar", SV, be_usv[q])
                    R.op("scalar", "copy", be4[q][:, 0:4 * T],
                         ps_be[:, 0:4 * T], inc=(SS, 1))
                    be_hold[0] = SS.n
                    be_rd[q] = SS.n
                emp(0, f_mm)
                emp(0, f_evac)
            for p in range(4):
                n = 2 * p
                nof = n * 8 * T
                ep = p % 2

                def f_arg(n=n, ep=ep):
                    if ess[0]:
                        R.wait("vector", SS, ess[0])
                    et = etp[0]
                    arg_o = bass.AP(et[:].tensor, et[:].offset,
                                    [et[:].ap[0], [T8, 2], [T, 8], [1, T]])
                    dt_i = bass.AP(dt8[jl][:].tensor, dt8[jl][:].offset,
                                   [dt8[jl][:].ap[0], [0, 2], [T, 8], [1, T]])
                    a_i = bass.AP(a16[:].tensor, a16[:].offset + n,
                                  [a16[:].ap[0], [1, 2], [8, 8], [0, T]])
                    R.op("vector", "tensor_mul", arg_o, dt_i, a_i, inc=(SV, 1))
                    argsv[0] = SV.n

                def f_tanh(n=n, nof=nof, ep=ep, p=p):
                    R.wait("scalar", SV, argsv[0])
                    et = etp[0]
                    R.op("scalar", "activation", et[:, 0:2 * T8], et[:, 0:2 * T8],
                         AF.Tanh, inc=(SS, 1))
                    R.op("scalar", "activation", dA_all[:, nof: nof + 2 * T8],
                         et[:, 0:2 * T8], AF.Exp, scale=16.0, inc=(SS, 1))
                    ess[0] = SS.n
                    essn[(j, p)] = SS.n
                pre_pairs[p].append((2.2, f_arg))
                pre_pairs[p].append((0, f_tanh))
            for p in range(4):
                n = 2 * p
                nof = n * 8 * T
                q = p // 2

                def f_u(n=n, nof=nof, q=q):
                    R.wait("vector", SS, be_rd[q])
                    u_o = bass.AP(u_all[:].tensor, u_all[:].offset + nof,
                                  [u_all[:].ap[0], [T8, 2], [T, 8], [1, T]])
                    dtx_i = bass.AP(dtx8[jl][:].tensor, dtx8[jl][:].offset,
                                    [dtx8[jl][:].ap[0], [0, 2], [T, 8], [1, T]])
                    be_i = bass.AP(be4[q][:].tensor,
                                   be4[q][:].offset + (n % 4) * T,
                                   [be4[q][:].ap[0], [T, 2], [0, 8], [1, T]])
                    R.op("vector", "tensor_mul", u_o, dtx_i, be_i, inc=(SV, 1))
                    be_usv[q] = SV.n

                def f_up(n=n, nof=nof, p=p):
                    R.wait("vector", SS, essn[(j, p)])
                    if up_T[0] != T:
                        R.op("vector", "memset", uptmp[:, 0:16 * TMAX], 0.0,
                             inc=(SV, 1))
                        up_T[0] = T
                    upd = bass.AP(uptmp[:].tensor, uptmp[:].offset,
                                  [uptmp[:].ap[0], [T8, 2], [T, 8], [10, nst]])
                    das = bass.AP(dA_all[:].tensor, dA_all[:].offset + nof,
                                  [dA_all[:].ap[0], [T8, 2], [T, 8], [10, nst]])
                    R.op("vector", "tensor_copy", upd, das, inc=(SV, 1))
                    R.op("vector", "memset", das, 0.0, inc=(SV, 1))

                def f_zscan(nof=nof):
                    R.op("vector", "tensor_tensor_scan",
                         Z3[:, zoff + nof: zoff + nof + 2 * T8],
                         dA_all[:, nof: nof + 2 * T8],
                         u_all[:, nof: nof + 2 * T8],
                         0.0, ALU.mult, ALU.add, inc=(SV, 1))

                def f_pscan(nof=nof):
                    R.op("vector", "tensor_tensor_scan",
                         P3[:, zoff + nof: zoff + nof + 2 * T8],
                         dA_all[:, nof: nof + 2 * T8], uptmp[:, 0:2 * T8],
                         0.0, ALU.mult, ALU.add, inc=(SV, 1))

                em(1.4, f_u)

                def f_upz(f1=f_up, f2=f_zscan):
                    f1()
                    f2()
                em(5.8, f_upz)
                em(4.6, f_pscan)

            def f_yinit():
                R.wait_tok("vector", st_y[js])
                y_o = bass.AP(y83[:].tensor, y83[:].offset + yoff,
                              [y83[:].ap[0], [T, 8], [1, T]])
                x_i = bass.AP(xn1T8[jl][:].tensor, xn1T8[jl][:].offset,
                              [xn1T8[jl][:].ap[0], [T, 8], [1, T]])
                d_i = bass.AP(dpar_c[:].tensor, dpar_c[:].offset,
                              [dpar_c[:].ap[0], [1, 8], [0, T]])
                R.op("vector", "tensor_mul", y_o, x_i, d_i, inc=(SV, 1))
            em(0.7, f_yinit)
            if j + 1 < NT:
                em(0, lambda j=j: emit_loads(j + 1))
            return pre_start, pre_pairs, ops

        def s2_stage(j):
            t0, T, nseg = TILES[j]
            tail = T - nseg * 10
            seglen = 10 if T >= 10 else T
            T8 = 8 * T
            js, jp = j % 3, j % 2
            zoff = js * 64 * TMAX
            yoff = js * 8 * TMAX
            ops = []
            em = lambda c, f: ops.append((c, f))
            def f_cpre():
                R.wait_tok("vector", cp_toks[js])
                ce_d = bass.AP(cp8[js][:].tensor, cp8[js][:].offset + seglen - 1,
                               [cp8[js][:].ap[0], [T, 8], [10, nseg]])
                cr_i = bass.AP(c_row2[jp][:].tensor, c_row2[jp][:].offset,
                               [c_row2[jp][:].ap[0], [0, 8], [1, nseg]])
                R.op("vector", "tensor_mul", ce_d, ce_d, cr_i, inc=(SV, 1))
                cpre_sv[0] = SV.n
            em(0.3, f_cpre)
            for q in range(2):
                def f_mm(q=q):
                    if ce_hold[0]:
                        R.wait("tensor", SS, ce_hold[0])
                    R.wait("tensor", SV, cpre_sv[0])
                    R.op("tensor", "matmul", ps_ce[:, 0:4 * T], ones_f[:],
                         cp8[js][0:1, q * 4 * T: (q + 1) * 4 * T],
                         start=True, stop=True, inc=(TN, 1))
                    cmtn[0] = TN.n

                def f_evac(q=q):
                    R.wait("scalar", TN, cmtn[0])
                    if ce_usv[q]:
                        R.wait("scalar", SV, ce_usv[q])
                    R.op("scalar", "copy", ce4[q][:, 0:4 * T],
                         ps_ce[:, 0:4 * T], inc=(SS, 1))
                    ce_hold[0] = SS.n
                    ce_rd[q] = SS.n
                em(0, f_mm)
                em(0, f_evac)
            seq = []
            adds_sg = {}
            hc_sv = {}
            y_sg = {}
            for p in range(4):
                n = 2 * p
                nof = n * 8 * T
                q = p // 2
                hb = htmp2[p % 2]

                def f_pmul(n=n, nof=nof, p=p, hb=hb):
                    for h in range(2):
                        hof = h * T8
                        p4 = bass.AP(P3[:].tensor,
                                     P3[:].offset + zoff + nof + hof,
                                     [P3[:].ap[0], [T, 8], [10, nseg],
                                      [1, seglen]])
                        h4 = bass.AP(Hprev2[jp][:].tensor,
                                     Hprev2[jp][:].offset + (n + h) * 8 * SSTR,
                                     [Hprev2[jp][:].ap[0], [SSTR, 8], [1, nseg],
                                      [0, seglen]])
                        o4 = bass.AP(hb[:].tensor, hb[:].offset + hof,
                                     [hb[:].ap[0], [T, 8], [10, nseg],
                                      [1, seglen]])
                        R.op("vector", "tensor_mul", o4, p4, h4, inc=(SV, 1))
                        if tail > 0:
                            pt = bass.AP(P3[:].tensor,
                                         P3[:].offset + zoff + nof + hof
                                         + nseg * 10,
                                         [P3[:].ap[0], [T, 8], [1, tail]])
                            ht = bass.AP(Hprev2[jp][:].tensor,
                                         Hprev2[jp][:].offset
                                         + (n + h) * 8 * SSTR + nseg,
                                         [Hprev2[jp][:].ap[0], [SSTR, 8],
                                          [0, tail]])
                            ot = bass.AP(hb[:].tensor,
                                         hb[:].offset + hof + nseg * 10,
                                         [hb[:].ap[0], [T, 8], [1, tail]])
                            R.op("vector", "tensor_mul", ot, pt, ht,
                                 inc=(SV, 1))

                def f_add(nof=nof, p=p, hb=hb):
                    R.op("vector", "tensor_add", hb[:, 0:2 * T8],
                         hb[:, 0:2 * T8],
                         Z3[:, zoff + nof: zoff + nof + 2 * T8], inc=(SV, 1))

                def f_hc(n=n, q=q, p=p, hb=hb):
                    R.wait("vector", SS, ce_rd[q])
                    h_i = bass.AP(hb[:].tensor, hb[:].offset,
                                  [hb[:].ap[0], [T8, 2], [T, 8], [1, T]])
                    ce_i = bass.AP(ce4[q][:].tensor,
                                   ce4[q][:].offset + (n % 4) * T,
                                   [ce4[q][:].ap[0], [T, 2], [0, 8], [1, T]])
                    R.op("vector", "tensor_mul", h_i, h_i, ce_i, inc=(SV, 1))
                    ce_usv[q] = SV.n
                    hc_sv[p] = SV.n

                def f_y(p=p, hb=hb):
                    for h in range(2):
                        R.op("vector", "tensor_add",
                             y83[:, yoff: yoff + T8],
                             y83[:, yoff: yoff + T8],
                             hb[:, h * T8: (h + 1) * T8], inc=(SV, 1))
                seq.append((p, f_pmul, f_add, f_hc, f_y))
            em(1.6, seq[0][1])
            em(1.2, seq[0][2])
            em(1.6, seq[1][1])
            em(1.2, seq[1][2])
            for p in range(4):
                if p + 2 < 4:
                    em(0.9, seq[p][3])
                    em(1.2, seq[p][4])
                    em(1.6, seq[p + 2][1])
                    em(1.2, seq[p + 2][2])
                else:
                    em(0.9, seq[p][3])
                    em(1.2, seq[p][4])
            def f_store():
                R.wait("sync", SV)
                dsty = bass.AP(y_hbm.tensor, y_hbm.offset + t0,
                               [[S, 128], [128 * S, 8], [1, T]])
                ysrc = bass.AP(y83[:].tensor, y83[:].offset + yoff,
                               [y83[:].ap[0], [T, 8], [1, T]])
                st_y[js] = R.dma(dsty, ysrc, allow_slow_non_contiguous=(T < 8))
            em(0, f_store)
            return ops

        def chain_stage(j, filler, bigq=None):
            t0, T, nseg = TILES[j]
            tail = T - nseg * 10
            nst = nseg + (1 if tail > 0 else 0)
            seglen = 10 if T >= 10 else T
            jp, js = j % 2, j % 3
            zoff = js * 64 * TMAX

            def take(budget):
                acc = 0.0
                while filler and acc < budget:
                    c, fn = filler.popleft()
                    fn()
                    acc += c
                if acc < budget and bigq:
                    c, fn = bigq.popleft()
                    fn()

            R.op("vector", "tensor_copy", hcar_in[:], hcar[:], inc=(SV, 1))
            take(1.0)
            for k in range(nseg):
                eoff = zoff + k * 10 + seglen - 1
                gsl = bass.AP(P3[:].tensor, P3[:].offset + eoff,
                              [P3[:].ap[0], [T, 64]])
                lsl = bass.AP(Z3[:].tensor, Z3[:].offset + eoff,
                              [Z3[:].ap[0], [T, 64]])
                hck = bass.AP(Hc[:].tensor, Hc[:].offset + k,
                              [Hc[:].ap[0], [SSTR, 64]])
                if k == 0:
                    R.op("vector", "scalar_tensor_tensor", tmp64[:], hcar_in[:],
                         1.0, gsl, ALU.mult, ALU.mult, inc=(SV, 1))
                else:
                    prev = bass.AP(Hc[:].tensor, Hc[:].offset + k - 1,
                                   [Hc[:].ap[0], [SSTR, 64]])
                    R.wait("vector", TN, cst["cb"])
                    R.op("vector", "scalar_tensor_tensor", tmp64[:], prev,
                         ps_cb[:, 0:1], gsl, ALU.mult, ALU.mult, inc=(SV, 1))
                R.op("vector", "tensor_add", hck, tmp64[:], lsl, inc=(SV, 1))
                R.op("vector", "scalar_tensor_tensor", sq64[:], hck, 1.0, hck,
                     ALU.mult, ALU.mult, accum_out=nrmcol[:], inc=(SV, 1))
                sv_head = SV.n
                R.wait("tensor", SV, sv_head)
                if cst["abd"]:
                    R.wait("tensor", SS, cst["abd"])
                R.op("tensor", "matmul", ps_abd[0:1, 0:1], ones_c[:], nrmcol[:],
                     start=True, stop=True, inc=(TN, 1))
                tn1 = TN.n
                R.wait("scalar", TN, tn1)
                R.op("scalar", "activation", c_sc[0:1, 0:1], ps_abd[0:1, 0:1],
                     AF.Ln, inc=(SS, 1))
                cst["abd"] = SS.n
                R.op("scalar", "activation", c_sc[0:1, 1:2], c_sc[0:1, 0:1],
                     AF.Exp, scale=-0.5, inc=(SS, 1))
                ss_k = SS.n
                take(2.2)
                R.wait("vector", SS, ss_k)
                if k == 0 and cst["cbf"]:
                    R.wait("vector", TN, cst["cbf"])
                R.op("vector", "tensor_scalar", c_row2[jp][0:1, k:k + 1],
                     c_sc[0:1, 1:2], 20.0, 1.0, ALU.mult, ALU.min, inc=(SV, 1))
                sv_tail = SV.n
                if k + 1 < nseg:
                    R.wait("tensor", SV, sv_tail)
                    R.op("tensor", "matmul", ps_cb[:, 0:1], ones_f[:],
                         c_row2[jp][0:1, k:k + 1], start=True, stop=True,
                         inc=(TN, 1))
                    cst["cb"] = TN.n
                take(1.2)
            R.wait("tensor", SV, sv_tail)
            R.op("tensor", "matmul", ps_cb[:, 0:nseg], ones_f[:],
                 c_row2[jp][0:1, 0:nseg], start=True, stop=True, inc=(TN, 1))
            cst["cbf"] = TN.n
            R.wait("vector", TN, cst["cbf"])
            R.op("vector", "tensor_copy", cbf2[jp][:, 0:nseg], ps_cb[:, 0:nseg],
                 inc=(SV, 1))
            hl = bass.AP(Hc[:].tensor, Hc[:].offset + nseg - 1,
                         [Hc[:].ap[0], [SSTR, 64]])
            R.op("vector", "tensor_scalar", hcar[:], hl,
                 cbf2[jp][:, nseg - 1:nseg], None, ALU.mult, inc=(SV, 1))
            hp0 = bass.AP(Hprev2[jp][:].tensor, Hprev2[jp][:].offset,
                          [Hprev2[jp][:].ap[0], [SSTR, 64]])
            R.op("vector", "tensor_copy", hp0, hcar_in[:], inc=(SV, 1))
            if nst > 1:
                hpk = bass.AP(Hprev2[jp][:].tensor, Hprev2[jp][:].offset + 1,
                              [Hprev2[jp][:].ap[0], [SSTR, 64], [1, nst - 1]])
                hck2 = bass.AP(Hc[:].tensor, Hc[:].offset,
                               [Hc[:].ap[0], [SSTR, 64], [1, nst - 1]])
                cbk = bass.AP(cbf2[jp][:].tensor, cbf2[jp][:].offset,
                              [cbf2[jp][:].ap[0], [0, 64], [1, nst - 1]])
                R.op("vector", "tensor_mul", hpk, hck2, cbk, inc=(SV, 1))

        emit_loads(0)
        sB = {}
        ps0, pp0, opsB0 = s1_stage(0)
        sB[0] = opsB0
        for c, fn in ps0:
            fn()
        for pl in pp0:
            for c, fn in pl:
                fn()
        for r in range(NT + 2):
            items = []
            if r - 2 >= 0:
                items.extend(s2_stage(r - 2))
            items.extend(sB.pop(r, []))
            if r + 1 < NT:
                pre_start, pre_pairs, opsB = s1_stage(r + 1)
                sB[r + 1] = opsB
            else:
                pre_start, pre_pairs = [], [[] for _ in range(4)]
            small = deque(x for x in items if x[0] <= 2.5)
            bigs = [x for x in items if x[0] > 2.5]
            big = deque()
            nb = len(bigs)
            same_T = (r < NT and r + 1 < NT and TILES[r][1] == TILES[r + 1][1])
            if same_T and nb:
                for i in range(max(nb, 8)):
                    if i < nb:
                        big.append(bigs[i])
                    if i == 1:
                        big.extend(pre_start)
                        pre_start = []
                    if i % 2 == 1 and i // 2 < 4:
                        big.extend(pre_pairs[i // 2])
            else:
                big.extend(bigs)
                big.extend(pre_start)
                for pl in pre_pairs:
                    big.extend(pl)
            if 0 <= r - 1 < NT:
                chain_stage(r - 1, small, big)
            for dq in (small, big):
                while dq:
                    c, fn = dq.popleft()
                    fn()
        R.wait_all_dma("sync")
        R.play()

    with ExitStack() as ph:
        R = Rec(nc, dpool, AUTO)
        yf2 = [ph.enter_context(nc.sbuf_tensor(f"yfc{i}", [128, 8 * TB], FP16))
               for i in range(2)]
        g1f2 = [ph.enter_context(nc.sbuf_tensor(f"g1fc{i}", [128, 8 * TB], FP16))
                for i in range(2)]
        sgfc = ph.enter_context(nc.sbuf_tensor("sgfc", [128, 8 * TB], F32))
        ob2 = [ph.enter_context(nc.sbuf_tensor(f"oblk{i}", [128, 4 * D], F32))
               for i in range(2)]
        xg2 = [ph.enter_context(nc.sbuf_tensor(f"xgblk{i}", [128, 4 * D], F32))
               for i in range(2)]
        ps_g2 = [ph.enter_context(nc.psum_tensor(f"gt_{i}", [128, 512], F32))
                 for i in range(2)]

        ytoks = {}

        def yg_load(b_, par_):
            t0_ = b_ * TB
            ysrc = bass.AP(y_hbm.tensor, y_hbm.offset + t0_,
                           [[S, 128], [128 * S, 8], [1, TB]])
            ydst = bass.AP(yf2[par_][:].tensor, yf2[par_][:].offset,
                           [yf2[par_][:].ap[0], [TB, 8], [1, TB]])
            ty = R.dma(ydst, ysrc)
            gsrc = bass.AP(g1_hbm.tensor, g1_hbm.offset + t0_,
                           [[S, 128], [128 * S, 8], [1, TB]])
            gdst = bass.AP(g1f2[par_][:].tensor, g1f2[par_][:].offset,
                           [g1f2[par_][:].ap[0], [TB, 8], [1, TB]])
            tg = R.dma(gdst, gsrc)
            ytoks[par_] = (ty, tg)

        xgtoks = {}

        def xg_load(b_, par_):
            src = bass.AP(x1_hbm.tensor, x1_hbm.offset + b_ * TB * D,
                          [[D, 128], [128 * D, 4], [1, D]])
            xg = xg2[par_]
            xdst = bass.AP(xg[:].tensor, xg[:].offset,
                           [xg[:].ap[0], [D, 4], [1, D]])
            xgtoks[par_] = R.dma(xdst, src)

        st_o = [None, None]
        xg_load(0, 0)
        yg_load(0, 0)
        trn_g = {0: 0, 1: 0}
        for b in range(NB):
            t0 = b * TB
            bpar = b % 2
            if b + 1 < NB:
                R.wait("sync", SV)
                R.wait_tok("sync", st_o[1 - bpar])
                xg_load(b + 1, 1 - bpar)
                yg_load(b + 1, 1 - bpar)
            R.wait_tok("vector", xgtoks[bpar])
            R.wait_tok("vector", st_o[bpar])
            ty, tg = ytoks[bpar]
            R.wait_tok("vector", ty)
            R.wait_tok("vector", tg)
            R.op("vector", "tensor_mul", sgfc[:], yf2[bpar][:], g1f2[bpar][:],
                 inc=(SV, 1))
            sg_sv = SV.n
            for dp in range(8):
                pst = ps_g2[dp % 2]
                R.wait("tensor", SV, max(trn_g[dp % 2], sg_sv))
                for tc in range(4):
                    R.op("tensor", "transpose", pst[:, bass.ts(tc, 128)],
                         sgfc[:, dp * TB + tc * 128: dp * TB + tc * 128 + 128],
                         identf[:], inc=(TN, 1))
                R.wait("vector", TN)
                ob = ob2[bpar]
                o_d = bass.AP(ob[:].tensor, ob[:].offset + dp * 128,
                              [ob[:].ap[0], [D, 4], [1, 128]])
                x_i = bass.AP(xg2[bpar][:].tensor, xg2[bpar][:].offset + dp * 128,
                              [xg2[bpar][:].ap[0], [D, 4], [1, 128]])
                p_i = bass.AP(pst[:].tensor, pst[:].offset,
                              [pst[:].ap[0], [128, 4], [1, 128]])
                R.op("vector", "tensor_add", o_d, p_i, x_i, inc=(SV, 1))
                trn_g[dp % 2] = SV.n
            R.wait("sync", SV)
            dsto = bass.AP(out.tensor, out.offset + t0 * D,
                           [[D, 128], [128 * D, 4], [1, D]])
            osrc = bass.AP(ob2[bpar][:].tensor, ob2[bpar][:].offset,
                           [ob2[bpar][:].ap[0], [D, 4], [1, D]])
            st_o[bpar] = R.dma(dsto, osrc)
        R.wait_all_dma("sync")
        R.play()
    l1.close()
    return nc


def kernel(**inp):
    nc = build()
    xs = np.ascontiguousarray(np.asarray(inp["L0_attn_x"], dtype=np.float32))

    def col8(v):
        return np.ascontiguousarray(np.asarray(v, dtype=np.float32).reshape(8, 128).T)

    base = {
        "qkv_w": np.ascontiguousarray(np.asarray(inp["L0_attn_qkv_w"], dtype=np.float32)),
        "qkv_b": np.ascontiguousarray(np.asarray(inp["L0_attn_qkv_b"], dtype=np.float32).reshape(1, -1)),
        "proj_w": np.ascontiguousarray(np.asarray(inp["L0_attn_proj_w"], dtype=np.float32)),
        "proj_b": col8(inp["L0_attn_proj_b"]),
        "f0_w": np.ascontiguousarray(np.asarray(inp["L0_ffn_weight"], dtype=np.float32)),
        "f0_b": col8(inp["L0_ffn_bias"]),
        "w_dt": np.ascontiguousarray(np.asarray(inp["L1_attn_W_dt"], dtype=np.float32)),
        "b_dt": col8(inp["L1_attn_b_dt"]),
        "w_B": np.ascontiguousarray(np.asarray(inp["L1_attn_W_B"], dtype=np.float32)),
        "w_C": np.ascontiguousarray(np.asarray(inp["L1_attn_W_C"], dtype=np.float32)),
        "a_log": np.ascontiguousarray(
            np.asarray(inp["L1_attn_A_log"], dtype=np.float32)
            .reshape(8, 128, 8).transpose(1, 0, 2).reshape(128, 64)),
        "d_par": col8(inp["L1_attn_D_param"]),
        "f1_w": np.ascontiguousarray(np.asarray(inp["L1_ffn_weight"], dtype=np.float32)),
    }
    in_maps = [dict(base, x=np.ascontiguousarray(xs[b])) for b in range(8)]
    res = run_bass_kernel_spmd(nc, in_maps, core_ids=list(range(8)))
    return np.stack([np.asarray(res.results[b]["out"], dtype=np.float32)
                     for b in range(8)], axis=0)


# version 29
# speedup vs baseline: 1.0492x; 1.0011x over previous
import os
import numpy as np
import ml_dtypes
import concourse.bass as bass
import concourse.mybir as mybir
from concourse.bass_utils import run_bass_kernel_spmd
from contextlib import ExitStack

F32 = mybir.dt.float32
BF16 = mybir.dt.bfloat16
FP16 = mybir.dt.float16
FP8 = mybir.dt.float8e4
AF = mybir.ActivationFunctionType
ALU = mybir.AluOpType
AX = mybir.AxisListType

S = 4096
D = 1024
H = 16
DH = 64
TB = 512
NB = S // TB
NST = 8
MAGIC = 12582912.0

ENGS = ("sync", "vector", "scalar", "tensor", "gpsimd")
SKIP = set(os.environ.get("K1_SKIP", "").split(","))

SCAN_TILES = [(0, 1, 1)] + [(1 + 320 * j, 320, 32) for j in range(12)] + [(3841, 255, 25)]


class Sem:
    def __init__(self, h):
        self.h = h
        self.n = 0


class Rec:

    def __init__(self, nc, dpool, auto):
        self.nc = nc
        self.q = {k: [] for k in ENGS}
        self.dpool = dpool
        self.didx = 0
        self.auto = auto

    def dma(self, dst, src, **kw):
        sem = self.dpool[self.didx % len(self.dpool)]
        self.didx += 1
        if sem.n > 0:
            self.wait("sync", sem)
        self.op("sync", "dma_start", inc=(sem, 16), out=dst, in_=src, **kw)
        return (sem, sem.n)

    def wait_tok(self, eng, tok):
        if tok is not None:
            self.q[eng].append(lambda e, h=tok[0].h, v=tok[1]: e.wait_ge(h, v))

    def wait_all_dma(self, eng):
        for sem in self.dpool:
            if sem.n:
                self.wait(eng, sem)

    def op(self, eng, method, *args, inc=None, **kw):
        if eng in self.auto:
            asem = self.auto[eng]
            if asem.n > 0:
                self.wait(eng, asem)
            if inc is None:
                inc = (asem, 1)
        tgt = None
        if inc is not None:
            sem, k = inc
            sem.n += k
            tgt = (sem.h, k)

        def thunk(e, method=method, args=args, kw=kw, tgt=tgt):
            o = getattr(e, method)(*args, **kw)
            if tgt is not None:
                o.then_inc(*tgt)
        self.q[eng].append(thunk)

    def wait(self, eng, sem, val=None):
        v = sem.n if val is None else val
        self.q[eng].append(lambda e, h=sem.h, v=v: e.wait_ge(h, v))

    def play(self):
        with self.nc.Block() as blk:
            for name in ENGS:
                lst = self.q[name]
                if lst:
                    getattr(blk, name)(lambda e, lst=lst: [f(e) for f in lst])
        self.nc.all_engine_barrier()
        self.q = {k: [] for k in ENGS}


def _np_rope_tables():
    inv = 1.0 / (10000.0 ** (np.arange(0, DH, 2, dtype=np.float32) / DH))
    t = np.arange(S, dtype=np.float32)
    fr = np.einsum("t,d->td", t, inv).astype(np.float32)
    fr = np.concatenate([fr, fr], axis=-1)
    cos = np.cos(fr).astype(np.float32).T
    sin = np.sin(fr).astype(np.float32).T
    cosT = np.concatenate([cos, cos], axis=0)
    sinT = np.concatenate([sin, sin], axis=0)
    return (np.asarray(cosT, dtype=ml_dtypes.bfloat16),
            np.asarray(sinT, dtype=ml_dtypes.bfloat16))


def build(stop_after=None):
    nc = bass.Bass("TRN2", target_bir_lowering=False)

    x_in = nc.dram_tensor("x", [S, D], F32, kind="ExternalInput").ap()
    qkv_w = nc.dram_tensor("qkv_w", [3 * D, D], F32, kind="ExternalInput").ap()
    qkv_b = nc.dram_tensor("qkv_b", [1, 3 * D], F32, kind="ExternalInput").ap()
    proj_w = nc.dram_tensor("proj_w", [D, D], F32, kind="ExternalInput").ap()
    proj_b = nc.dram_tensor("proj_b", [128, 8], F32, kind="ExternalInput").ap()
    f0_w = nc.dram_tensor("f0_w", [D, D], F32, kind="ExternalInput").ap()
    f0_b = nc.dram_tensor("f0_b", [128, 8], F32, kind="ExternalInput").ap()
    w_dt = nc.dram_tensor("w_dt", [D, D], F32, kind="ExternalInput").ap()
    b_dt = nc.dram_tensor("b_dt", [128, 8], F32, kind="ExternalInput").ap()
    w_B = nc.dram_tensor("w_B", [NST, D], F32, kind="ExternalInput").ap()
    w_C = nc.dram_tensor("w_C", [NST, D], F32, kind="ExternalInput").ap()
    a_log = nc.dram_tensor("a_log", [128, 64], F32, kind="ExternalInput").ap()
    d_par = nc.dram_tensor("d_par", [128, 8], F32, kind="ExternalInput").ap()
    f1_w = nc.dram_tensor("f1_w", [D, D], F32, kind="ExternalInput").ap()
    out = nc.dram_tensor("out", [S, D], F32, kind="ExternalOutput").ap()

    ident_d = nc.inline_tensor(np.eye(128, dtype=np.float32), name="ident_f").ap()
    cosT_np, sinT_np = _np_rope_tables()
    cos_d = nc.inline_tensor(cosT_np, name="cosT").ap()
    sin_d = nc.inline_tensor(sinT_np, name="sinT").ap()
    Rm = np.zeros((64, 64), dtype=np.float32)
    Rm[0:32, 32:64] = -np.eye(32)
    Rm[32:64, 0:32] = np.eye(32)
    RT2 = np.zeros((128, 128), dtype=np.float32)
    RT2[0:64, 0:64] = Rm.T
    RT2[64:128, 64:128] = Rm.T
    rot_d = nc.inline_tensor(np.asarray(RT2, dtype=ml_dtypes.bfloat16), name="rotT").ap()
    tri_np = (np.arange(128)[:, None] <= np.arange(128)[None, :])
    tri_d = nc.inline_tensor(np.asarray(tri_np, dtype=ml_dtypes.bfloat16), name="tri").ap()

    def scratch(name, shape, dt):
        return nc.dram_tensor(name, shape, dt, kind="Internal").ap()

    wqkvT_h = scratch("wqkvT_h", [D, 3 * D], BF16)
    wprojT_h = scratch("wprojT_h", [D, D], BF16)
    wdtT_h = scratch("wdtT_h", [D, D], BF16)
    wf0T_h = scratch("wf0T_h", [D, D], FP8)
    wq1T_h = scratch("wq1T_h", [D, D], BF16)
    k_hbm = scratch("k_hbm", [4, 128, S], BF16)
    v_hbm = scratch("v_hbm", [128, 8 * 32 * 65], BF16)
    g_hbm = scratch("g_hbm", [D, S], FP16)
    ctx_hbm = scratch("ctx_hbm", [512, S], BF16)
    xnT0_hbm = scratch("xnT0_hbm", [D, S], BF16)
    x1_hbm = scratch("x1_hbm", [S, D], F32)
    xn1T_hbm = scratch("xn1T_hbm", [D, S], BF16)
    dt_hbm = scratch("dt_hbm", [D, S], F32)
    dtx_hbm = scratch("dtx_hbm", [D, S], FP16)
    g1_hbm = scratch("g1_hbm", [D, S], FP16)
    y_hbm = scratch("y_hbm", [D, S], FP16)

    top = ExitStack()
    dpool = [Sem(top.enter_context(nc.semaphore(f"DP{i}"))) for i in range(12)]
    SV = Sem(top.enter_context(nc.semaphore("SV")))
    SS = Sem(top.enter_context(nc.semaphore("SS")))
    TN = Sem(top.enter_context(nc.semaphore("TN")))
    SG = Sem(top.enter_context(nc.semaphore("SG")))
    AUTO = {"vector": SV, "scalar": SS, "gpsimd": SG}

    sbt = lambda n, s, d=F32: top.enter_context(nc.sbuf_tensor(n, s, d))
    identf = sbt("identf", [128, 128])
    rotT = sbt("rotTs", [128, 128], BF16)
    tri = sbt("tris", [128, 128], BF16)
    ones_b = sbt("ones_b", [1, TB], BF16)
    ones_f = sbt("ones_f", [1, 128])
    ones_c = sbt("ones_c", [128, 1])
    c1 = sbt("c1", [128, 1])
    cS1 = sbt("cS1", [128, 1])
    qkvb_row = sbt("qkvb_row", [1, 3 * D], BF16)
    projb_c = sbt("projb_c", [128, 8])
    f0b_c = sbt("f0b_c", [128, 8])
    bdt_c = sbt("bdt_c", [128, 8])
    dpar_c = sbt("dpar_c", [128, 8])
    a16 = sbt("a16", [128, 64])
    wBT = sbt("wBT", [128, 64])
    wCT = sbt("wCT", [128, 64])
    wsinv_c = sbt("wsinv_c", [128, 1])
    wsd127 = sbt("wsd127", [1, 1])

    with ExitStack() as ph:
        R = Rec(nc, dpool, AUTO)
        ps_t = ph.enter_context(nc.psum_tensor("ps_t", [128, 512], F32))
        wld2 = [ph.enter_context(nc.sbuf_tensor(f"wld{i}", [128, D], F32))
                for i in range(2)]
        wtmp = ph.enter_context(nc.sbuf_tensor("wtmp", [128, D], BF16))
        wtmp8 = ph.enter_context(nc.sbuf_tensor("wtmp8", [128, D], FP8))
        rowtmp = ph.enter_context(nc.sbuf_tensor("rowtmp", [1, 3 * D], F32))
        colstage = ph.enter_context(nc.sbuf_tensor("colstage", [128, 64], F32))
        wqtmp = ph.enter_context(nc.sbuf_tensor("wqtmp", [128, D], F32))
        redtmp = ph.enter_context(nc.sbuf_tensor("redtmp", [128, 16], F32))
        sc1 = ph.enter_context(nc.sbuf_tensor("sc1", [1, 16], F32))

        for dst, src in ((identf[:], ident_d), (rotT[:], rot_d), (tri[:], tri_d),
                         (rowtmp[:], qkv_b), (projb_c[:], proj_b),
                         (f0b_c[:], f0_b), (bdt_c[:], b_dt), (dpar_c[:], d_par),
                         (colstage[:], a_log)):
            R.dma(dst, src)
        R.wait_all_dma("vector")
        R.wait_all_dma("scalar")
        for t_, v_ in ((ones_b, 1.0), (ones_f, 1.0), (ones_c, 1.0), (c1, 1.0), (cS1, -0.0505 / 0.099)):
            R.op("vector", "memset", t_[:], v_)
        R.op("vector", "tensor_copy", qkvb_row[:], rowtmp[:], inc=(SV, 1))
        R.op("scalar", "activation", colstage[:], colstage[:], AF.Tanh, scale=0.1,
             inc=(SS, 1))
        R.op("scalar", "activation", colstage[:], colstage[:], AF.Exp, scale=10.0,
             inc=(SS, 1))
        R.wait("vector", SS)
        R.op("vector", "tensor_scalar", a16[:], colstage[:], -1.0 / 16.0, None,
             ALU.mult, inc=(SV, 1))

        wtoks = {}
        wtrn = {0: 0, 1: 0}
        for wi, (src, dst) in enumerate(((w_B, wBT), (w_C, wCT))):
            R.wait("sync", TN)
            wld = wld2[wi % 2]
            tok = R.dma(wld[0:8, :], src)
            for dc in range(8):
                R.wait_tok("tensor", tok)
                R.wait("tensor", SV)
                R.op("tensor", "transpose", ps_t[:, 0:8], wld[0:8, bass.ts(dc, 128)],
                     identf[0:8, 0:8], inc=(TN, 1))
                R.wait("vector", TN)
                wtrn[wi % 2] = TN.n
                R.op("vector", "tensor_copy", dst[:, bass.ts(dc, 8)], ps_t[:, 0:8],
                     inc=(SV, 1))

        def stage_weight(src_hbm, n_oc, dst_hbm, dst_cols, cast_tile, oc_off=0,
                         st_tok=[None]):
            def issue(oc):
                src_ap = bass.AP(src_hbm.tensor,
                                 src_hbm.offset + (oc_off + oc * 128) * D,
                                 [[D, 128], [1, D]])
                if wtrn[oc % 2]:
                    R.wait("sync", TN, wtrn[oc % 2])
                wtoks[oc % 2] = R.dma(wld2[oc % 2][:], src_ap)
            issue(0)
            for oc in range(n_oc):
                par = oc % 2
                if oc + 1 < n_oc:
                    issue(oc + 1)
                wld = wld2[par]
                R.wait_tok("tensor", wtoks[par])
                for half in range(2):
                    R.wait("tensor", SV)
                    for q4 in range(4):
                        R.op("tensor", "transpose", ps_t[:, bass.ts(q4, 128)],
                             wld[:, bass.ts(half * 4 + q4, 128)], identf[:],
                             inc=(TN, 1))
                    R.wait("vector", TN)
                    if half == 0:
                        R.wait_tok("vector", st_tok[0])
                    R.op("vector", "tensor_copy",
                         cast_tile[:, half * 512: half * 512 + 512],
                         ps_t[:, 0:512], inc=(SV, 1))
                wtrn[par] = TN.n
                R.wait("sync", SV)
                dst_ap = bass.AP(dst_hbm.tensor, dst_hbm.offset + oc * 128,
                                 [[dst_cols, 128], [dst_cols * 128, 8], [1, 128]])
                src_s = bass.AP(cast_tile[:].tensor, cast_tile[:].offset,
                                [[cast_tile[:].ap[0][0], 128], [128, 8], [1, 128]])
                st_tok[0] = R.dma(dst_ap, src_s)

        stage_weight(qkv_w, 24, wqkvT_h, 3 * D, wtmp)
        stage_weight(proj_w, 8, wprojT_h, D, wtmp)
        stage_weight(w_dt, 8, wdtT_h, D, wtmp)
        stage_weight(f0_w, 8, wf0T_h, D, wtmp8)

        wsv = {0: 0, 1: 0}

        def issue_f1(oc):
            src_ap = bass.AP(f1_w.tensor, f1_w.offset + oc * 128 * D,
                             [[D, 128], [1, D]])
            if wtrn[oc % 2]:
                R.wait("sync", TN, wtrn[oc % 2])
            if wsv[oc % 2]:
                R.wait("sync", SV, wsv[oc % 2])
            wtoks[oc % 2] = R.dma(wld2[oc % 2][:], src_ap)
        issue_f1(0)
        for oc in range(8):
            par = oc % 2
            if oc + 1 < 8:
                issue_f1(oc + 1)
            R.wait_tok("vector", wtoks[par])
            R.op("vector", "reduce_sum", redtmp[:, oc:oc + 1], wld2[par][:],
                 AX.X, apply_absolute_value=True, inc=(SV, 1))
            wsv[par] = SV.n
        R.wait("tensor", SV)
        R.op("tensor", "matmul", ps_t[0:1, 0:8], ones_c[:], redtmp[:, 0:8],
             start=True, stop=True, inc=(TN, 1))
        R.wait("vector", TN)
        R.op("vector", "reduce_sum", sc1[:, 0:1], ps_t[0:1, 0:8], AX.X, inc=(SV, 1))
        R.op("vector", "tensor_scalar", sc1[:, 1:2], sc1[:, 0:1],
             1.0 / (D * D), 1e-8, ALU.mult, ALU.add, inc=(SV, 1))
        R.op("vector", "reciprocal", sc1[:, 2:3], sc1[:, 1:2], inc=(SV, 1))
        R.op("vector", "tensor_scalar", wsd127[:], sc1[:, 1:2], 1.0 / 127.0, None,
             ALU.mult, inc=(SV, 1))
        R.wait("tensor", SV)
        R.op("tensor", "matmul", ps_t[:, 0:1], ones_f[:], sc1[:, 2:3],
             start=True, stop=True, inc=(TN, 1))
        R.wait("vector", TN)
        R.op("vector", "tensor_copy", wsinv_c[:], ps_t[:, 0:1], inc=(SV, 1))
        wq_st = None
        issue_f1(0)
        for oc in range(8):
            par = oc % 2
            if oc + 1 < 8:
                issue_f1(oc + 1)
            wld = wld2[par]
            R.wait_tok("vector", wtoks[par])
            R.op("vector", "tensor_scalar", wqtmp[:], wld[:], wsinv_c[:, 0:1], None,
                 ALU.mult, inc=(SV, 1))
            R.op("vector", "tensor_scalar", wld[:], wqtmp[:], 0.5, None, ALU.is_ge,
                 inc=(SV, 1))
            R.op("vector", "tensor_scalar", wqtmp[:], wqtmp[:], -0.5, None,
                 ALU.is_le, inc=(SV, 1))
            R.op("vector", "tensor_sub", wld[:], wld[:], wqtmp[:], inc=(SV, 1))
            for half in range(2):
                R.wait("tensor", SV)
                for q4 in range(4):
                    R.op("tensor", "transpose", ps_t[:, bass.ts(q4, 128)],
                         wld[:, bass.ts(half * 4 + q4, 128)], identf[:],
                         inc=(TN, 1))
                R.wait("vector", TN)
                if half == 0:
                    R.wait_tok("vector", wq_st)
                R.op("vector", "tensor_copy",
                     wtmp[:, half * 512: half * 512 + 512],
                     ps_t[:, 0:512], inc=(SV, 1))
            wtrn[par] = TN.n
            wsv[par] = SV.n
            R.wait("sync", SV)
            dst_ap = bass.AP(wq1T_h.tensor, wq1T_h.offset + oc * 128,
                             [[D, 128], [D * 128, 8], [1, 128]])
            src_s = bass.AP(wtmp[:].tensor, wtmp[:].offset,
                            [[wtmp[:].ap[0][0], 128], [128, 8], [1, 128]])
            wq_st = R.dma(dst_ap, src_s)
        R.wait_all_dma("sync")
        R.play()
    if stop_after == "p0":
        return nc

    l0 = ExitStack()
    s0 = lambda n, s, d=F32: l0.enter_context(nc.sbuf_tensor(n, s, d))
    cos_blk2 = [s0(f"cos_blk{i}", [128, TB], BF16) for i in range(2)]
    sin_blk2 = [s0(f"sin_blk{i}", [128, TB], BF16) for i in range(2)]
    kv_res = s0("kv_res", [128, 4 * S], BF16)
    v_res = s0("v_res", [128, 8 * 32 * 65], BF16)
    xblk2 = [s0(f"xblk{i}", [128, 4 * D]) for i in range(2)]
    xnT2 = [s0(f"xnT2_{i}", [128, 8 * TB], BF16) for i in range(2)]

    def emit_ln4(R, src_tile, dst_tile, sq, s2, st):
        m4, var4, rstd4 = st["m"], st["var"], st["rstd"]
        y24, u4, w4 = st["y2"], st["u"], st["w"]
        for tc in range(4):
            R.op("vector", "reduce_sum", m4[:, tc:tc + 1],
                 src_tile[:, bass.ts(tc, D)], AX.X)
            R.wait("scalar", SV)
            R.op("scalar", "activation", sq[:], src_tile[:, bass.ts(tc, D)],
                 AF.Square, accum_out=s2[:, tc:tc + 1], inc=(SS, 1))
        R.wait("vector", SS)
        R.op("vector", "tensor_scalar", m4[:], m4[:], 1.0 / D, None, ALU.mult)
        R.op("vector", "tensor_scalar", var4[:], s2[:], 1.0 / D, None, ALU.mult)
        R.op("vector", "tensor_mul", y24[:], m4[:], m4[:])
        R.op("vector", "tensor_sub", var4[:], var4[:], y24[:])
        R.op("vector", "tensor_scalar", var4[:], var4[:], 1.0, 1e-5, ALU.mult,
             ALU.add)
        R.op("vector", "reciprocal", rstd4[:], var4[:])
        for _ in range(4):
            R.op("vector", "tensor_mul", y24[:], rstd4[:], rstd4[:])
            R.op("vector", "tensor_mul", u4[:], var4[:], y24[:])
            R.op("vector", "tensor_scalar", w4[:], u4[:], -0.5, 1.5, ALU.mult,
                 ALU.add)
            R.op("vector", "tensor_mul", rstd4[:], rstd4[:], w4[:])
        for tc in range(4):
            R.op("vector", "tensor_scalar", dst_tile[:, bass.ts(tc, D)],
                 src_tile[:, bass.ts(tc, D)], m4[:, tc:tc + 1],
                 rstd4[:, tc:tc + 1], ALU.subtract, ALU.mult, inc=(SV, 1))

    xtoks = {}
    xnt_toks = {}

    def emit_xload(R, blkidx, par, want_x=True, want_xnt=False):
        t0_ = blkidx * TB
        toks = []
        if want_x:
            src = bass.AP(x_in.tensor, x_in.offset + blkidx * TB * D,
                          [[D, 128], [128 * D, 4], [1, D]])
            xb = xblk2[par]
            xdst = bass.AP(xb[:].tensor, xb[:].offset,
                           [xb[:].ap[0], [D, 4], [1, D]])
            toks.append(R.dma(xdst, src))
        cs = bass.AP(cos_d.tensor, cos_d.offset + t0_, [[S, 128], [1, TB]])
        sn = bass.AP(sin_d.tensor, sin_d.offset + t0_, [[S, 128], [1, TB]])
        toks.append(R.dma(cos_blk2[par][:], cs))
        toks.append(R.dma(sin_blk2[par][:], sn))
        xtoks[par] = tuple(toks)
        if want_xnt:
            src = bass.AP(xnT0_hbm.tensor, xnT0_hbm.offset + t0_,
                          [[S, 128], [S * 128, 8], [1, TB]])
            dst = bass.AP(xnT2[par][:].tensor, xnT2[par][:].offset,
                          [xnT2[par][:].ap[0], [TB, 8], [1, TB]])
            xnt_toks[par] = R.dma(dst, src)

    with ExitStack() as ph:
        R = Rec(nc, dpool, AUTO)
        wkvT = ph.enter_context(nc.sbuf_tensor("wkvT", [128, 8 * 2048], BF16))
        wf0T = ph.enter_context(nc.sbuf_tensor("wf0T", [128, 8 * D], FP8))
        xnT_f8 = ph.enter_context(nc.sbuf_tensor("xnT_f8", [128, 8 * TB], FP8))
        k_raw = ph.enter_context(nc.sbuf_tensor("k_raw", [128, TB], BF16))
        k_t1 = ph.enter_context(nc.sbuf_tensor("k_t1", [128, TB], BF16))
        k_t2 = ph.enter_context(nc.sbuf_tensor("k_t2", [128, TB], BF16))
        kstage = ph.enter_context(nc.sbuf_tensor("kstage", [128, TB], BF16))
        vstage = ph.enter_context(nc.sbuf_tensor("vstage", [128, TB], BF16))
        gstage = ph.enter_context(nc.sbuf_tensor("gstage", [128, TB], FP16))
        xnblk = ph.enter_context(nc.sbuf_tensor("xnblk", [128, 4 * D], F32))
        sqdump = ph.enter_context(nc.sbuf_tensor("sqdump", [128, D], F32))
        s2t = ph.enter_context(nc.sbuf_tensor("s2t", [128, 4], F32))
        stat = {k: ph.enter_context(nc.sbuf_tensor("st_" + k, [128, 4], F32))
                for k in ("m", "var", "rstd", "y2", "u", "w")}
        ps_a = ph.enter_context(nc.psum_tensor("p1_a", [128, 512], F32))
        ps_k = [ph.enter_context(nc.psum_tensor(f"p1_k{i}", [128, 512], F32))
                for i in range(2)]
        ps_r = ph.enter_context(nc.psum_tensor("p1_r", [128, 512], F32))
        ps_v = [ph.enter_context(nc.psum_tensor(f"p1_v{i}", [128, 512], F32))
                for i in range(2)]
        ps_g = [ph.enter_context(nc.psum_tensor(f"p1_g{i}", [128, 512], F32))
                for i in range(2)]

        if "ones" not in SKIP:
            ones_ap = bass.AP(v_res[:].tensor, v_res[:].offset + 64,
                              [v_res[:].ap[0], [65, 8 * 32]])
            R.op("vector", "memset", ones_ap, 1.0, inc=(SV, 1))
        wkv_toks = []
        for ic in range(8):
            wsrc = bass.AP(wqkvT_h.tensor, wqkvT_h.offset + ic * 3 * D * 128 + 1024,
                           [[3 * D, 128], [1, 2048]])
            wkv_toks.append(R.dma(wkvT[:, ic * 2048:(ic + 1) * 2048], wsrc))
        wf0_toks = []
        for ic in range(8):
            w8src = bass.AP(wf0T_h.tensor, wf0T_h.offset + ic * D * 128,
                            [[D, 128], [1, D]])
            wf0_toks.append(R.dma(wf0T[:, ic * D:(ic + 1) * D], w8src))
        st_k = None
        st_v = None
        st_g = None
        st_xnt = [None, None]
        emit_xload(R, 0, 0)
        for b in range(NB):
            t0 = b * TB
            par = b % 2
            cos_blk, sin_blk = cos_blk2[par], sin_blk2[par]
            xnT_bf = xnT2[par]
            xblk = xblk2[par]
            for tk in xtoks[par]:
                R.wait_tok("vector", tk)
            R.wait_tok("scalar", xtoks[par][0])
            emit_ln4(R, xblk[:], xnblk[:], sqdump, s2t, stat)
            R.wait_tok("vector", st_xnt[par])
            p1tr = {0: 0, 1: 0}
            R.wait("tensor", SV)
            R.wait("tensor", SS)
            for ic in range(8):
                pst = (ps_a, ps_r)[ic % 2]
                if p1tr[ic % 2]:
                    R.wait("tensor", SV, p1tr[ic % 2])
                for tc in range(4):
                    R.op("tensor", "transpose", pst[:, bass.ts(tc, 128)],
                         xnblk[:, tc * D + ic * 128: tc * D + ic * 128 + 128],
                         identf[:], inc=(TN, 1))
                R.wait("vector", TN)
                R.op("vector", "tensor_copy", xnT_bf[:, bass.ts(ic, TB)],
                     pst[:, 0:512], inc=(SV, 1))
                R.op("vector", "tensor_copy", xnT_f8[:, bass.ts(ic, TB)],
                     pst[:, 0:512], inc=(SV, 1))
                p1tr[ic % 2] = SV.n
            R.wait("sync", SV)
            xts = bass.AP(xnT_bf[:].tensor, xnT_bf[:].offset,
                          [xnT_bf[:].ap[0], [TB, 8], [1, TB]])
            xtd = bass.AP(xnT0_hbm.tensor, xnT0_hbm.offset + t0,
                          [[S, 128], [S * 128, 8], [1, TB]])
            st_xnt[par] = R.dma(xtd, xts)
            if b + 1 < NB:
                R.wait("sync", SV)
                emit_xload(R, b + 1, 1 - par)
            def emit_k(kc):
                nonlocal st_k
                pk = ps_k[kc % 2]
                R.wait("tensor", SS)
                R.wait("tensor", SV)
                for ic in range(8):
                    if b == 0:
                        R.wait_tok("tensor", wkv_toks[ic])
                    R.op("tensor", "matmul", pk[:],
                         wkvT[:, ic * 2048 + kc * 128: ic * 2048 + kc * 128 + 128],
                         xnT_bf[:, bass.ts(ic, TB)],
                         start=(ic == 0), stop=False, inc=(TN, 1))
                R.op("tensor", "matmul", pk[:],
                     qkvb_row[0:1, 1024 + kc * 128: 1024 + kc * 128 + 128],
                     ones_b[:], start=False, stop=True, inc=(TN, 1))
                R.wait("scalar", TN)
                R.wait("scalar", SV)
                R.op("scalar", "copy", k_raw[:], pk[:], inc=(SS, 1))
                R.wait("tensor", SS)
                R.wait("tensor", SV)
                R.op("tensor", "matmul", ps_r[:], rotT[:], k_raw[:],
                     start=True, stop=True, inc=(TN, 1))
                R.wait("vector", SS)
                R.op("vector", "tensor_mul", k_t1[:], k_raw[:],
                     cos_blk[:], inc=(SV, 1))
                R.wait("vector", TN)
                R.op("vector", "tensor_mul", k_t2[:], ps_r[:],
                     sin_blk[:], inc=(SV, 1))
                if kc < 4:
                    R.op("vector", "tensor_add",
                         kv_res[:, kc * S + t0: kc * S + t0 + TB],
                         k_t1[:], k_t2[:], inc=(SV, 1))
                else:
                    R.wait_tok("vector", st_k)
                    R.op("vector", "tensor_add", kstage[:], k_t1[:], k_t2[:],
                         inc=(SV, 1))
                    R.wait("sync", SV)
                    dst = bass.AP(k_hbm.tensor,
                                  k_hbm.offset + (kc - 4) * 128 * S + t0,
                                  [[S, 128], [1, TB]])
                    st_k = R.dma(dst, kstage[:])

            def emit_v(vi):
                nonlocal st_v
                tc, dvc = vi // 2, vi % 2
                pv = ps_v[dvc]
                R.wait("tensor", SV)
                for ic in range(8):
                    if b == 0:
                        R.wait_tok("tensor", wkv_toks[ic])
                    R.op("tensor", "matmul", pv[:],
                         xnT_bf[:, ic * TB + tc * 128: ic * TB + tc * 128 + 128],
                         wkvT[:, ic * 2048 + 1024 + dvc * 512:
                              ic * 2048 + 1024 + dvc * 512 + 512],
                         start=(ic == 0), stop=False, inc=(TN, 1))
                R.op("tensor", "matmul", pv[:], ones_b[0:1, 0:128],
                     qkvb_row[0:1, 2048 + dvc * 512: 2048 + dvc * 512 + 512],
                     start=False, stop=True, inc=(TN, 1))
                R.wait("vector", TN)
                kcg = b * 4 + tc
                if dvc == 0:
                    dsts = bass.AP(v_res[:].tensor, v_res[:].offset + kcg * 65,
                                   [v_res[:].ap[0], [32 * 65, 8], [1, 64]])
                    srcs = bass.AP(pv[:].tensor, pv[:].offset,
                                   [pv[:].ap[0], [64, 8], [1, 64]])
                    R.op("vector", "tensor_copy", dsts, srcs, inc=(SV, 1))
                else:
                    R.wait_tok("vector", st_v)
                    R.op("vector", "tensor_copy", vstage[:], pv[:], inc=(SV, 1))
                    R.wait("sync", SV)
                    dst = bass.AP(v_hbm.tensor, v_hbm.offset + kcg * 65,
                                  [[8 * 32 * 65, 128], [32 * 65, 8], [1, 64]])
                    vsrc = bass.AP(vstage[:].tensor, vstage[:].offset,
                                   [vstage[:].ap[0], [64, 8], [1, 64]])
                    st_v = R.dma(dst, vsrc)

            def emit_f(fc):
                nonlocal st_g
                pg = ps_g[fc % 2]
                R.wait("tensor", SS)
                for ic in range(8):
                    if b == 0:
                        R.wait_tok("tensor", wf0_toks[ic])
                    R.op("tensor", "matmul", pg[:],
                         wf0T[:, ic * D + fc * 128: ic * D + fc * 128 + 128],
                         xnT_f8[:, bass.ts(ic, TB)],
                         start=(ic == 0), stop=(ic == 7), inc=(TN, 1))
                R.wait("scalar", TN)
                R.wait_tok("scalar", st_g)
                R.op("scalar", "activation", gstage[:], pg[:], AF.Silu,
                     bias=f0b_c[:, fc:fc + 1], inc=(SS, 1))
                R.wait("sync", SS)
                dst = bass.AP(g_hbm.tensor, g_hbm.offset + fc * 128 * S + t0,
                              [[S, 128], [1, TB]])
                st_g = R.dma(dst, gstage[:])

            for i in range(8):
                if 'k' not in SKIP:
                    emit_k(i)
                if 'v' not in SKIP:
                    emit_v(i)
                if 'ffn0' not in SKIP:
                    emit_f(i)
        R.wait_all_dma("sync")
        R.play()

    def attn_phase(group):
        G = f"g{group}_"
        with ExitStack() as ph:
            R = Rec(nc, dpool, AUTO)
            wqT = ph.enter_context(nc.sbuf_tensor(G + "wqT", [128, 8 * 512], BF16))
            q_res2 = [ph.enter_context(nc.sbuf_tensor(G + f"q_res{i}", [128, 4 * TB], BF16))
                      for i in range(2)]
            q_raw = ph.enter_context(nc.sbuf_tensor(G + "q_raw", [128, TB], BF16))
            q_t1 = ph.enter_context(nc.sbuf_tensor(G + "q_t1", [128, TB], BF16))
            pTAB = ph.enter_context(nc.sbuf_tensor(G + "pTAB", [128, 4096], BF16))
            zrow = ph.enter_context(nc.sbuf_tensor(G + "zrow", [1, 512], F32))
            zrow2 = ph.enter_context(nc.sbuf_tensor(G + "zrow2", [1, 512], F32))
            zbf = ph.enter_context(nc.sbuf_tensor(G + "zbf", [128, 512], F32))
            ctxst = ph.enter_context(nc.sbuf_tensor(G + "ctxst", [64, 512], BF16))
            ctxst2 = ph.enter_context(nc.sbuf_tensor(G + "ctxst2", [64, 512], BF16))
            ps_AB = ph.enter_context(nc.psum_tensor(G + "p2_AB", [128, 2048], F32))
            ps_C = ph.enter_context(nc.psum_tensor(G + "p2_C", [128, 512], F32))
            ps_cA = ph.enter_context(nc.psum_tensor(G + "p2_cA", [65, 512], F32))
            ps_cB = ph.enter_context(nc.psum_tensor(G + "p2_cB", [65, 512], F32))
            ps_q = ph.enter_context(nc.psum_tensor(G + "p2_q", [128, 512], F32))
            if group == 1:
                wpT = ph.enter_context(nc.sbuf_tensor(G + "wpT", [128, 8 * D], BF16))
                ctxa_ld = ph.enter_context(nc.sbuf_tensor(G + "ctxa_ld", [128, 2048], BF16))
                ctxb_res = ph.enter_context(nc.sbuf_tensor(G + "ctxb_res", [128, 2048], BF16))
                g_fc = ph.enter_context(nc.sbuf_tensor(G + "g_fc", [128, 512], FP16))
                sg_fc = ph.enter_context(nc.sbuf_tensor(G + "sg_fc", [128, 512], F32))

            wsrc = bass.AP(wqkvT_h.tensor, wqkvT_h.offset + group * 512,
                           [[3 * D, 128], [3 * D * 128, 8], [1, 512]])
            wdstq = bass.AP(wqT[:].tensor, wqT[:].offset,
                            [wqT[:].ap[0], [512, 8], [1, 512]])
            wq_tok = R.dma(wdstq, wsrc)
            k_toks = [None] * 4
            v_toks = [None] * 4
            wp_tok = None
            if group == 1:
                for hp in range(4):
                    src = bass.AP(k_hbm.tensor, k_hbm.offset + hp * 128 * S,
                                  [[S, 128], [1, S]])
                    k_toks[hp] = R.dma(kv_res[:, hp * S:(hp + 1) * S], src)
                    vsw = bass.AP(v_hbm.tensor,
                                  v_hbm.offset + hp * 2 * 32 * 65,
                                  [[8 * 32 * 65, 128], [1, 2 * 32 * 65]])
                    v_toks[hp] = R.dma(
                        v_res[:, hp * 2 * 32 * 65:(hp + 1) * 2 * 32 * 65], vsw)
                wps = bass.AP(wprojT_h.tensor, wprojT_h.offset,
                              [[D, 128], [D * 128, 8], [1, D]])
                wpdst = bass.AP(wpT[:].tensor, wpT[:].offset,
                                [wpT[:].ap[0], [D, 8], [1, D]])
                wp_tok = R.dma(wpdst, wps)
                ones_ap2 = bass.AP(v_res[:].tensor, v_res[:].offset + 64,
                                   [v_res[:].ap[0], [65, 8 * 32]])
                for hp in range(4):
                    R.wait_tok("vector", v_toks[hp])
                R.op("vector", "memset", ones_ap2, 1.0, inc=(SV, 1))
                ones_sv = SV.n
            R.wait_tok("tensor", wq_tok)
            st_ctx_h = [None, None]
            st_x1 = None
            st_out = None
            slot_ss = [0, 0]
            ctx_sv = [0]
            q_ss = [0]
            q_sv = [0, 0]
            q_psv = [0]
            gate_sv = [0]
            pend_nt = [None]
            znt_tn = [0]

            def norm_tail():
                if pend_nt[0] is None:
                    return
                hA_, t0_, rsv = pend_nt[0]
                pend_nt[0] = None
                R.wait("tensor", SV, rsv)
                R.op("tensor", "matmul", ps_C[0:64, :], ones_f[0:1, 0:64],
                     zrow[:], start=True, stop=True, inc=(TN, 1))
                R.op("tensor", "matmul", ps_C[64:128, :], ones_f[0:1, 0:64],
                     zrow2[:], start=True, stop=True, inc=(TN, 1))
                znt_tn[0] = TN.n
                R.wait("vector", TN, znt_tn[0])
                R.op("vector", "tensor_copy", zbf[:], ps_C[:], inc=(SV, 1))
                if group == 0:
                    R.wait_tok("vector", st_ctx_h[0])
                    R.op("vector", "tensor_mul", ctxst[:], ps_cA[0:64, :],
                         zbf[0:64, :], inc=(SV, 1))
                    R.wait_tok("vector", st_ctx_h[1])
                    R.op("vector", "tensor_mul", ctxst2[:], ps_cB[0:64, :],
                         zbf[64:128, :], inc=(SV, 1))
                    R.wait("sync", SV)
                    dstA = bass.AP(ctx_hbm.tensor,
                                   ctx_hbm.offset + (hA_ * 64) * S + t0_,
                                   [[S, 64], [1, TB]])
                    st_ctx_h[0] = R.dma(dstA, ctxst[:])
                    dstB = bass.AP(ctx_hbm.tensor,
                                   ctx_hbm.offset + ((hA_ + 1) * 64) * S + t0_,
                                   [[S, 64], [1, TB]])
                    st_ctx_h[1] = R.dma(dstB, ctxst2[:])
                else:
                    R.op("vector", "tensor_mul",
                         ctxb_res[0:64, bass.ts(hA_ // 2, TB)], ps_cA[0:64, :],
                         zbf[0:64, :], inc=(SV, 1))
                    R.op("vector", "tensor_mul",
                         ctxb_res[64:128, bass.ts(hA_ // 2, TB)], ps_cB[0:64, :],
                         zbf[64:128, :], inc=(SV, 1))
                ctx_sv[0] = SV.n

            def emit_qblock(b_, par_):
                q_res = q_res2[par_]
                cosb, sinb = cos_blk2[par_], sin_blk2[par_]
                xnt = xnT2[par_]
                R.wait_tok("tensor", xnt_toks[par_])
                for tk in xtoks[par_]:
                    R.wait_tok("vector", tk)
                for qc in range(4):
                    if q_ss[0]:
                        R.wait("tensor", SS, q_ss[0])
                    if q_psv[0]:
                        R.wait("tensor", SV, q_psv[0])
                    for ic in range(8):
                        R.op("tensor", "matmul", ps_q[:],
                             wqT[:, ic * 512 + qc * 128: ic * 512 + qc * 128 + 128],
                             xnt[:, bass.ts(ic, TB)],
                             start=(ic == 0), stop=False, inc=(TN, 1))
                    R.op("tensor", "matmul", ps_q[:],
                         qkvb_row[0:1, group * 512 + qc * 128:
                                  group * 512 + qc * 128 + 128],
                         ones_b[:], start=False, stop=True, inc=(TN, 1))
                    R.wait("scalar", TN)
                    R.wait("scalar", SV)
                    R.op("scalar", "copy", q_raw[:], ps_q[:], inc=(SS, 1))
                    q_ss[0] = SS.n
                    R.wait("tensor", SS, q_ss[0])
                    R.op("tensor", "matmul", ps_q[:], rotT[:], q_raw[:],
                         start=True, stop=True, inc=(TN, 1))
                    rot_tn = TN.n
                    R.wait("vector", SS, q_ss[0])
                    R.op("vector", "tensor_mul", q_t1[:], q_raw[:],
                         cosb[:], inc=(SV, 1))
                    R.wait("vector", TN, rot_tn)
                    R.op("vector", "tensor_mul", q_res[:, bass.ts(qc, TB)],
                         ps_q[:], sinb[:], inc=(SV, 1))
                    q_psv[0] = SV.n
                    R.op("vector", "tensor_add", q_res[:, bass.ts(qc, TB)],
                         q_res[:, bass.ts(qc, TB)], q_t1[:], inc=(SV, 1))
                q_sv[par_] = SV.n
            emit_xload(R, 0, 0, want_x=(group == 1), want_xnt=True)
            for b in range(NB):
                t0 = b * TB
                par = b % 2
                cos_blk, sin_blk = cos_blk2[par], sin_blk2[par]
                xblk = xblk2[par]
                xnT_bf = xnT2[par]
                for tk in xtoks[par]:
                    R.wait_tok("vector", tk)
                R.wait_tok("tensor", xnt_toks[par])
                if b + 1 < NB:
                    R.wait("sync", SV)
                    R.wait("sync", TN)
                    if group == 1:
                        R.wait_tok("sync", st_x1)
                        R.wait_tok("sync", st_out)
                    emit_xload(R, b + 1, 1 - par, want_x=(group == 1),
                               want_xnt=True)
                if b == 0:
                    emit_qblock(0, 0)
                q_res = q_res2[par]
                R.wait("tensor", SV, q_sv[par])
                if gate_sv[0]:
                    R.wait("tensor", SV, gate_sv[0])
                for hp in range(4):
                    hA = 2 * hp
                    n_kc = b + 1
                    qsl = slice(hp * TB, hp * TB + TB)
                    if group == 1 and b == 0:
                        R.wait_tok("tensor", k_toks[hp])
                        R.wait_tok("tensor", v_toks[hp])
                        R.wait("tensor", SV, ones_sv)
                    exp_ss = [0] * 4
                    mask_sv = [0] * 4

                    def emit_av(kc, kk, diag):
                        R.wait("tensor", SS, exp_ss[kk])
                        if diag:
                            R.wait("tensor", SV, mask_sv[kk])
                        qs = kk * 128 if diag else 0
                        kcg = kc * 4 + kk
                        first = (kc == 0 and kk == 0)
                        last = (diag and kk == 3)
                        if first:
                            R.wait("tensor", SV, ctx_sv[0])
                        vA = bass.AP(v_res[:].tensor,
                                     v_res[:].offset + ((hA) * 32 + kcg) * 65,
                                     [v_res[:].ap[0], [1, 65]])
                        vB = bass.AP(v_res[:].tensor,
                                     v_res[:].offset + ((hA + 1) * 32 + kcg) * 65,
                                     [v_res[:].ap[0], [1, 65]])
                        R.op("tensor", "matmul", ps_cA[:, qs:512], vA,
                             pTAB[:, kk * 1024 + qs: kk * 1024 + 512],
                             start=first, stop=last, inc=(TN, 1),
                             skip_group_check=True)
                        R.op("tensor", "matmul", ps_cB[:, qs:512], vB,
                             pTAB[:, kk * 1024 + 512 + qs: kk * 1024 + 1024],
                             start=first, stop=last, inc=(TN, 1),
                             skip_group_check=True)

                    pend = []
                    for kc in range(n_kc):
                        diag = (kc == b)
                        for kk in range(4):
                            kcol = kc * 512 + kk * 128
                            sl = kk % 2
                            if slot_ss[sl]:
                                R.wait("tensor", SS, slot_ss[sl])
                            R.op("tensor", "matmul",
                                 ps_AB[:, sl * 1024: sl * 1024 + 512],
                                 kv_res[0:64, hp * S + kcol: hp * S + kcol + 128],
                                 q_res[0:64, qsl], start=True, stop=True,
                                 tile_position=(0, 0), inc=(TN, 1))
                            R.op("tensor", "matmul",
                                 ps_AB[:, sl * 1024 + 512: sl * 1024 + 1024],
                                 kv_res[64:128, hp * S + kcol: hp * S + kcol + 128],
                                 q_res[64:128, qsl], start=True, stop=True,
                                 tile_position=(64, 0), inc=(TN, 1))
                            R.wait("scalar", TN)
                            R.op("scalar", "activation",
                                 pTAB[:, kk * 1024: kk * 1024 + 1024],
                                 ps_AB[:, sl * 1024: sl * 1024 + 1024],
                                 AF.Exp, scale=0.125, inc=(SS, 1))
                            exp_ss[kk] = SS.n
                            slot_ss[sl] = SS.n
                            if diag:
                                R.wait("vector", SS, exp_ss[kk])
                                for half in range(2):
                                    col = kk * 1024 + half * 512 + kk * 128
                                    R.op("vector", "tensor_mul",
                                         pTAB[:, col:col + 128],
                                         pTAB[:, col:col + 128], tri[:],
                                         inc=(SV, 1))
                                mask_sv[kk] = SV.n
                            pend.append((kc, kk, diag))
                            if kc == 0 and kk == 1:
                                norm_tail()
                            if len(pend) > 2:
                                emit_av(*pend.pop(0))
                    for args_ in pend:
                        emit_av(*args_)
                    pend = []
                    if hp == 0 and b + 1 < NB:
                        emit_qblock(b + 1, 1 - par)
                    R.wait("vector", TN)
                    if znt_tn[0]:
                        R.wait("vector", TN, znt_tn[0])
                    R.op("vector", "reciprocal", zrow[:], ps_cA[64:65, :],
                         inc=(SV, 1))
                    R.op("vector", "reciprocal", zrow2[:], ps_cB[64:65, :],
                         inc=(SV, 1))
                    pend_nt[0] = (hA, t0, SV.n)
                norm_tail()
                if group == 1:
                    ctx_toks = []
                    for ic in range(4):
                        src = bass.AP(ctx_hbm.tensor,
                                      ctx_hbm.offset + ic * 128 * S + t0,
                                      [[S, 128], [1, TB]])
                        ctx_toks.append(R.dma(ctxa_ld[:, bass.ts(ic, TB)], src))
                    for dp in range(8):
                        if dp == 0:
                            R.wait_tok("tensor", wp_tok)
                            for tk in ctx_toks:
                                R.wait_tok("tensor", tk)
                        R.wait("tensor", SV)
                        R.wait("tensor", SS)
                        for hd in range(8):
                            lhs = wpT[:, hd * D + dp * 128: hd * D + dp * 128 + 128]
                            rhs = (ctxa_ld[:, bass.ts(hd, TB)] if hd < 4
                                   else ctxb_res[:, bass.ts(hd - 4, TB)])
                            R.op("tensor", "matmul", ps_C[:], lhs, rhs,
                                 start=(hd == 0), stop=(hd == 7), inc=(TN, 1))
                        src = bass.AP(g_hbm.tensor, g_hbm.offset + dp * 128 * S + t0,
                                      [[S, 128], [1, TB]])
                        R.wait("sync", SV)
                        tok_g = R.dma(g_fc[:], src)
                        R.wait_tok("vector", tok_g)
                        R.wait("vector", TN)
                        R.op("vector", "scalar_tensor_tensor", sg_fc[:], ps_C[:],
                             projb_c[:, dp:dp + 1], g_fc[:], ALU.add, ALU.mult,
                             inc=(SV, 1))
                        R.wait("tensor", SV)
                        R.wait("tensor", SS)
                        for tc in range(4):
                            R.op("tensor", "transpose",
                                 ps_AB[:, bass.ts(tc, 128)],
                                 sg_fc[:, bass.ts(tc, 128)], identf[:],
                                 inc=(TN, 1))
                        R.wait("vector", TN)
                        o_d = bass.AP(xblk[:].tensor, xblk[:].offset + dp * 128,
                                      [xblk[:].ap[0], [D, 4], [1, 128]])
                        p_i = bass.AP(ps_AB[:].tensor, ps_AB[:].offset,
                                      [ps_AB[:].ap[0], [128, 4], [1, 128]])
                        R.op("vector", "tensor_add", o_d, p_i, o_d, inc=(SV, 1))
                    gate_sv[0] = SV.n
                    R.wait("sync", SV)
                    dst = bass.AP(x1_hbm.tensor, x1_hbm.offset + t0 * D,
                                  [[D, 128], [128 * D, 4], [1, D]])
                    x1src = bass.AP(xblk[:].tensor, xblk[:].offset,
                                    [xblk[:].ap[0], [D, 4], [1, D]])
                    st_x1 = R.dma(dst, x1src)
                    if stop_after == "l0":
                        dst2 = bass.AP(out.tensor, out.offset + t0 * D,
                                       [[D, 128], [128 * D, 4], [1, D]])
                        st_out = R.dma(dst2, x1src)
            R.wait_all_dma("sync")
            R.play()

    if stop_after == "p1":
        l0.close()
        return nc
    attn_phase(0)
    if stop_after == "p2a":
        l0.close()
        return nc
    attn_phase(1)
    l0.close()
    nc.all_engine_barrier()
    if stop_after == "l0":
        return nc

    bp_hbm = scratch("bp_hbm", [NST, S], F32)
    cp_hbm = scratch("cp_hbm", [NST, S], F32)

    l1 = ExitStack()

    with ExitStack() as ph:
        R = Rec(nc, dpool, AUTO)
        xb2 = [ph.enter_context(nc.sbuf_tensor(f"xblk1{i}", [128, 4 * D], F32))
               for i in range(2)]
        xnblk1 = ph.enter_context(nc.sbuf_tensor("xnblk1", [128, 4 * D], F32))
        sqdump1 = ph.enter_context(nc.sbuf_tensor("sqdump1", [128, D], F32))
        s2t1 = ph.enter_context(nc.sbuf_tensor("s2t1", [128, 4], F32))
        stat1 = {k: ph.enter_context(nc.sbuf_tensor("s1_" + k, [128, 4], F32))
                 for k in ("m", "var", "rstd", "y2", "u", "w")}
        wdtT = ph.enter_context(nc.sbuf_tensor("wdtT", [128, 8 * D], BF16))
        wq1T = ph.enter_context(nc.sbuf_tensor("wq1T", [128, 8 * D], BF16))
        xn1T_bf = ph.enter_context(nc.sbuf_tensor("xn1T_bf", [128, 8 * TB], BF16))
        xn1T_f = ph.enter_context(nc.sbuf_tensor("xn1T_f", [128, 8 * TB], F32))
        aqT = ph.enter_context(nc.sbuf_tensor("aqT", [128, 8 * TB], BF16))
        dtblk = ph.enter_context(nc.sbuf_tensor("dtblk", [128, 8 * TB], F32))
        dtxblk = ph.enter_context(nc.sbuf_tensor("dtxblk", [128, 8 * TB], FP16))
        t1t = ph.enter_context(nc.sbuf_tensor("t1t", [128, 8 * TB], F32))
        aqtmp = ph.enter_context(nc.sbuf_tensor("aqtmp", [128, 4 * D], F32))
        rmax4 = ph.enter_context(nc.sbuf_tensor("rmax4", [128, 4], F32))
        arcp = ph.enter_context(nc.sbuf_tensor("arcp", [128, 1], F32))
        sclrow = ph.enter_context(nc.sbuf_tensor("sclrow", [1, TB], F32))
        sclb = ph.enter_context(nc.sbuf_tensor("sclb", [128, TB], F32))
        g1stage = ph.enter_context(nc.sbuf_tensor("g1stage", [128, TB], FP16))
        bpstage = ph.enter_context(nc.sbuf_tensor("bpstage", [8, TB], F32))
        cpstage = ph.enter_context(nc.sbuf_tensor("cpstage", [8, TB], F32))
        ps_tr = ph.enter_context(nc.psum_tensor("l1_tr", [128, 512], F32))
        ps_dt = [ph.enter_context(nc.psum_tensor(f"l1_dt{i}", [128, 512], F32))
                 for i in range(2)]
        ps_bp = ph.enter_context(nc.psum_tensor("l1_bp", [8, 512], F32))
        ps_bn = [ph.enter_context(nc.psum_tensor(f"l1_bn{i}", [128, 512], F32))
                 for i in range(2)]
        ps_msc = ph.enter_context(nc.psum_tensor("l1_msc", [128, 512], F32))

        wdts = bass.AP(wdtT_h.tensor, wdtT_h.offset,
                       [[D, 128], [D * 128, 8], [1, D]])
        wdtd = bass.AP(wdtT[:].tensor, wdtT[:].offset,
                       [wdtT[:].ap[0], [D, 8], [1, D]])
        tok_a = R.dma(wdtd, wdts)
        wqs = bass.AP(wq1T_h.tensor, wq1T_h.offset,
                      [[D, 128], [D * 128, 8], [1, D]])
        wqd = bass.AP(wq1T[:].tensor, wq1T[:].offset,
                      [wq1T[:].ap[0], [D, 8], [1, D]])
        tok_b = R.dma(wqd, wqs)
        R.wait_tok("tensor", tok_a)
        R.wait_tok("tensor", tok_b)
        st_dt = None
        st_dtx = None
        aq_tn = [0]
        bp_sv = [0]
        st_g1 = None
        st_bp = None
        st_cp = None
        st_xt = None

        x1toks = {}

        def xload1(b_, par_):
            src = bass.AP(x1_hbm.tensor, x1_hbm.offset + b_ * TB * D,
                          [[D, 128], [128 * D, 4], [1, D]])
            xb = xb2[par_]
            xdst = bass.AP(xb[:].tensor, xb[:].offset,
                           [xb[:].ap[0], [D, 4], [1, D]])
            x1toks[par_] = R.dma(xdst, src)

        def emit_ln14(R, src_tile, dst_tile):
            m4, var4, rstd4 = stat1["m"], stat1["var"], stat1["rstd"]
            y24, u4, w4 = stat1["y2"], stat1["u"], stat1["w"]
            for tc in range(4):
                R.op("vector", "reduce_sum", m4[:, tc:tc + 1],
                     src_tile[:, bass.ts(tc, D)], AX.X)
                R.wait("scalar", SV)
                R.op("scalar", "activation", sqdump1[:],
                     src_tile[:, bass.ts(tc, D)], AF.Square,
                     accum_out=s2t1[:, tc:tc + 1], inc=(SS, 1))
            R.wait("vector", SS)
            R.op("vector", "tensor_scalar", m4[:], m4[:], 1.0 / D, None, ALU.mult)
            R.op("vector", "tensor_scalar", var4[:], s2t1[:], 1.0 / D, None,
                 ALU.mult)
            R.op("vector", "tensor_mul", y24[:], m4[:], m4[:])
            R.op("vector", "tensor_sub", var4[:], var4[:], y24[:])
            R.op("vector", "tensor_scalar", var4[:], var4[:], 1.0, 1e-5,
                 ALU.mult, ALU.add)
            R.op("vector", "reciprocal", rstd4[:], var4[:])
            for _ in range(4):
                R.op("vector", "tensor_mul", y24[:], rstd4[:], rstd4[:])
                R.op("vector", "tensor_mul", u4[:], var4[:], y24[:])
                R.op("vector", "tensor_scalar", w4[:], u4[:], -0.5, 1.5,
                     ALU.mult, ALU.add)
                R.op("vector", "tensor_mul", rstd4[:], rstd4[:], w4[:])
            for tc in range(4):
                R.op("vector", "tensor_scalar", dst_tile[:, bass.ts(tc, D)],
                     src_tile[:, bass.ts(tc, D)], m4[:, tc:tc + 1],
                     rstd4[:, tc:tc + 1], ALU.subtract, ALU.mult, inc=(SV, 1))

        xload1(0, 0)
        for b in range(NB):
            t0 = b * TB
            par = b % 2
            xblk1 = xb2[par]
            R.wait_tok("vector", x1toks[par])
            R.wait_tok("scalar", x1toks[par])
            if b + 1 < NB:
                R.wait("sync", SV)
                R.wait("sync", SS)
                xload1(b + 1, 1 - par)
            emit_ln14(R, xblk1[:], xnblk1[:])
            R.wait_tok("vector", st_xt)
            xtsv = {0: 0, 1: 0}
            R.wait("tensor", SV)
            for ic in range(8):
                pst = (ps_tr, ps_msc)[ic % 2]
                if xtsv[ic % 2]:
                    R.wait("tensor", SV, xtsv[ic % 2])
                for tc in range(4):
                    R.op("tensor", "transpose", pst[:, bass.ts(tc, 128)],
                         xnblk1[:, tc * D + ic * 128: tc * D + ic * 128 + 128],
                         identf[:], inc=(TN, 1))
                R.wait("vector", TN)
                R.op("vector", "tensor_copy", xn1T_bf[:, bass.ts(ic, TB)],
                     pst[:, 0:512])
                R.op("vector", "tensor_copy", xn1T_f[:, bass.ts(ic, TB)],
                     pst[:, 0:512])
                xtsv[ic % 2] = SV.n
            xnev_sv = SV.n
            R.wait("sync", SV)
            R.wait_tok("sync", st_xt)
            xts = bass.AP(xn1T_bf[:].tensor, xn1T_bf[:].offset,
                          [xn1T_bf[:].ap[0], [TB, 8], [1, TB]])
            xtd = bass.AP(xn1T_hbm.tensor, xn1T_hbm.offset + t0,
                          [[S, 128], [S * 128, 8], [1, TB]])
            st_xt = R.dma(xtd, xts)
            for tc in range(4):
                xc = xnblk1[:, bass.ts(tc, D)]
                R.op("vector", "reduce_max", rmax4[:, tc:tc + 1], xc, AX.X,
                     apply_absolute_value=True, inc=(SV, 1))
            R.op("vector", "tensor_scalar", stat1["u"][:], rmax4[:],
                 1.0, 1e-8, ALU.mult, ALU.add, inc=(SV, 1))
            R.op("vector", "reciprocal", stat1["u"][:], stat1["u"][:],
                 inc=(SV, 1))
            if aq_tn[0]:
                R.wait("vector", TN, aq_tn[0])
            for tc in range(4):
                xc = xnblk1[:, bass.ts(tc, D)]
                R.op("vector", "tensor_scalar", aqtmp[:, bass.ts(tc, D)], xc,
                     stat1["u"][:, tc:tc + 1], 127.0, ALU.mult, ALU.mult,
                     inc=(SV, 1))
            R.op("vector", "tensor_scalar", aqtmp[:], aqtmp[:], MAGIC, MAGIC,
                 ALU.add, ALU.subtract, inc=(SV, 1))
            aqc_sv = SV.n
            for dc in range(8):
                pd = ps_dt[dc % 2]
                R.wait("tensor", SS)
                R.wait("tensor", SV, xnev_sv)
                for ic in range(8):
                    R.op("tensor", "matmul", pd[:],
                         wdtT[:, ic * D + dc * 128: ic * D + dc * 128 + 128],
                         xn1T_bf[:, bass.ts(ic, TB)],
                         start=(ic == 0), stop=(ic == 7), inc=(TN, 1))
                R.wait("scalar", TN)
                if dc == 0:
                    R.wait("scalar", SV)
                R.op("scalar", "activation", t1t[:, bass.ts(dc, TB)], pd[:],
                     AF.Exp, bias=bdt_c[:, dc:dc + 1], inc=(SS, 1))
            R.op("scalar", "activation", t1t[:], t1t[:], AF.Ln,
                 bias=c1[:, 0:1], inc=(SS, 1))
            R.op("scalar", "activation", t1t[:], t1t[:], AF.Tanh,
                 scale=1.0 / 0.099, bias=cS1[:, 0:1], inc=(SS, 1))
            R.wait("vector", SS)
            R.wait_tok("vector", st_dt)
            R.op("vector", "tensor_scalar", dtblk[:], t1t[:], 0.099, 0.0505,
                 ALU.mult, ALU.add, inc=(SV, 1))
            R.wait_tok("vector", st_dtx)
            R.op("vector", "tensor_mul", dtxblk[:], dtblk[:], xn1T_f[:],
                 inc=(SV, 1))
            R.wait("sync", SV)
            dstd = bass.AP(dt_hbm.tensor, dt_hbm.offset + t0,
                           [[S, 128], [128 * S, 8], [1, TB]])
            dsrc = bass.AP(dtblk[:].tensor, dtblk[:].offset,
                           [dtblk[:].ap[0], [TB, 8], [1, TB]])
            st_dt = R.dma(dstd, dsrc)
            dstx = bass.AP(dtx_hbm.tensor, dtx_hbm.offset + t0,
                           [[S, 128], [128 * S, 8], [1, TB]])
            xsrc = bass.AP(dtxblk[:].tensor, dtxblk[:].offset,
                           [dtxblk[:].ap[0], [TB, 8], [1, TB]])
            st_dtx = R.dma(dstx, xsrc)
            trsv = {0: 0, 1: 0}
            R.wait("tensor", SV, aqc_sv)
            for ic in range(8):
                pst = (ps_tr, ps_msc)[ic % 2]
                if trsv[ic % 2]:
                    R.wait("tensor", SV, trsv[ic % 2])
                for tc in range(4):
                    R.op("tensor", "transpose", pst[:, bass.ts(tc, 128)],
                         aqtmp[:, tc * D + ic * 128: tc * D + ic * 128 + 128],
                         identf[:], inc=(TN, 1))
                R.wait("vector", TN)
                dsta = bass.AP(aqT[:].tensor, aqT[:].offset + ic * TB,
                               [aqT[:].ap[0], [128, 4], [1, 128]])
                srca = bass.AP(pst[:].tensor, pst[:].offset,
                               [pst[:].ap[0], [128, 4], [1, 128]])
                R.op("vector", "tensor_copy", dsta, srca, inc=(SV, 1))
                trsv[ic % 2] = SV.n
                aq_tn[0] = TN.n
            R.wait("tensor", SV)
            for tc in range(4):
                R.op("tensor", "transpose", ps_msc[0:1, bass.ts(tc, 128)],
                     rmax4[:, tc:tc + 1], identf[:], inc=(TN, 1))
            R.wait("scalar", TN)
            R.op("scalar", "activation", sclrow[:], ps_msc[0:1, 0:512], AF.Identity,
                 scale=wsd127[0:1, 0:1], inc=(SS, 1))
            R.wait("tensor", SS)
            R.op("tensor", "matmul", ps_msc[:, 0:512], ones_f[:], sclrow[:],
                 start=True, stop=True, inc=(TN, 1))
            R.wait("vector", TN)
            R.op("vector", "tensor_copy", sclb[:], ps_msc[:, 0:512], inc=(SV, 1))
            for wt, stage, hbm, sttok in ((wBT, bpstage, bp_hbm, "bp"),
                                          (wCT, cpstage, cp_hbm, "cp")):
                if bp_sv[0]:
                    R.wait("tensor", SV, bp_sv[0])
                for ic in range(8):
                    R.op("tensor", "matmul", ps_bp[:],
                         wt[:, ic * 8: ic * 8 + 8],
                         xn1T_f[:, bass.ts(ic, TB)],
                         start=(ic == 0), stop=(ic == 7), inc=(TN, 1))
                R.wait("vector", TN)
                R.wait_tok("vector", st_bp if sttok == "bp" else st_cp)
                R.op("vector", "tensor_copy", stage[:], ps_bp[:], inc=(SV, 1))
                bp_sv[0] = SV.n
                R.wait("sync", SV)
                dstb = bass.AP(hbm.tensor, hbm.offset + t0, [[S, 8], [1, TB]])
                if sttok == "bp":
                    st_bp = R.dma(dstb, stage[:])
                else:
                    st_cp = R.dma(dstb, stage[:])
            for fc in range(8):
                pb = ps_bn[fc % 2]
                R.wait("tensor", SV)
                for ic in range(8):
                    R.op("tensor", "matmul", pb[:],
                         wq1T[:, ic * D + fc * 128: ic * D + fc * 128 + 128],
                         aqT[:, bass.ts(ic, TB)],
                         start=(ic == 0), stop=(ic == 7), inc=(TN, 1))
                R.wait("vector", TN)
                R.wait("vector", SS)
                R.wait_tok("vector", st_g1)
                R.op("vector", "tensor_mul", g1stage[:], pb[:], sclb[:],
                     inc=(SV, 1))
                R.wait("scalar", SV)
                R.op("scalar", "activation", g1stage[:], g1stage[:], AF.Silu,
                     inc=(SS, 1))
                R.wait("sync", SS)
                dst = bass.AP(g1_hbm.tensor, g1_hbm.offset + fc * 128 * S + t0,
                              [[S, 128], [1, TB]])
                st_g1 = R.dma(dst, g1stage[:])
        R.wait_all_dma("sync")
        R.play()

    if stop_after == "l1a":
        l1.close()
        return nc

    TMAX = 120
    SSTR = 12
    from collections import deque
    with ExitStack() as ph:
        R = Rec(nc, dpool, AUTO)
        sb = lambda nm, sh, d=F32: ph.enter_context(nc.sbuf_tensor(nm, sh, d))
        dA_all = sb("dA_all", [128, 64 * TMAX], FP16)
        u_all = sb("u_all", [128, 64 * TMAX], FP16)
        Z3 = sb("Z3", [128, 3 * 64 * TMAX], FP16)
        P3 = sb("P3", [128, 3 * 64 * TMAX], FP16)
        dt8 = [sb(f"dt8_{i}", [128, 8 * TMAX]) for i in range(2)]
        dtx8 = [sb(f"dtx8_{i}", [128, 8 * TMAX], FP16) for i in range(2)]
        xn1T8 = [sb(f"xn1T8_{i}", [128, 8 * TMAX], BF16) for i in range(2)]
        bp8 = [sb(f"bp8_{i}", [1, 8 * TMAX]) for i in range(2)]
        cp8 = [sb(f"cp8_{i}", [1, 8 * TMAX]) for i in range(3)]
        y83 = sb("y83", [128, 3 * 8 * TMAX], FP16)
        etp = [sb("etp_0", [128, 16 * TMAX])]
        uptmp = sb("uptmp", [128, 16 * TMAX], FP16)
        htmp2 = [sb(f"htmp8_{i}", [128, 16 * TMAX], FP16) for i in range(2)]
        be4 = [sb(f"be4_{i}", [128, 4 * TMAX], FP16) for i in range(2)]
        ce4 = [sb(f"ce4_{i}", [128, 4 * TMAX], FP16) for i in range(2)]
        Hc = sb("Hc", [128, 64 * SSTR])
        Hprev2 = [sb(f"Hprev{i}", [128, 64 * SSTR]) for i in range(2)]
        cbf2 = [sb(f"cbf{i}", [128, SSTR]) for i in range(2)]
        tmp64 = sb("tmp64", [128, 64])
        sq64 = sb("sq64", [128, 64])
        nrmcol = sb("nrmcol", [128, 1])
        c_row2 = [sb(f"c_row{i}", [1, SSTR]) for i in range(2)]
        c_sc = sb("c_sc", [1, 8])
        hcar = sb("hcar", [128, 64])
        hcar_in = sb("hcar_in", [128, 64])
        ps_be = ph.enter_context(nc.psum_tensor("sc_be", [128, 4 * TMAX], F32))
        ps_ce = ph.enter_context(nc.psum_tensor("sc_ce", [128, 4 * TMAX], F32))
        ps_abd = ph.enter_context(nc.psum_tensor("sc_abd", [1, 8], F32))
        ps_cb = ph.enter_context(nc.psum_tensor("sc_cb", [128, SSTR], F32))

        TILES = ([(0, 1, 1)] + [(1 + 120 * j, 120, 12) for j in range(34)]
                 + [(4081, 15, 1)])
        NT = len(TILES)
        R.op("vector", "memset", hcar[:], 0.0, inc=(SV, 1))

        ld_toks = [None, None]
        cp_toks = [None, None, None]
        st_y = [None, None, None]
        ess = [0]
        essn = {}
        argsv = [0]
        be_hold = [0]
        be_rd = [0, 0]
        be_usv = [0, 0]
        bmtn = [0]
        ce_hold = [0]
        ce_rd = [0, 0]
        ce_usv = [0, 0]
        cmtn = [0]
        cpre_sv = [0]
        ht_sg = [0, 0]
        up_T = [0]
        cst = {"cb": 0, "abd": 0, "cbf": 0}

        def emit_loads(j):
            t0, T, _ = TILES[j]
            jl, js = j % 2, j % 3
            R.wait("sync", SV)
            R.wait("sync", TN)
            toks = []
            for hbm_t, dstl in ((dt_hbm, dt8), (dtx_hbm, dtx8),
                                (xn1T_hbm, xn1T8)):
                dst = dstl[jl]
                srcp = bass.AP(hbm_t.tensor, hbm_t.offset + t0,
                               [[S, 128], [S * 128, 8], [1, T]])
                dstp = bass.AP(dst[:].tensor, dst[:].offset,
                               [dst[:].ap[0], [T, 8], [1, T]])
                toks.append(R.dma(dstp, srcp,
                                  allow_slow_non_contiguous=(T < 8)))
            bdst = bass.AP(bp8[jl][:].tensor, bp8[jl][:].offset,
                           [bp8[jl][:].ap[0], [T, 8], [1, T]])
            toks.append(R.dma(bdst,
                              bass.AP(bp_hbm.tensor, bp_hbm.offset + t0,
                                      [[S * 8, 1], [S, 8], [1, T]]),
                              allow_slow_non_contiguous=(T < 8)))
            ld_toks[jl] = toks
            cdst = bass.AP(cp8[js][:].tensor, cp8[js][:].offset,
                           [cp8[js][:].ap[0], [T, 8], [1, T]])
            cp_toks[js] = R.dma(cdst,
                                bass.AP(cp_hbm.tensor, cp_hbm.offset + t0,
                                        [[S * 8, 1], [S, 8], [1, T]]),
                                allow_slow_non_contiguous=(T < 8))

        def s1_stage(j):
            t0, T, nseg = TILES[j]
            tail = T - nseg * 10
            nst = nseg + (1 if tail > 0 else 0)
            seglen = 10 if T >= 10 else T
            T8 = 8 * T
            jl, js = j % 2, j % 3
            zoff = js * 64 * TMAX
            yoff = js * 8 * TMAX
            pre_start = []
            pre_pairs = [[] for _ in range(4)]
            ops = []
            em = lambda c, f: ops.append((c, f))
            emp = lambda c, f: pre_start.append((c, f))

            def fstart():
                for tk in ld_toks[jl]:
                    R.wait_tok("vector", tk)
                R.wait_tok("tensor", ld_toks[jl][3])
            emp(0, fstart)
            for q in range(2):
                def f_mm(q=q):
                    if be_hold[0]:
                        R.wait("tensor", SS, be_hold[0])
                    R.op("tensor", "matmul", ps_be[:, 0:4 * T], ones_f[:],
                         bp8[jl][0:1, q * 4 * T: (q + 1) * 4 * T],
                         start=True, stop=True, inc=(TN, 1))
                    bmtn[0] = TN.n

                def f_evac(q=q):
                    R.wait("scalar", TN, bmtn[0])
                    if be_usv[q]:
                        R.wait("scalar", SV, be_usv[q])
                    R.op("scalar", "copy", be4[q][:, 0:4 * T],
                         ps_be[:, 0:4 * T], inc=(SS, 1))
                    be_hold[0] = SS.n
                    be_rd[q] = SS.n
                emp(0, f_mm)
                emp(0, f_evac)
            for p in range(4):
                n = 2 * p
                nof = n * 8 * T
                ep = p % 2

                def f_arg(n=n, ep=ep):
                    if ess[0]:
                        R.wait("vector", SS, ess[0])
                    et = etp[0]
                    arg_o = bass.AP(et[:].tensor, et[:].offset,
                                    [et[:].ap[0], [T8, 2], [T, 8], [1, T]])
                    dt_i = bass.AP(dt8[jl][:].tensor, dt8[jl][:].offset,
                                   [dt8[jl][:].ap[0], [0, 2], [T, 8], [1, T]])
                    a_i = bass.AP(a16[:].tensor, a16[:].offset + n,
                                  [a16[:].ap[0], [1, 2], [8, 8], [0, T]])
                    R.op("vector", "tensor_mul", arg_o, dt_i, a_i, inc=(SV, 1))
                    argsv[0] = SV.n

                def f_tanh(n=n, nof=nof, ep=ep, p=p):
                    R.wait("scalar", SV, argsv[0])
                    et = etp[0]
                    R.op("scalar", "activation", et[:, 0:2 * T8], et[:, 0:2 * T8],
                         AF.Tanh, inc=(SS, 1))
                    R.op("scalar", "activation", dA_all[:, nof: nof + 2 * T8],
                         et[:, 0:2 * T8], AF.Exp, scale=16.0, inc=(SS, 1))
                    ess[0] = SS.n
                    essn[(j, p)] = SS.n
                pre_pairs[p].append((2.2, f_arg))
                pre_pairs[p].append((0, f_tanh))
            for p in range(4):
                n = 2 * p
                nof = n * 8 * T
                q = p // 2

                def f_u(n=n, nof=nof, q=q):
                    R.wait("vector", SS, be_rd[q])
                    u_o = bass.AP(u_all[:].tensor, u_all[:].offset + nof,
                                  [u_all[:].ap[0], [T8, 2], [T, 8], [1, T]])
                    dtx_i = bass.AP(dtx8[jl][:].tensor, dtx8[jl][:].offset,
                                    [dtx8[jl][:].ap[0], [0, 2], [T, 8], [1, T]])
                    be_i = bass.AP(be4[q][:].tensor,
                                   be4[q][:].offset + (n % 4) * T,
                                   [be4[q][:].ap[0], [T, 2], [0, 8], [1, T]])
                    R.op("vector", "tensor_mul", u_o, dtx_i, be_i, inc=(SV, 1))
                    be_usv[q] = SV.n

                def f_up(n=n, nof=nof, p=p):
                    R.wait("vector", SS, essn[(j, p)])
                    if up_T[0] != T:
                        R.op("vector", "memset", uptmp[:, 0:16 * TMAX], 0.0,
                             inc=(SV, 1))
                        up_T[0] = T
                    upd = bass.AP(uptmp[:].tensor, uptmp[:].offset,
                                  [uptmp[:].ap[0], [T8, 2], [T, 8], [10, nst]])
                    das = bass.AP(dA_all[:].tensor, dA_all[:].offset + nof,
                                  [dA_all[:].ap[0], [T8, 2], [T, 8], [10, nst]])
                    R.op("vector", "tensor_copy", upd, das, inc=(SV, 1))
                    R.op("vector", "memset", das, 0.0, inc=(SV, 1))

                def f_zscan(nof=nof):
                    R.op("vector", "tensor_tensor_scan",
                         Z3[:, zoff + nof: zoff + nof + 2 * T8],
                         dA_all[:, nof: nof + 2 * T8],
                         u_all[:, nof: nof + 2 * T8],
                         0.0, ALU.mult, ALU.add, inc=(SV, 1))

                def f_pscan(nof=nof):
                    R.op("vector", "tensor_tensor_scan",
                         P3[:, zoff + nof: zoff + nof + 2 * T8],
                         dA_all[:, nof: nof + 2 * T8], uptmp[:, 0:2 * T8],
                         0.0, ALU.mult, ALU.add, inc=(SV, 1))

                em(1.4, f_u)

                def f_upz(f1=f_up, f2=f_zscan):
                    f1()
                    f2()
                em(5.8, f_upz)
                em(4.6, f_pscan)

            def f_yinit():
                R.wait_tok("vector", st_y[js])
                y_o = bass.AP(y83[:].tensor, y83[:].offset + yoff,
                              [y83[:].ap[0], [T, 8], [1, T]])
                x_i = bass.AP(xn1T8[jl][:].tensor, xn1T8[jl][:].offset,
                              [xn1T8[jl][:].ap[0], [T, 8], [1, T]])
                d_i = bass.AP(dpar_c[:].tensor, dpar_c[:].offset,
                              [dpar_c[:].ap[0], [1, 8], [0, T]])
                R.op("vector", "tensor_mul", y_o, x_i, d_i, inc=(SV, 1))
            em(0.7, f_yinit)
            if j + 1 < NT:
                em(0, lambda j=j: emit_loads(j + 1))
            return pre_start, pre_pairs, ops

        def s2_stage(j):
            t0, T, nseg = TILES[j]
            tail = T - nseg * 10
            seglen = 10 if T >= 10 else T
            T8 = 8 * T
            js, jp = j % 3, j % 2
            zoff = js * 64 * TMAX
            yoff = js * 8 * TMAX
            ops = []
            em = lambda c, f: ops.append((c, f))
            def f_cpre():
                R.wait_tok("vector", cp_toks[js])
                ce_d = bass.AP(cp8[js][:].tensor, cp8[js][:].offset + seglen - 1,
                               [cp8[js][:].ap[0], [T, 8], [10, nseg]])
                cr_i = bass.AP(c_row2[jp][:].tensor, c_row2[jp][:].offset,
                               [c_row2[jp][:].ap[0], [0, 8], [1, nseg]])
                R.op("vector", "tensor_mul", ce_d, ce_d, cr_i, inc=(SV, 1))
                cpre_sv[0] = SV.n
            em(0.3, f_cpre)
            for q in range(2):
                def f_mm(q=q):
                    if ce_hold[0]:
                        R.wait("tensor", SS, ce_hold[0])
                    R.wait("tensor", SV, cpre_sv[0])
                    R.op("tensor", "matmul", ps_ce[:, 0:4 * T], ones_f[:],
                         cp8[js][0:1, q * 4 * T: (q + 1) * 4 * T],
                         start=True, stop=True, inc=(TN, 1))
                    cmtn[0] = TN.n

                def f_evac(q=q):
                    R.wait("scalar", TN, cmtn[0])
                    if ce_usv[q]:
                        R.wait("scalar", SV, ce_usv[q])
                    R.op("scalar", "copy", ce4[q][:, 0:4 * T],
                         ps_ce[:, 0:4 * T], inc=(SS, 1))
                    ce_hold[0] = SS.n
                    ce_rd[q] = SS.n
                em(0, f_mm)
                em(0, f_evac)
            seq = []
            adds_sg = {}
            hc_sv = {}
            y_sg = {}
            for p in range(4):
                n = 2 * p
                nof = n * 8 * T
                q = p // 2
                hb = htmp2[p % 2]

                def f_pmul(n=n, nof=nof, p=p, hb=hb):
                    for h in range(2):
                        hof = h * T8
                        p4 = bass.AP(P3[:].tensor,
                                     P3[:].offset + zoff + nof + hof,
                                     [P3[:].ap[0], [T, 8], [10, nseg],
                                      [1, seglen]])
                        h4 = bass.AP(Hprev2[jp][:].tensor,
                                     Hprev2[jp][:].offset + (n + h) * 8 * SSTR,
                                     [Hprev2[jp][:].ap[0], [SSTR, 8], [1, nseg],
                                      [0, seglen]])
                        o4 = bass.AP(hb[:].tensor, hb[:].offset + hof,
                                     [hb[:].ap[0], [T, 8], [10, nseg],
                                      [1, seglen]])
                        R.op("vector", "tensor_mul", o4, p4, h4, inc=(SV, 1))
                        if tail > 0:
                            pt = bass.AP(P3[:].tensor,
                                         P3[:].offset + zoff + nof + hof
                                         + nseg * 10,
                                         [P3[:].ap[0], [T, 8], [1, tail]])
                            ht = bass.AP(Hprev2[jp][:].tensor,
                                         Hprev2[jp][:].offset
                                         + (n + h) * 8 * SSTR + nseg,
                                         [Hprev2[jp][:].ap[0], [SSTR, 8],
                                          [0, tail]])
                            ot = bass.AP(hb[:].tensor,
                                         hb[:].offset + hof + nseg * 10,
                                         [hb[:].ap[0], [T, 8], [1, tail]])
                            R.op("vector", "tensor_mul", ot, pt, ht,
                                 inc=(SV, 1))

                def f_add(nof=nof, p=p, hb=hb):
                    R.op("vector", "tensor_add", hb[:, 0:2 * T8],
                         hb[:, 0:2 * T8],
                         Z3[:, zoff + nof: zoff + nof + 2 * T8], inc=(SV, 1))

                def f_hc(n=n, q=q, p=p, hb=hb):
                    R.wait("vector", SS, ce_rd[q])
                    h_i = bass.AP(hb[:].tensor, hb[:].offset,
                                  [hb[:].ap[0], [T8, 2], [T, 8], [1, T]])
                    ce_i = bass.AP(ce4[q][:].tensor,
                                   ce4[q][:].offset + (n % 4) * T,
                                   [ce4[q][:].ap[0], [T, 2], [0, 8], [1, T]])
                    R.op("vector", "tensor_mul", h_i, h_i, ce_i, inc=(SV, 1))
                    ce_usv[q] = SV.n
                    hc_sv[p] = SV.n

                def f_y(p=p, hb=hb):
                    for h in range(2):
                        R.op("vector", "tensor_add",
                             y83[:, yoff: yoff + T8],
                             y83[:, yoff: yoff + T8],
                             hb[:, h * T8: (h + 1) * T8], inc=(SV, 1))
                seq.append((p, f_pmul, f_add, f_hc, f_y))
            em(1.6, seq[0][1])
            em(1.2, seq[0][2])
            em(1.6, seq[1][1])
            em(1.2, seq[1][2])
            for p in range(4):
                if p + 2 < 4:
                    em(0.9, seq[p][3])
                    em(1.2, seq[p][4])
                    em(1.6, seq[p + 2][1])
                    em(1.2, seq[p + 2][2])
                else:
                    em(0.9, seq[p][3])
                    em(1.2, seq[p][4])
            def f_store():
                R.wait("sync", SV)
                dsty = bass.AP(y_hbm.tensor, y_hbm.offset + t0,
                               [[S, 128], [128 * S, 8], [1, T]])
                ysrc = bass.AP(y83[:].tensor, y83[:].offset + yoff,
                               [y83[:].ap[0], [T, 8], [1, T]])
                st_y[js] = R.dma(dsty, ysrc, allow_slow_non_contiguous=(T < 8))
            em(0, f_store)
            return ops

        def chain_stage(j, filler, bigq=None):
            t0, T, nseg = TILES[j]
            tail = T - nseg * 10
            nst = nseg + (1 if tail > 0 else 0)
            seglen = 10 if T >= 10 else T
            jp, js = j % 2, j % 3
            zoff = js * 64 * TMAX

            def take(budget):
                acc = 0.0
                while filler and acc < budget:
                    c, fn = filler.popleft()
                    fn()
                    acc += c
                if acc < budget and bigq:
                    c, fn = bigq.popleft()
                    fn()

            R.op("vector", "tensor_copy", hcar_in[:], hcar[:], inc=(SV, 1))
            take(1.0)
            for k in range(nseg):
                eoff = zoff + k * 10 + seglen - 1
                gsl = bass.AP(P3[:].tensor, P3[:].offset + eoff,
                              [P3[:].ap[0], [T, 64]])
                lsl = bass.AP(Z3[:].tensor, Z3[:].offset + eoff,
                              [Z3[:].ap[0], [T, 64]])
                hck = bass.AP(Hc[:].tensor, Hc[:].offset + k,
                              [Hc[:].ap[0], [SSTR, 64]])
                if k == 0:
                    R.op("vector", "scalar_tensor_tensor", tmp64[:], hcar_in[:],
                         1.0, gsl, ALU.mult, ALU.mult, inc=(SV, 1))
                else:
                    prev = bass.AP(Hc[:].tensor, Hc[:].offset + k - 1,
                                   [Hc[:].ap[0], [SSTR, 64]])
                    R.wait("vector", TN, cst["cb"])
                    R.op("vector", "scalar_tensor_tensor", tmp64[:], prev,
                         ps_cb[:, 0:1], gsl, ALU.mult, ALU.mult, inc=(SV, 1))
                R.op("vector", "tensor_add", hck, tmp64[:], lsl, inc=(SV, 1))
                R.op("vector", "scalar_tensor_tensor", sq64[:], hck, 1.0, hck,
                     ALU.mult, ALU.mult, accum_out=nrmcol[:], inc=(SV, 1))
                sv_head = SV.n
                R.wait("tensor", SV, sv_head)
                if cst["abd"]:
                    R.wait("tensor", SS, cst["abd"])
                R.op("tensor", "matmul", ps_abd[0:1, 0:1], ones_c[:], nrmcol[:],
                     start=True, stop=True, inc=(TN, 1))
                tn1 = TN.n
                R.wait("scalar", TN, tn1)
                R.op("scalar", "activation", c_sc[0:1, 0:1], ps_abd[0:1, 0:1],
                     AF.Ln, inc=(SS, 1))
                cst["abd"] = SS.n
                R.op("scalar", "activation", c_sc[0:1, 1:2], c_sc[0:1, 0:1],
                     AF.Exp, scale=-0.5, inc=(SS, 1))
                ss_k = SS.n
                take(1.3)
                R.wait("vector", SS, ss_k)
                if k == 0 and cst["cbf"]:
                    R.wait("vector", TN, cst["cbf"])
                R.op("vector", "tensor_scalar", c_row2[jp][0:1, k:k + 1],
                     c_sc[0:1, 1:2], 20.0, 1.0, ALU.mult, ALU.min, inc=(SV, 1))
                sv_tail = SV.n
                if k + 1 < nseg:
                    R.wait("tensor", SV, sv_tail)
                    R.op("tensor", "matmul", ps_cb[:, 0:1], ones_f[:],
                         c_row2[jp][0:1, k:k + 1], start=True, stop=True,
                         inc=(TN, 1))
                    cst["cb"] = TN.n
                take(0.9)
            R.wait("tensor", SV, sv_tail)
            R.op("tensor", "matmul", ps_cb[:, 0:nseg], ones_f[:],
                 c_row2[jp][0:1, 0:nseg], start=True, stop=True, inc=(TN, 1))
            cst["cbf"] = TN.n
            R.wait("vector", TN, cst["cbf"])
            R.op("vector", "tensor_copy", cbf2[jp][:, 0:nseg], ps_cb[:, 0:nseg],
                 inc=(SV, 1))
            hl = bass.AP(Hc[:].tensor, Hc[:].offset + nseg - 1,
                         [Hc[:].ap[0], [SSTR, 64]])
            R.op("vector", "tensor_scalar", hcar[:], hl,
                 cbf2[jp][:, nseg - 1:nseg], None, ALU.mult, inc=(SV, 1))
            hp0 = bass.AP(Hprev2[jp][:].tensor, Hprev2[jp][:].offset,
                          [Hprev2[jp][:].ap[0], [SSTR, 64]])
            R.op("vector", "tensor_copy", hp0, hcar_in[:], inc=(SV, 1))
            if nst > 1:
                hpk = bass.AP(Hprev2[jp][:].tensor, Hprev2[jp][:].offset + 1,
                              [Hprev2[jp][:].ap[0], [SSTR, 64], [1, nst - 1]])
                hck2 = bass.AP(Hc[:].tensor, Hc[:].offset,
                               [Hc[:].ap[0], [SSTR, 64], [1, nst - 1]])
                cbk = bass.AP(cbf2[jp][:].tensor, cbf2[jp][:].offset,
                              [cbf2[jp][:].ap[0], [0, 64], [1, nst - 1]])
                R.op("vector", "tensor_mul", hpk, hck2, cbk, inc=(SV, 1))

        emit_loads(0)
        sB = {}
        ps0, pp0, opsB0 = s1_stage(0)
        sB[0] = opsB0
        for c, fn in ps0:
            fn()
        for pl in pp0:
            for c, fn in pl:
                fn()
        for r in range(NT + 2):
            items = []
            if r - 2 >= 0:
                items.extend(s2_stage(r - 2))
            items.extend(sB.pop(r, []))
            if r + 1 < NT:
                pre_start, pre_pairs, opsB = s1_stage(r + 1)
                sB[r + 1] = opsB
            else:
                pre_start, pre_pairs = [], [[] for _ in range(4)]
            small = deque(x for x in items if x[0] <= 2.5)
            bigs = [x for x in items if x[0] > 2.5]
            big = deque()
            nb = len(bigs)
            same_T = (r < NT and r + 1 < NT and TILES[r][1] == TILES[r + 1][1])
            if same_T and nb:
                for i in range(max(nb, 8)):
                    if i < nb:
                        big.append(bigs[i])
                    if i == 1:
                        big.extend(pre_start)
                        pre_start = []
                    if i % 2 == 1 and i // 2 < 4:
                        big.extend(pre_pairs[i // 2])
            else:
                big.extend(bigs)
                big.extend(pre_start)
                for pl in pre_pairs:
                    big.extend(pl)
            if 0 <= r - 1 < NT:
                chain_stage(r - 1, small, big)
            for dq in (small, big):
                while dq:
                    c, fn = dq.popleft()
                    fn()
        R.wait_all_dma("sync")
        R.play()

    with ExitStack() as ph:
        R = Rec(nc, dpool, AUTO)
        yf2 = [ph.enter_context(nc.sbuf_tensor(f"yfc{i}", [128, 8 * TB], FP16))
               for i in range(2)]
        g1f2 = [ph.enter_context(nc.sbuf_tensor(f"g1fc{i}", [128, 8 * TB], FP16))
                for i in range(2)]
        sgfc = ph.enter_context(nc.sbuf_tensor("sgfc", [128, 8 * TB], F32))
        ob2 = [ph.enter_context(nc.sbuf_tensor(f"oblk{i}", [128, 4 * D], F32))
               for i in range(2)]
        xg2 = [ph.enter_context(nc.sbuf_tensor(f"xgblk{i}", [128, 4 * D], F32))
               for i in range(2)]
        ps_g2 = [ph.enter_context(nc.psum_tensor(f"gt_{i}", [128, 512], F32))
                 for i in range(2)]

        ytoks = {}

        def yg_load(b_, par_):
            t0_ = b_ * TB
            ysrc = bass.AP(y_hbm.tensor, y_hbm.offset + t0_,
                           [[S, 128], [128 * S, 8], [1, TB]])
            ydst = bass.AP(yf2[par_][:].tensor, yf2[par_][:].offset,
                           [yf2[par_][:].ap[0], [TB, 8], [1, TB]])
            ty = R.dma(ydst, ysrc)
            gsrc = bass.AP(g1_hbm.tensor, g1_hbm.offset + t0_,
                           [[S, 128], [128 * S, 8], [1, TB]])
            gdst = bass.AP(g1f2[par_][:].tensor, g1f2[par_][:].offset,
                           [g1f2[par_][:].ap[0], [TB, 8], [1, TB]])
            tg = R.dma(gdst, gsrc)
            ytoks[par_] = (ty, tg)

        xgtoks = {}

        def xg_load(b_, par_):
            src = bass.AP(x1_hbm.tensor, x1_hbm.offset + b_ * TB * D,
                          [[D, 128], [128 * D, 4], [1, D]])
            xg = xg2[par_]
            xdst = bass.AP(xg[:].tensor, xg[:].offset,
                           [xg[:].ap[0], [D, 4], [1, D]])
            xgtoks[par_] = R.dma(xdst, src)

        st_o = [None, None]
        xg_load(0, 0)
        yg_load(0, 0)
        trn_g = {0: 0, 1: 0}
        for b in range(NB):
            t0 = b * TB
            bpar = b % 2
            if b + 1 < NB:
                R.wait("sync", SV)
                R.wait_tok("sync", st_o[1 - bpar])
                xg_load(b + 1, 1 - bpar)
                yg_load(b + 1, 1 - bpar)
            R.wait_tok("vector", xgtoks[bpar])
            R.wait_tok("vector", st_o[bpar])
            ty, tg = ytoks[bpar]
            R.wait_tok("vector", ty)
            R.wait_tok("vector", tg)
            R.op("vector", "tensor_mul", sgfc[:], yf2[bpar][:], g1f2[bpar][:],
                 inc=(SV, 1))
            sg_sv = SV.n
            for dp in range(8):
                pst = ps_g2[dp % 2]
                R.wait("tensor", SV, max(trn_g[dp % 2], sg_sv))
                for tc in range(4):
                    R.op("tensor", "transpose", pst[:, bass.ts(tc, 128)],
                         sgfc[:, dp * TB + tc * 128: dp * TB + tc * 128 + 128],
                         identf[:], inc=(TN, 1))
                R.wait("vector", TN)
                ob = ob2[bpar]
                o_d = bass.AP(ob[:].tensor, ob[:].offset + dp * 128,
                              [ob[:].ap[0], [D, 4], [1, 128]])
                x_i = bass.AP(xg2[bpar][:].tensor, xg2[bpar][:].offset + dp * 128,
                              [xg2[bpar][:].ap[0], [D, 4], [1, 128]])
                p_i = bass.AP(pst[:].tensor, pst[:].offset,
                              [pst[:].ap[0], [128, 4], [1, 128]])
                R.op("vector", "tensor_add", o_d, p_i, x_i, inc=(SV, 1))
                trn_g[dp % 2] = SV.n
            R.wait("sync", SV)
            dsto = bass.AP(out.tensor, out.offset + t0 * D,
                           [[D, 128], [128 * D, 4], [1, D]])
            osrc = bass.AP(ob2[bpar][:].tensor, ob2[bpar][:].offset,
                           [ob2[bpar][:].ap[0], [D, 4], [1, D]])
            st_o[bpar] = R.dma(dsto, osrc)
        R.wait_all_dma("sync")
        R.play()
    l1.close()
    return nc


def kernel(**inp):
    nc = build()
    xs = np.ascontiguousarray(np.asarray(inp["L0_attn_x"], dtype=np.float32))

    def col8(v):
        return np.ascontiguousarray(np.asarray(v, dtype=np.float32).reshape(8, 128).T)

    base = {
        "qkv_w": np.ascontiguousarray(np.asarray(inp["L0_attn_qkv_w"], dtype=np.float32)),
        "qkv_b": np.ascontiguousarray(np.asarray(inp["L0_attn_qkv_b"], dtype=np.float32).reshape(1, -1)),
        "proj_w": np.ascontiguousarray(np.asarray(inp["L0_attn_proj_w"], dtype=np.float32)),
        "proj_b": col8(inp["L0_attn_proj_b"]),
        "f0_w": np.ascontiguousarray(np.asarray(inp["L0_ffn_weight"], dtype=np.float32)),
        "f0_b": col8(inp["L0_ffn_bias"]),
        "w_dt": np.ascontiguousarray(np.asarray(inp["L1_attn_W_dt"], dtype=np.float32)),
        "b_dt": col8(inp["L1_attn_b_dt"]),
        "w_B": np.ascontiguousarray(np.asarray(inp["L1_attn_W_B"], dtype=np.float32)),
        "w_C": np.ascontiguousarray(np.asarray(inp["L1_attn_W_C"], dtype=np.float32)),
        "a_log": np.ascontiguousarray(
            np.asarray(inp["L1_attn_A_log"], dtype=np.float32)
            .reshape(8, 128, 8).transpose(1, 0, 2).reshape(128, 64)),
        "d_par": col8(inp["L1_attn_D_param"]),
        "f1_w": np.ascontiguousarray(np.asarray(inp["L1_ffn_weight"], dtype=np.float32)),
    }
    in_maps = [dict(base, x=np.ascontiguousarray(xs[b])) for b in range(8)]
    res = run_bass_kernel_spmd(nc, in_maps, core_ids=list(range(8)))
    return np.stack([np.asarray(res.results[b]["out"], dtype=np.float32)
                     for b in range(8)], axis=0)
